# version 03
# baseline (speedup 1.0000x reference)
import math
import numpy as np
import concourse.bass as bass
import concourse.mybir as mybir
from concourse.bass_utils import run_bass_kernel_spmd
from contextlib import ExitStack

F32 = mybir.dt.float32
BF16 = mybir.dt.bfloat16
AF = mybir.ActivationFunctionType
ALU = mybir.AluOpType
AX = mybir.AxisListType

S = 2048
NCH = 32
ST = 1024
AST = 512
SC = 1.0 / math.sqrt(128.0)
SHIFT = -12.0

ENGS = ("tensor", "vector", "scalar", "sync", "gpsimd")

A_H = 0
A_SLAB0 = 65536
A_SLAB1 = 69632
A_STG0 = 73728
A_STG1 = 74752
A_ABC_A = 75776
A_ABC_B = 76800
A_GA = 77824
A_IDENT = 78976
ARENA_COLS = 79104

PT0 = 48 * 1024
ARAW0 = 52 * 1024
AFIN0 = 54 * 1024
MASK0 = 56 * 1024


class Buf:
    __slots__ = ("w", "r")

    def __init__(self):
        self.w = None
        self.r = []


class Prog:
    def __init__(self):
        self.ops = {e: [] for e in ENGS}
        self.cnt = {e: 0 for e in ENGS}

    def emit(self, eng, fn, reads=(), writes=(), extra=(), dma=False):
        deps = {}

        def add(t):
            if t is None:
                return
            e, v = t
            if v > deps.get(e, 0):
                deps[e] = v

        for t in extra:
            add(t)
        for b in reads:
            add(b.w)
        for b in writes:
            add(b.w)
            for t in b.r:
                add(t)
        if dma:
            step = 16
        elif eng in ("sync", "gpsimd"):
            step = 0
        else:
            step = 1
        self.ops[eng].append((deps, fn, step))
        self.cnt[eng] += step
        tok = (eng, self.cnt[eng])
        for b in reads:
            b.r.append(tok)
        for b in writes:
            b.w = tok
            b.r = []
        return tok

    def finalize(self):
        waited = {"sync": set(), "gpsimd": set()}
        for ename in ENGS:
            for deps, fn, step in self.ops[ename]:
                for de, dv in deps.items():
                    if de in waited:
                        waited[de].add(dv)
        for q in ("sync", "gpsimd"):
            newops = []
            cum = 0
            for deps, fn, step in self.ops[q]:
                deps = {k: v for k, v in deps.items() if k != q}
                newops.append((deps, fn, step))
                if step == 16:
                    cum += 16
                    if cum in waited[q]:
                        newops.append(({q: cum}, None, 0))
            self.ops[q] = newops

    def replay(self, eng_name, eng, sems):
        last_wait = {}
        last_drain = 0
        own = 0
        for deps, fn, step in self.ops[eng_name]:
            if fn is None:
                eng.wait_ge(sems[eng_name], deps[eng_name])
                continue
            for e, v in deps.items():
                if e == eng_name:
                    if eng_name != "tensor" and v > last_drain:
                        eng.drain()
                        last_drain = own
                else:
                    if v > last_wait.get(e, 0):
                        eng.wait_ge(sems[e], v)
                        last_wait[e] = v
            inst = fn(eng)
            if step:
                own += step
                inst.then_inc(sems[eng_name], step)


def build():
    nc = bass.Bass("TRN2", target_bir_lowering=False)

    xt = nc.dram_tensor("xt", [128, NCH, S], F32, kind="ExternalInput")
    w1 = nc.dram_tensor("w1", [128, 48, NCH, 128], F32, kind="ExternalInput")
    w2 = nc.dram_tensor("w2", [128, 32, NCH, 128], F32, kind="ExternalInput")
    w3 = nc.dram_tensor("w3", [128, 32, NCH, 128], F32, kind="ExternalInput")
    gah = nc.dram_tensor("gah", [128, NCH, 36], F32, kind="ExternalInput")
    bfh = nc.dram_tensor("bfh", [32, 4096], F32, kind="ExternalInput")
    brh = nc.dram_tensor("brh", [1, 4], F32, kind="ExternalInput")
    idh = nc.dram_tensor("idh", [128, 128], F32, kind="ExternalInput")
    mkh = nc.dram_tensor("mkh", [128, 2048], F32, kind="ExternalInput")

    qscr = nc.dram_tensor("qscr", [128, 32, S], BF16, kind="Internal")
    kscr = nc.dram_tensor("kscr", [128, 8, S], BF16, kind="Internal")
    vscr = nc.dram_tensor("vscr", [128, 16, 1024], BF16, kind="Internal")
    ascr = nc.dram_tensor("ascr", [128, 32, S], BF16, kind="Internal")
    x1scr = nc.dram_tensor("x1scr", [128, NCH, S], F32, kind="Internal")
    outt = nc.dram_tensor("out", [128, NCH, S], F32, kind="ExternalOutput")

    w1r = w1.rearrange("p j c k -> p j (c k)")
    w2r = w2.rearrange("p j c k -> p j (c k)")
    w3r = w3.rearrange("p j c k -> p j (c k)")
    gar = gah.rearrange("p c e -> p (c e)")

    P = Prog()

    with ExitStack() as ex:
        E = ex.enter_context
        arena = E(nc.sbuf_tensor("arena", [128, ARENA_COLS], BF16))
        fa = E(nc.sbuf_tensor("fa", [128, 2048], F32))
        rows = E(nc.sbuf_tensor("rows", [1, 4096], F32))
        rowsb = E(nc.sbuf_tensor("rowsb", [1, 3072], BF16))
        bf_sb = E(nc.sbuf_tensor("bf_sb", [32, 4096], BF16))
        zmT = E(nc.sbuf_tensor("zmT", [32, 2048], BF16))
        zm = E(nc.sbuf_tensor("zm", [128, 32], BF16))
        mx = E(nc.sbuf_tensor("mx", [128, 1], F32))
        oh = E(nc.sbuf_tensor("oh", [128, 4], F32))
        onc = E(nc.sbuf_tensor("onc", [128, 1], BF16))
        onr = E(nc.sbuf_tensor("onr", [1, 128], BF16))
        bias_sb = E(nc.sbuf_tensor("bias_sb", [1, 4], BF16))
        rrec = E(nc.sbuf_tensor("rrec", [1, 512], F32))

        psA = E(nc.psum_tensor("psA", [128, 1024], F32))
        psB = E(nc.psum_tensor("psB", [128, 1024], F32))
        psC = E(nc.psum_tensor("psC", [128, 1024], F32))
        psD = E(nc.psum_tensor("psD", [128, 1024], F32))

        sems = {e: E(nc.semaphore(f"sem_{e}")) for e in ENGS}
        blk = E(nc.Block())

        AHg = [Buf() for _ in range(64)]

        def AH(c0, c1):
            return AHg[c0 // 1024:(c1 + 1023) // 1024]

        slabB = [Buf(), Buf()]
        stgB = [Buf(), Buf()]
        abcB = [Buf(), Buf()]
        gaB, identB, bfB, biasB = Buf(), Buf(), Buf(), Buf()
        oncB, onrB = Buf(), Buf()
        rowB = [Buf() for _ in range(4)]
        rowbB = [Buf() for _ in range(4)]
        psAh = [Buf(), Buf()]
        psBh = [Buf(), Buf()]
        psCh = [Buf(), Buf()]
        psDh = [Buf(), Buf()]
        zmB, zmTB, mxB, ohB, rrecB = Buf(), Buf(), Buf(), Buf(), Buf()
        faxB = [Buf(), Buf()]
        faoB = [Buf(), Buf()]
        qscrB = [Buf() for _ in range(32)]
        kscrB = [Buf() for _ in range(8)]
        vscrB = [Buf() for _ in range(16)]
        ascrB = [Buf() for _ in range(32)]
        x1B = [Buf() for _ in range(32)]
        out_toks = []

        SLAB = [A_SLAB0, A_SLAB1]
        STG = [A_STG0, A_STG1]
        FAX = [0, 512]
        FAO = [1024, 1536]

        def ah_ap(c0, width):
            return arena[:, c0:c0 + width]

        P.emit("gpsimd", lambda e: e.dma_start(out=bf_sb[:, :], in_=bfh[:, :]),
               writes=(bfB,), dma=True)
        P.emit("gpsimd", lambda e: e.dma_start(out=arena[:, A_GA:A_GA + 1152], in_=gar[:, :]),
               writes=(gaB,), dma=True)
        P.emit("gpsimd", lambda e: e.dma_start(out=bias_sb[:, :], in_=brh[:, :]),
               writes=(biasB,), dma=True)
        P.emit("gpsimd", lambda e: e.dma_start(out=arena[:, A_IDENT:A_IDENT + 128], in_=idh[:, :]),
               writes=(identB,), dma=True)
        P.emit("vector", lambda e: e.memset(onc[:, :], 1.0), writes=(oncB,))
        P.emit("vector", lambda e: e.memset(onr[:, :], 1.0), writes=(onrB,))

        def ln_phase(src_dram, src_bufs):
            for st in range(2):
                for c in range(NCH):
                    dst = ah_ap(c * S + st * ST, ST)
                    src = src_dram[:, c, st * ST:(st + 1) * ST]
                    extra = []
                    if src_bufs is not None:
                        extra = [src_bufs[c].w] if src_bufs[c].w else []
                    P.emit("gpsimd",
                           lambda e, d=dst, s=src: e.dma_start(out=d, in_=s),
                           writes=AH(c * S + st * ST, c * S + st * ST + ST),
                           extra=extra, dma=True)
            for st in range(2):
                for c in range(NCH):
                    col = c * S + st * ST
                    x_ap = ah_ap(col, ST)
                    sq_ap = arena[:, STG[c % 2]:STG[c % 2] + ST]
                    P.emit("vector",
                           lambda e, o=sq_ap, i=x_ap: e.tensor_mul(o, i, i),
                           reads=AH(col, col + ST), writes=(stgB[c % 2],))
                    for h in range(2):
                        P.emit("tensor",
                               lambda e, r=x_ap, hh=h, s=(c == 0), t=(c == NCH - 1):
                               e.matmul(psC[0:1, hh * 512:(hh + 1) * 512], onc[:, :],
                                        r[:, hh * 512:(hh + 1) * 512], start=s, stop=t,
                                        skip_group_check=True),
                               reads=AH(col, col + ST) + [oncB],
                               writes=(psCh[h],))
                        P.emit("tensor",
                               lambda e, r=sq_ap, hh=h, s=(c == 0), t=(c == NCH - 1):
                               e.matmul(psC[32:33, hh * 512:(hh + 1) * 512], onc[:, :],
                                        r[:, hh * 512:(hh + 1) * 512], start=s, stop=t,
                                        skip_group_check=True),
                               reads=(stgB[c % 2], oncB),
                               writes=(psCh[h],))
                P.emit("vector",
                       lambda e: e.tensor_scalar(rows[0:1, 0:1024], psC[0:1, :],
                                                 1.0 / 4096.0, None, ALU.mult),
                       reads=(psCh[0], psCh[1]), writes=(rowB[0],))
                P.emit("vector",
                       lambda e: e.tensor_scalar(rows[0:1, 1024:2048], psC[32:33, :],
                                                 1.0 / 4096.0, None, ALU.mult),
                       reads=(psCh[0], psCh[1]), writes=(rowB[1],))
                P.emit("vector",
                       lambda e: e.tensor_mul(rows[0:1, 2048:3072], rows[0:1, 0:1024], rows[0:1, 0:1024]),
                       reads=(rowB[0],), writes=(rowB[2],))
                P.emit("vector",
                       lambda e: e.tensor_sub(rows[0:1, 1024:2048], rows[0:1, 1024:2048], rows[0:1, 2048:3072]),
                       reads=(rowB[1], rowB[2]), writes=(rowB[1],))
                P.emit("scalar",
                       lambda e: e.activation(rows[0:1, 3072:4096], rows[0:1, 1024:2048], AF.Sqrt,
                                              bias=np.float32(1e-5), scale=np.float32(1.0)),
                       reads=(rowB[1],), writes=(rowB[3],))
                P.emit("vector",
                       lambda e: e.reciprocal(rows[0:1, 1024:2048], rows[0:1, 3072:4096]),
                       reads=(rowB[3],), writes=(rowB[1],))
                P.emit("scalar",
                       lambda e: e.copy(rowsb[0:1, 0:1024], rows[0:1, 1024:2048]),
                       reads=(rowB[1],), writes=(rowbB[0],))
                P.emit("vector",
                       lambda e: e.tensor_mul(rows[0:1, 2048:3072], rows[0:1, 0:1024], rows[0:1, 1024:2048]),
                       reads=(rowB[0], rowB[1]), writes=(rowB[2],))
                P.emit("vector",
                       lambda e: e.tensor_scalar(rowsb[0:1, 1024:2048], rows[0:1, 2048:3072],
                                                 -1.0, None, ALU.mult),
                       reads=(rowB[2],), writes=(rowbB[1],))
                for h in range(2):
                    P.emit("tensor",
                           lambda e, hh=h: e.matmul(psD[:, hh * 512:(hh + 1) * 512], onr[:, :],
                                              rowsb[0:1, hh * 512:(hh + 1) * 512],
                                              start=True, stop=True, skip_group_check=True),
                           reads=(rowbB[0], onrB), writes=(psDh[h],))
                    P.emit("tensor",
                           lambda e, hh=h: e.matmul(psC[:, hh * 512:(hh + 1) * 512], onr[:, :],
                                              rowsb[0:1, 1024 + hh * 512:1024 + (hh + 1) * 512],
                                              start=True, stop=True, skip_group_check=True),
                           reads=(rowbB[1], onrB), writes=(psCh[h],))
                P.emit("scalar",
                       lambda e: e.copy(arena[:, A_ABC_A:A_ABC_A + ST], psD[:, :]),
                       reads=(psDh[0], psDh[1]), writes=(abcB[0],))
                P.emit("scalar",
                       lambda e: e.copy(arena[:, A_ABC_B:A_ABC_B + ST], psC[:, :]),
                       reads=(psCh[0], psCh[1]), writes=(abcB[1],))
                for c in range(NCH):
                    col = c * S + st * ST
                    x_ap = ah_ap(col, ST)
                    a_ap = arena[:, A_ABC_A:A_ABC_A + ST]
                    b_ap = arena[:, A_ABC_B:A_ABC_B + ST]
                    P.emit("vector",
                           lambda e, x=x_ap, a=a_ap: e.tensor_mul(x, x, a),
                           reads=AH(col, col + ST) + [abcB[0]],
                           writes=AH(col, col + ST))
                    P.emit("vector",
                           lambda e, x=x_ap, b=b_ap: e.tensor_add(x, x, b),
                           reads=AH(col, col + ST) + [abcB[1]],
                           writes=AH(col, col + ST))

        slab_seq = []

        def run_slab_stream():
            n = len(slab_seq)
            for i in range(n + 1):
                if i < n:
                    src, jb, _ = slab_seq[i]
                    slot = i % 2
                    P.emit("gpsimd",
                           lambda e, s=slot, sr=src, j=jb:
                           e.dma_start(out=arena[:, SLAB[s]:SLAB[s] + 4096],
                                       in_=sr[:, j, :]),
                           writes=(slabB[slot],), dma=True)
                if i >= 1:
                    _, _, fn = slab_seq[i - 1]
                    fn((i - 1) % 2)

        ln_phase(xt, None)

        def qk_compute(jb):
            def fn(slot):
                for st in range(2):
                    ps = psA if st == 0 else psB
                    psh = psAh if st == 0 else psBh
                    for c in range(NCH):
                        col = c * S + st * ST
                        lhs = arena[:, SLAB[slot] + c * 128:SLAB[slot] + (c + 1) * 128]
                        for h in range(2):
                            rhs = ah_ap(col + h * 512, 512)
                            P.emit("tensor",
                                   lambda e, l=lhs, r=rhs, p=ps, hh=h,
                                          s=(c == 0), t=(c == NCH - 1):
                                   e.matmul(p[:, hh * 512:(hh + 1) * 512], l, r,
                                            start=s, stop=t, skip_group_check=True),
                                   reads=AH(col, col + ST) + [slabB[slot]],
                                   writes=(psh[h],))
                    stg_ap = arena[:, STG[st]:STG[st] + ST]
                    P.emit("scalar",
                           lambda e, o=stg_ap, p=ps: e.copy(o, p[:, :]),
                           reads=(psh[0], psh[1]), writes=(stgB[st],))
                    if jb < 32:
                        dst = qscr[:, jb, st * ST:(st + 1) * ST]
                        wb = qscrB[jb]
                    else:
                        dst = kscr[:, jb - 32, st * ST:(st + 1) * ST]
                        wb = kscrB[jb - 32]
                    P.emit("sync",
                           lambda e, d=dst, s=stg_ap: e.dma_start(out=d, in_=s),
                           reads=(stgB[st],), writes=(wb,), dma=True)
            return fn

        for jb in range(40):
            slab_seq.append((w1r, jb, qk_compute(jb)))

        def v_compute(dvt):
            def fn(slot):
                rot = [(psC, psCh, 0), (psC, psCh, 1), (psD, psDh, 0), (psD, psDh, 1)]
                for tsub in range(16):
                    ps, psh, hh = rot[tsub % 4]
                    ps_ap = ps[:, hh * 512:hh * 512 + 128]
                    st = tsub // 8
                    for c in range(NCH):
                        col = c * S + tsub * 128
                        lhs = ah_ap(col, 128)
                        rhs = arena[:, SLAB[slot] + c * 128:SLAB[slot] + (c + 1) * 128]
                        P.emit("tensor",
                               lambda e, l=lhs, r=rhs, p=ps_ap, s=(c == 0), t=(c == NCH - 1):
                               e.matmul(p, l, r, start=s, stop=t,
                                        skip_group_check=True),
                               reads=AH(col, col + 128) + [slabB[slot]],
                               writes=(psh[hh],))
                    stg_ap = arena[:, STG[tsub % 2]:STG[tsub % 2] + 128]
                    P.emit("scalar",
                           lambda e, o=stg_ap, p=ps_ap: e.copy(o, p),
                           reads=(psh[hh],), writes=(stgB[tsub % 2],))
                    dst = vscr[:, tsub, dvt * 128:(dvt + 1) * 128]
                    P.emit("sync",
                           lambda e, d=dst, s=stg_ap: e.dma_start(out=d, in_=s),
                           reads=(stgB[tsub % 2],), writes=(vscrB[tsub],), dma=True)
            return fn

        for dvt in range(8):
            slab_seq.append((w1r, 40 + dvt, v_compute(dvt)))

        run_slab_stream()
        slab_seq = []

        P.emit("gpsimd", lambda e: e.dma_start(out=arena[:, MASK0:MASK0 + 2048], in_=mkh[:, :]),
               writes=AH(MASK0, MASK0 + 2048), dma=True)
        QOFF, KOFF, VOFF = 0, 16384, 32768
        for st2 in range(4):
            for hq in range(32):
                dcol = QOFF + hq * 512
                P.emit("sync",
                       lambda e, h=hq, d=dcol, s=st2:
                       e.dma_start(out=arena[:, d:d + 512],
                                   in_=qscr[:, h, s * 512:(s + 1) * 512]),
                       reads=(qscrB[hq],), writes=AH(dcol, dcol + 512), dma=True)
            for kb in range(8):
                dcol = KOFF + kb * S + st2 * 512
                P.emit("sync",
                       lambda e, k=kb, d=dcol, s=st2:
                       e.dma_start(out=arena[:, d:d + 512],
                                   in_=kscr[:, k, s * 512:(s + 1) * 512]),
                       reads=(kscrB[kb],), writes=AH(dcol, dcol + 512), dma=True)
            for i in range(4):
                tc = st2 * 4 + i
                dcol = VOFF + tc * 1024
                P.emit("sync",
                       lambda e, t=tc, d=dcol:
                       e.dma_start(out=arena[:, d:d + 1024], in_=vscr[:, t, :]),
                       reads=(vscrB[tc],), writes=AH(dcol, dcol + 1024), dma=True)

            for kb in range(8):
                for g in range(4):
                    hq = kb * 4 + g
                    avt, avhl = [(psA, psAh), (psA, psAh), (psB, psBh), (psB, psBh)][g]
                    avhh = g % 2
                    av_ap = avt[:, avhh * 512:(avhh + 1) * 512]
                    ntc = st2 * 4 + 4
                    q_ap = arena[:, QOFF + hq * 512:QOFF + (hq + 1) * 512]
                    qg = AH(QOFF + hq * 512, QOFF + (hq + 1) * 512)

                    def qk_step(tc):
                        kcol = KOFF + kb * S + tc * 128
                        k_ap = arena[:, kcol:kcol + 128]
                        qk_ap = psC[:, (tc % 2) * 512:(tc % 2 + 1) * 512]
                        pt_col = PT0 + (tc % 4) * 1024
                        pt_ap = arena[:, pt_col:pt_col + 512]
                        P.emit("tensor",
                               lambda e, l=k_ap, r=q_ap, p=qk_ap:
                               e.matmul(p, l, r, start=True, stop=True,
                                        skip_group_check=True),
                               reads=AH(kcol, kcol + 128) + qg,
                               writes=(psCh[tc % 2],))
                        P.emit("scalar",
                               lambda e, o=pt_ap, i=qk_ap:
                               e.activation(o, i, AF.Exp,
                                            bias=np.float32(SHIFT), scale=np.float32(SC)),
                               reads=(psCh[tc % 2],),
                               writes=AH(pt_col, pt_col + 512))
                        if tc >= st2 * 4:
                            r_idx = tc - st2 * 4
                            m_ap = arena[:, MASK0 + r_idx * 512:MASK0 + (r_idx + 1) * 512]
                            P.emit("vector",
                                   lambda e, p=pt_ap, m=m_ap: e.tensor_mul(p, p, m),
                                   reads=AH(pt_col, pt_col + 512) +
                                         AH(MASK0 + r_idx * 512, MASK0 + (r_idx + 1) * 512),
                                   writes=AH(pt_col, pt_col + 512))

                    def dav_step(tc):
                        pt_col = PT0 + (tc % 4) * 1024
                        pt_ap = arena[:, pt_col:pt_col + 512]
                        vcol = VOFF + tc * 1024 + kb * 128
                        v_ap = arena[:, vcol:vcol + 128]
                        P.emit("tensor",
                               lambda e, p=pt_ap, s=(tc == 0), t=(tc == ntc - 1), gg=g:
                               e.matmul(psD[0:1, 0:512], onc[:, :], p,
                                        start=s, stop=t, skip_group_check=True),
                               reads=AH(pt_col, pt_col + 512) + [oncB],
                               writes=(psDh[0],))
                        P.emit("tensor",
                               lambda e, l=v_ap, p=pt_ap, o=av_ap,
                                      s=(tc == 0), t=(tc == ntc - 1):
                               e.matmul(o, l, p, start=s, stop=t,
                                        skip_group_check=True),
                               reads=AH(pt_col, pt_col + 512) + AH(vcol, vcol + 128),
                               writes=(avhl[avhh],))

                    for tc in range(ntc + 1):
                        if tc < ntc:
                            qk_step(tc)
                        if tc >= 1:
                            dav_step(tc - 1)

                    P.emit("vector",
                           lambda e: e.reciprocal(rrec[:, :], psD[0:1, 0:512]),
                           reads=(psDh[0],), writes=(rrecB,))
                    P.emit("scalar",
                           lambda e: e.copy(rowsb[0:1, 2048:2560], rrec[:, :]),
                           reads=(rrecB,), writes=(rowbB[2],))
                    P.emit("tensor",
                           lambda e: e.matmul(psD[:, 512:1024], onr[:, :],
                                              rowsb[0:1, 2048:2560],
                                              start=True, stop=True,
                                              skip_group_check=True),
                           reads=(rowbB[2], onrB), writes=(psDh[1],))
                    ar_col = ARAW0 + (g % 2) * 1024
                    ar_ap = arena[:, ar_col:ar_col + 512]
                    P.emit("scalar",
                           lambda e, o=ar_ap, i=av_ap: e.copy(o, i),
                           reads=(avhl[avhh],), writes=AH(ar_col, ar_col + 512))
                    af_col = AFIN0 + (g % 2) * 1024
                    af_ap = arena[:, af_col:af_col + 512]
                    P.emit("vector",
                           lambda e, o=af_ap, a=ar_ap:
                           e.tensor_mul(o, psD[:, 512:1024], a),
                           reads=[psDh[1]] + AH(ar_col, ar_col + 512),
                           writes=AH(af_col, af_col + 512))
                    P.emit("sync",
                           lambda e, h=hq, s=st2, a=af_ap:
                           e.dma_start(out=ascr[:, h, s * 512:(s + 1) * 512], in_=a),
                           reads=AH(af_col, af_col + 512),
                           writes=(ascrB[hq],), dma=True)

        for hq in range(32):
            dcol = hq * S
            P.emit("sync",
                   lambda e, h=hq, d=dcol:
                   e.dma_start(out=arena[:, d:d + S], in_=ascr[:, h, :]),
                   reads=(ascrB[hq],), writes=AH(dcol, dcol + S), dma=True)

        def dense_compute(ob, wsrc, out_is_x1):
            def fn(slot):
                for st in range(2):
                    ps = psA if st == 0 else psB
                    psh = psAh if st == 0 else psBh
                    for c in range(NCH):
                        col = c * S + st * ST
                        lhs = arena[:, SLAB[slot] + c * 128:SLAB[slot] + (c + 1) * 128]
                        for h in range(2):
                            rhs = ah_ap(col + h * 512, 512)
                            P.emit("tensor",
                                   lambda e, l=lhs, r=rhs, p=ps, hh=h, s=(c == 0),
                                          t=(c == NCH - 1 and out_is_x1):
                                   e.matmul(p[:, hh * 512:(hh + 1) * 512], l, r,
                                            start=s, stop=t, skip_group_check=True),
                                   reads=AH(col, col + ST) + [slabB[slot]],
                                   writes=(psh[h],))
                    if not out_is_x1:
                        for h in range(2):
                            P.emit("tensor",
                                   lambda e, p=ps, o=ob, s2=st, hh=h:
                                   e.matmul(p[:, hh * 512:(hh + 1) * 512],
                                            bf_sb[0:32, o * 128:(o + 1) * 128],
                                            zmT[0:32, s2 * ST + hh * 512:s2 * ST + (hh + 1) * 512],
                                            start=False, stop=True, skip_group_check=True),
                                   reads=(bfB, zmTB), writes=(psh[h],))
                    for half in range(2):
                        src = (xt if out_is_x1 else x1scr)[
                            :, ob, st * ST + half * 512:st * ST + (half + 1) * 512]
                        fx_ap = fa[:, FAX[half]:FAX[half] + 512]
                        extra = [] if out_is_x1 else ([x1B[ob].w] if x1B[ob].w else [])
                        P.emit("sync",
                               lambda e, d=fx_ap, s=src: e.dma_start(out=d, in_=s),
                               writes=(faxB[half],), extra=extra, dma=True)
                        fo_ap = fa[:, FAO[half]:FAO[half] + 512]
                        ps_ap = ps[:, half * 512:(half + 1) * 512]
                        P.emit("vector",
                               lambda e, o=fo_ap, p=ps_ap, x=fx_ap:
                               e.tensor_tensor(o, p, x, ALU.add),
                               reads=(psh[half], faxB[half]), writes=(faoB[half],))
                        dstd = (x1scr if out_is_x1 else outt)[
                            :, ob, st * ST + half * 512:st * ST + (half + 1) * 512]
                        wb = (x1B[ob],) if out_is_x1 else ()
                        tok = P.emit("sync",
                                     lambda e, d=dstd, s=fo_ap: e.dma_start(out=d, in_=s),
                                     reads=(faoB[half],), writes=wb, dma=True)
                        if not out_is_x1:
                            out_toks.append(tok)
            return fn

        for ob in range(32):
            slab_seq.append((w2r, ob, dense_compute(ob, w2r, True)))
        run_slab_stream()
        slab_seq = []

        ln_phase(x1scr, x1B)

        for ssg in range(16):
            scol = ssg * 128
            for c in range(NCH):
                col = c * S + scol
                lhs = ah_ap(col, 128)
                rhs = arena[:, A_GA + c * 36:A_GA + (c + 1) * 36]
                P.emit("tensor",
                       lambda e, l=lhs, r=rhs, s=(c == 0):
                       e.matmul(psC[:, 0:36], l, r, start=s, stop=False,
                                skip_group_check=True),
                       reads=AH(col, col + 128) + [gaB],
                       writes=(psCh[0],))
            P.emit("tensor",
                   lambda e: e.matmul(psC[:, 0:4], onr[:, :], bias_sb[:, :],
                                      start=False, stop=True, skip_group_check=True),
                   reads=(biasB, onrB), writes=(psCh[0],))
            P.emit("vector",
                   lambda e: e.reduce_max(mx[:, :], psC[:, 0:4], axis=AX.X),
                   reads=(psCh[0],), writes=(mxB,))
            P.emit("vector",
                   lambda e: e.tensor_scalar(oh[:, :], psC[:, 0:4], mx[:, 0:1],
                                             None, ALU.is_ge),
                   reads=(psCh[0], mxB), writes=(ohB,))
            for ee in range(4):
                P.emit("vector",
                       lambda e, v=ee: e.tensor_scalar(zm[:, v * 8:(v + 1) * 8],
                                                       psC[:, 4 + v * 8:4 + (v + 1) * 8],
                                                       oh[:, v:v + 1], None, ALU.mult),
                       reads=(psCh[0], ohB), writes=(zmB,))
            P.emit("tensor",
                   lambda e: e.matmul(psD[0:32, 0:128], zm[:, :],
                                      arena[:, A_IDENT:A_IDENT + 128],
                                      start=True, stop=True, skip_group_check=True),
                   reads=(zmB, identB), writes=(psDh[0],))
            P.emit("scalar",
                   lambda e, sc=scol: e.copy(zmT[0:32, sc:sc + 128], psD[0:32, 0:128]),
                   reads=(psDh[0],), writes=(zmTB,))

        for ob in range(32):
            slab_seq.append((w3r, ob, dense_compute(ob, w3r, False)))
        run_slab_stream()

        P.emit("gpsimd", lambda e: e.nop(), extra=out_toks)

        P.finalize()

        blk.tensor(lambda e: P.replay("tensor", e, sems))
        blk.vector(lambda e: P.replay("vector", e, sems))
        blk.scalar(lambda e: P.replay("scalar", e, sems))
        blk.sync(lambda e: P.replay("sync", e, sems))
        blk.gpsimd(lambda e: P.replay("gpsimd", e, sems))

    return nc


_cache = {}


def kernel(L0_attn_q, L0_attn_w_qkv, L0_attn_w_dense, L0_ffn_W_up,
           L0_ffn_gate_w, L0_ffn_expert_bias, L0_ffn_lora_A, L0_ffn_lora_B):
    x = np.asarray(L0_attn_q, dtype=np.float32)
    xt_all = np.ascontiguousarray(
        x.reshape(8, 2048, 32, 128).transpose(0, 3, 2, 1))
    wq = np.ascontiguousarray(np.asarray(L0_attn_w_qkv, np.float32)
                              .reshape(48, 128, 32, 128).transpose(3, 0, 2, 1))
    wd = np.ascontiguousarray(np.asarray(L0_attn_w_dense, np.float32)
                              .reshape(32, 128, 32, 128).transpose(3, 0, 2, 1))
    wu = np.ascontiguousarray(np.asarray(L0_ffn_W_up, np.float32)
                              .reshape(32, 128, 32, 128).transpose(3, 0, 2, 1))
    ga = np.zeros((128, 32, 36), np.float32)
    ga[:, :, 0:4] = np.asarray(L0_ffn_gate_w, np.float32) \
        .reshape(4, 32, 128).transpose(2, 1, 0)
    ga[:, :, 4:36] = np.asarray(L0_ffn_lora_A, np.float32) \
        .reshape(32, 32, 128).transpose(2, 1, 0)
    bf = np.ascontiguousarray(np.asarray(L0_ffn_lora_B, np.float32)
                              .transpose(0, 2, 1).reshape(32, 4096))
    br = np.asarray(L0_ffn_expert_bias, np.float32).reshape(1, 4)
    idh = np.eye(128, dtype=np.float32)
    mkh = np.zeros((128, 2048), np.float32)
    pp = np.arange(128)[:, None]
    cc = np.arange(512)[None, :]
    for r_i, r in enumerate((0, 128, 256, 384)):
        mkh[:, r_i * 512:(r_i + 1) * 512] = (pp + r <= cc).astype(np.float32)

    if "nc" not in _cache:
        _cache["nc"] = build()
    nc = _cache["nc"]
    shared = dict(w1=wq, w2=wd, w3=wu, gah=ga, bfh=bf, brh=br, idh=idh, mkh=mkh)
    in_maps = [dict(xt=xt_all[i], **shared) for i in range(8)]
    res = run_bass_kernel_spmd(nc, in_maps, core_ids=list(range(8)))
    out = np.stack([res.results[i]["out"] for i in range(8)])
    return np.ascontiguousarray(
        out.transpose(0, 3, 2, 1).reshape(8, 2048, 4096)).astype(np.float32)


# version 04
# speedup vs baseline: 1.1035x; 1.1035x over previous
import math
import numpy as np
import concourse.bass as bass
import concourse.mybir as mybir
from concourse.bass_utils import run_bass_kernel_spmd
from contextlib import ExitStack

F32 = mybir.dt.float32
BF16 = mybir.dt.bfloat16
AF = mybir.ActivationFunctionType
ALU = mybir.AluOpType
AX = mybir.AxisListType

S = 2048
NCH = 32
ST = 1024
AST = 512
SC = 1.0 / math.sqrt(128.0)
SHIFT = -12.0

ENGS = ("tensor", "vector", "scalar", "sync", "gpsimd")

A_H = 0
A_SLAB0 = 65536
A_SLAB1 = 69632
A_STG0 = 73728
A_STG1 = 74752
A_ABC_A = 75776
A_ABC_B = 76800
A_GA = 77824
A_IDENT = 78976
ARENA_COLS = 79104

PT0 = 48 * 1024
ARAW0 = 52 * 1024
AFIN0 = 54 * 1024
MASK0 = 56 * 1024


class Buf:
    __slots__ = ("w", "r")

    def __init__(self):
        self.w = None
        self.r = []


class Prog:
    def __init__(self):
        self.ops = {e: [] for e in ENGS}
        self.cnt = {e: 0 for e in ENGS}

    def emit(self, eng, fn, reads=(), writes=(), extra=(), dma=False):
        deps = {}

        def add(t):
            if t is None:
                return
            e, v = t
            if v > deps.get(e, 0):
                deps[e] = v

        for t in extra:
            add(t)
        for b in reads:
            add(b.w)
        for b in writes:
            add(b.w)
            for t in b.r:
                add(t)
        if dma:
            step = 16
        elif eng in ("sync", "gpsimd"):
            step = 0
        else:
            step = 1
        self.ops[eng].append((deps, fn, step))
        self.cnt[eng] += step
        tok = (eng, self.cnt[eng])
        for b in reads:
            b.r.append(tok)
        for b in writes:
            b.w = tok
            b.r = []
        return tok

    def finalize(self):
        waited = {"sync": set(), "gpsimd": set()}
        for ename in ENGS:
            for deps, fn, step in self.ops[ename]:
                for de, dv in deps.items():
                    if de in waited:
                        waited[de].add(dv)
        for q in ("sync", "gpsimd"):
            newops = []
            cum = 0
            for deps, fn, step in self.ops[q]:
                deps = {k: v for k, v in deps.items() if k != q}
                newops.append((deps, fn, step))
                if step == 16:
                    cum += 16
                    if cum in waited[q]:
                        newops.append(({q: cum}, None, 0))
            self.ops[q] = newops

    def replay(self, eng_name, eng, sems):
        last_wait = {}
        last_drain = 0
        own = 0
        for deps, fn, step in self.ops[eng_name]:
            if fn is None:
                eng.wait_ge(sems[eng_name], deps[eng_name])
                continue
            for e, v in deps.items():
                if e == eng_name:
                    if eng_name != "tensor" and v > last_drain:
                        eng.drain()
                        last_drain = own
                else:
                    if v > last_wait.get(e, 0):
                        eng.wait_ge(sems[e], v)
                        last_wait[e] = v
            inst = fn(eng)
            if step:
                own += step
                inst.then_inc(sems[eng_name], step)


def build():
    nc = bass.Bass("TRN2", target_bir_lowering=False)

    xt = nc.dram_tensor("xt", [128, NCH, S], F32, kind="ExternalInput")
    w1 = nc.dram_tensor("w1", [128, 48, NCH, 128], F32, kind="ExternalInput")
    w2 = nc.dram_tensor("w2", [128, 32, NCH, 128], F32, kind="ExternalInput")
    w3 = nc.dram_tensor("w3", [128, 32, NCH, 128], F32, kind="ExternalInput")
    gah = nc.dram_tensor("gah", [128, NCH, 36], F32, kind="ExternalInput")
    bfh = nc.dram_tensor("bfh", [32, 4096], F32, kind="ExternalInput")
    brh = nc.dram_tensor("brh", [1, 4], F32, kind="ExternalInput")
    idh = nc.dram_tensor("idh", [128, 128], F32, kind="ExternalInput")
    mkh = nc.dram_tensor("mkh", [128, 2048], F32, kind="ExternalInput")

    qscr = nc.dram_tensor("qscr", [128, 32, S], BF16, kind="Internal")
    kscr = nc.dram_tensor("kscr", [128, 8, S], BF16, kind="Internal")
    vscr = nc.dram_tensor("vscr", [128, 16, 1024], BF16, kind="Internal")
    ascr = nc.dram_tensor("ascr", [128, 32, S], BF16, kind="Internal")
    x1scr = nc.dram_tensor("x1scr", [128, NCH, S], F32, kind="Internal")
    outt = nc.dram_tensor("out", [128, NCH, S], F32, kind="ExternalOutput")

    w1r = w1.rearrange("p j c k -> p j (c k)")
    w2r = w2.rearrange("p j c k -> p j (c k)")
    w3r = w3.rearrange("p j c k -> p j (c k)")
    gar = gah.rearrange("p c e -> p (c e)")

    P = Prog()

    with ExitStack() as ex:
        E = ex.enter_context
        arena = E(nc.sbuf_tensor("arena", [128, ARENA_COLS], BF16))
        fa = E(nc.sbuf_tensor("fa", [128, 2048], F32))
        rows = E(nc.sbuf_tensor("rows", [1, 4096], F32))
        rowsb = E(nc.sbuf_tensor("rowsb", [1, 3072], BF16))
        bf_sb = E(nc.sbuf_tensor("bf_sb", [32, 4096], BF16))
        zmT = E(nc.sbuf_tensor("zmT", [32, 2048], BF16))
        zm = E(nc.sbuf_tensor("zm", [128, 32], BF16))
        mx = E(nc.sbuf_tensor("mx", [128, 1], F32))
        oh = E(nc.sbuf_tensor("oh", [128, 4], F32))
        onc = E(nc.sbuf_tensor("onc", [128, 1], BF16))
        onr = E(nc.sbuf_tensor("onr", [1, 128], BF16))
        bias_sb = E(nc.sbuf_tensor("bias_sb", [1, 4], BF16))
        rrec = E(nc.sbuf_tensor("rrec", [1, 512], F32))
        rden = E(nc.sbuf_tensor("rden", [1, 512], F32))

        psA = E(nc.psum_tensor("psA", [128, 1024], F32))
        psB = E(nc.psum_tensor("psB", [128, 1024], F32))
        psC = E(nc.psum_tensor("psC", [128, 1024], F32))
        psD = E(nc.psum_tensor("psD", [128, 1024], F32))

        sems = {e: E(nc.semaphore(f"sem_{e}")) for e in ENGS}
        blk = E(nc.Block())

        AHg = [Buf() for _ in range(64)]

        def AH(c0, c1):
            return AHg[c0 // 1024:(c1 + 1023) // 1024]

        slabB = [Buf(), Buf()]
        stgB = [Buf(), Buf()]
        abcB = [Buf(), Buf()]
        gaB, identB, bfB, biasB = Buf(), Buf(), Buf(), Buf()
        oncB, onrB = Buf(), Buf()
        rowB = [Buf() for _ in range(4)]
        rowbB = [Buf() for _ in range(4)]
        psAh = [Buf(), Buf()]
        psBh = [Buf(), Buf()]
        psCh = [Buf(), Buf()]
        psDh = [Buf(), Buf()]
        psDd = [Buf(), Buf()]
        zmB, zmTB, mxB, ohB, rrecB = Buf(), Buf(), Buf(), Buf(), Buf()
        rdenB = Buf()
        faxB = [Buf(), Buf()]
        faoB = [Buf(), Buf()]
        qscrB = [Buf() for _ in range(32)]
        kscrB = [Buf() for _ in range(8)]
        vscrB = [Buf() for _ in range(16)]
        ascrB = [Buf() for _ in range(32)]
        x1B = [Buf() for _ in range(32)]
        out_toks = []

        SLAB = [A_SLAB0, A_SLAB1]
        STG = [A_STG0, A_STG1]
        FAX = [0, 512]
        FAO = [1024, 1536]

        def ah_ap(c0, width):
            return arena[:, c0:c0 + width]

        P.emit("gpsimd", lambda e: e.dma_start(out=bf_sb[:, :], in_=bfh[:, :]),
               writes=(bfB,), dma=True)
        P.emit("gpsimd", lambda e: e.dma_start(out=arena[:, A_GA:A_GA + 1152], in_=gar[:, :]),
               writes=(gaB,), dma=True)
        P.emit("gpsimd", lambda e: e.dma_start(out=bias_sb[:, :], in_=brh[:, :]),
               writes=(biasB,), dma=True)
        P.emit("gpsimd", lambda e: e.dma_start(out=arena[:, A_IDENT:A_IDENT + 128], in_=idh[:, :]),
               writes=(identB,), dma=True)
        P.emit("vector", lambda e: e.memset(onc[:, :], 1.0), writes=(oncB,))
        P.emit("vector", lambda e: e.memset(onr[:, :], 1.0), writes=(onrB,))

        def ln_phase(src_dram, src_bufs):
            ahview = arena[:, 0:65536].rearrange("p (c s) -> p c s", c=32)
            for st in range(2):
                for cg in range(4):
                    c0 = cg * 8
                    dst = ahview[:, c0:c0 + 8, st * ST:(st + 1) * ST]
                    srcv = src_dram[:, c0:c0 + 8, st * ST:(st + 1) * ST]
                    extra = []
                    if src_bufs is not None:
                        extra = [src_bufs[c].w for c in range(c0, c0 + 8)
                                 if src_bufs[c].w]
                    wr = []
                    for c in range(c0, c0 + 8):
                        wr.extend(AH(c * S + st * ST, c * S + st * ST + ST))
                    P.emit("gpsimd",
                           lambda e, d=dst, s=srcv: e.dma_start(out=d, in_=s),
                           writes=wr, extra=extra, dma=True)
            for st in range(2):
                for c in range(NCH):
                    col = c * S + st * ST
                    x_ap = ah_ap(col, ST)
                    sq_ap = arena[:, STG[c % 2]:STG[c % 2] + ST]
                    P.emit("vector",
                           lambda e, o=sq_ap, i=x_ap: e.tensor_mul(o, i, i),
                           reads=AH(col, col + ST), writes=(stgB[c % 2],))
                    for h in range(2):
                        P.emit("tensor",
                               lambda e, r=x_ap, hh=h, s=(c == 0), t=(c == NCH - 1):
                               e.matmul(psC[0:1, hh * 512:(hh + 1) * 512], onc[:, :],
                                        r[:, hh * 512:(hh + 1) * 512], start=s, stop=t,
                                        skip_group_check=True),
                               reads=AH(col, col + ST) + [oncB],
                               writes=(psCh[h],))
                        P.emit("tensor",
                               lambda e, r=sq_ap, hh=h, s=(c == 0), t=(c == NCH - 1):
                               e.matmul(psC[32:33, hh * 512:(hh + 1) * 512], onc[:, :],
                                        r[:, hh * 512:(hh + 1) * 512], start=s, stop=t,
                                        skip_group_check=True),
                               reads=(stgB[c % 2], oncB),
                               writes=(psCh[h],))
                P.emit("vector",
                       lambda e: e.tensor_scalar(rows[0:1, 0:1024], psC[0:1, :],
                                                 1.0 / 4096.0, None, ALU.mult),
                       reads=(psCh[0], psCh[1]), writes=(rowB[0],))
                P.emit("vector",
                       lambda e: e.tensor_scalar(rows[0:1, 1024:2048], psC[32:33, :],
                                                 1.0 / 4096.0, None, ALU.mult),
                       reads=(psCh[0], psCh[1]), writes=(rowB[1],))
                P.emit("vector",
                       lambda e: e.tensor_mul(rows[0:1, 2048:3072], rows[0:1, 0:1024], rows[0:1, 0:1024]),
                       reads=(rowB[0],), writes=(rowB[2],))
                P.emit("vector",
                       lambda e: e.tensor_sub(rows[0:1, 1024:2048], rows[0:1, 1024:2048], rows[0:1, 2048:3072]),
                       reads=(rowB[1], rowB[2]), writes=(rowB[1],))
                P.emit("scalar",
                       lambda e: e.activation(rows[0:1, 3072:4096], rows[0:1, 1024:2048], AF.Sqrt,
                                              bias=np.float32(1e-5), scale=np.float32(1.0)),
                       reads=(rowB[1],), writes=(rowB[3],))
                P.emit("vector",
                       lambda e: e.reciprocal(rows[0:1, 1024:2048], rows[0:1, 3072:4096]),
                       reads=(rowB[3],), writes=(rowB[1],))
                P.emit("scalar",
                       lambda e: e.copy(rowsb[0:1, 0:1024], rows[0:1, 1024:2048]),
                       reads=(rowB[1],), writes=(rowbB[0],))
                P.emit("vector",
                       lambda e: e.tensor_mul(rows[0:1, 2048:3072], rows[0:1, 0:1024], rows[0:1, 1024:2048]),
                       reads=(rowB[0], rowB[1]), writes=(rowB[2],))
                P.emit("vector",
                       lambda e: e.tensor_scalar(rowsb[0:1, 1024:2048], rows[0:1, 2048:3072],
                                                 -1.0, None, ALU.mult),
                       reads=(rowB[2],), writes=(rowbB[1],))
                for h in range(2):
                    P.emit("tensor",
                           lambda e, hh=h: e.matmul(psD[:, hh * 512:(hh + 1) * 512], onr[:, :],
                                              rowsb[0:1, hh * 512:(hh + 1) * 512],
                                              start=True, stop=True, skip_group_check=True),
                           reads=(rowbB[0], onrB),
                           writes=(psDh[h], psDd[0], psDd[1]) if h == 0 else (psDh[h],))
                    P.emit("tensor",
                           lambda e, hh=h: e.matmul(psC[:, hh * 512:(hh + 1) * 512], onr[:, :],
                                              rowsb[0:1, 1024 + hh * 512:1024 + (hh + 1) * 512],
                                              start=True, stop=True, skip_group_check=True),
                           reads=(rowbB[1], onrB), writes=(psCh[h],))
                P.emit("scalar",
                       lambda e: e.copy(arena[:, A_ABC_A:A_ABC_A + ST], psD[:, :]),
                       reads=(psDh[0], psDh[1]), writes=(abcB[0],))
                P.emit("scalar",
                       lambda e: e.copy(arena[:, A_ABC_B:A_ABC_B + ST], psC[:, :]),
                       reads=(psCh[0], psCh[1]), writes=(abcB[1],))
                for c in range(NCH):
                    col = c * S + st * ST
                    x_ap = ah_ap(col, ST)
                    a_ap = arena[:, A_ABC_A:A_ABC_A + ST]
                    b_ap = arena[:, A_ABC_B:A_ABC_B + ST]
                    P.emit("vector",
                           lambda e, x=x_ap, a=a_ap: e.tensor_mul(x, x, a),
                           reads=AH(col, col + ST) + [abcB[0]],
                           writes=AH(col, col + ST))
                    P.emit("vector",
                           lambda e, x=x_ap, b=b_ap: e.tensor_add(x, x, b),
                           reads=AH(col, col + ST) + [abcB[1]],
                           writes=AH(col, col + ST))

        slab_seq = []

        def run_slab_stream():
            n = len(slab_seq)
            for i in range(n + 1):
                if i < n:
                    src, jb, _ = slab_seq[i]
                    slot = i % 2
                    P.emit("gpsimd",
                           lambda e, s=slot, sr=src, j=jb:
                           e.dma_start(out=arena[:, SLAB[s]:SLAB[s] + 4096],
                                       in_=sr[:, j, :]),
                           writes=(slabB[slot],), dma=True)
                if i >= 1:
                    _, _, fn = slab_seq[i - 1]
                    fn((i - 1) % 2)

        ln_phase(xt, None)

        def qk_compute(jb):
            def fn(slot):
                for st in range(2):
                    ps = psA if st == 0 else psB
                    psh = psAh if st == 0 else psBh
                    for c in range(NCH):
                        col = c * S + st * ST
                        lhs = arena[:, SLAB[slot] + c * 128:SLAB[slot] + (c + 1) * 128]
                        for h in range(2):
                            rhs = ah_ap(col + h * 512, 512)
                            P.emit("tensor",
                                   lambda e, l=lhs, r=rhs, p=ps, hh=h,
                                          s=(c == 0), t=(c == NCH - 1):
                                   e.matmul(p[:, hh * 512:(hh + 1) * 512], l, r,
                                            start=s, stop=t, skip_group_check=True),
                                   reads=AH(col, col + ST) + [slabB[slot]],
                                   writes=(psh[h],))
                    stg_ap = arena[:, STG[st]:STG[st] + ST]
                    P.emit("scalar",
                           lambda e, o=stg_ap, p=ps: e.copy(o, p[:, :]),
                           reads=(psh[0], psh[1]), writes=(stgB[st],))
                    if jb < 32:
                        dst = qscr[:, jb, st * ST:(st + 1) * ST]
                        wb = qscrB[jb]
                    else:
                        dst = kscr[:, jb - 32, st * ST:(st + 1) * ST]
                        wb = kscrB[jb - 32]
                    P.emit("sync",
                           lambda e, d=dst, s=stg_ap: e.dma_start(out=d, in_=s),
                           reads=(stgB[st],), writes=(wb,), dma=True)
            return fn

        for jb in range(40):
            slab_seq.append((w1r, jb, qk_compute(jb)))

        def v_compute(dvt):
            def fn(slot):
                rot = [(psC, psCh, 0), (psC, psCh, 1), (psD, psDh, 0), (psD, psDh, 1)]
                for tsub in range(16):
                    ps, psh, hh = rot[tsub % 4]
                    ps_ap = ps[:, hh * 512:hh * 512 + 128]
                    st = tsub // 8
                    vw = (psh[hh], psDd[0], psDd[1]) if (ps is psD and hh == 0) else (psh[hh],)
                    for c in range(NCH):
                        col = c * S + tsub * 128
                        lhs = ah_ap(col, 128)
                        rhs = arena[:, SLAB[slot] + c * 128:SLAB[slot] + (c + 1) * 128]
                        P.emit("tensor",
                               lambda e, l=lhs, r=rhs, p=ps_ap, s=(c == 0), t=(c == NCH - 1):
                               e.matmul(p, l, r, start=s, stop=t,
                                        skip_group_check=True),
                               reads=AH(col, col + 128) + [slabB[slot]],
                               writes=vw)
                    stg_ap = arena[:, STG[tsub % 2]:STG[tsub % 2] + 128]
                    P.emit("scalar",
                           lambda e, o=stg_ap, p=ps_ap: e.copy(o, p),
                           reads=(psh[hh],), writes=(stgB[tsub % 2],))
                    dst = vscr[:, tsub, dvt * 128:(dvt + 1) * 128]
                    P.emit("sync",
                           lambda e, d=dst, s=stg_ap: e.dma_start(out=d, in_=s),
                           reads=(stgB[tsub % 2],), writes=(vscrB[tsub],), dma=True)
            return fn

        for dvt in range(8):
            slab_seq.append((w1r, 40 + dvt, v_compute(dvt)))

        run_slab_stream()
        slab_seq = []

        P.emit("gpsimd", lambda e: e.dma_start(out=arena[:, MASK0:MASK0 + 2048], in_=mkh[:, :]),
               writes=AH(MASK0, MASK0 + 2048), dma=True)
        QOFF, KOFF, VOFF = 0, 16384, 32768
        pend = [None]
        qview = arena[:, QOFF:QOFF + 16384].rearrange("p (hq s) -> p hq s", hq=32)
        kview = arena[:, KOFF:KOFF + 16384].rearrange("p (kb s) -> p kb s", kb=8)
        vflat = vscr.rearrange("p t d -> p (t d)")
        for st2 in range(4):
            for qg4 in range(4):
                q0 = qg4 * 8
                P.emit("sync",
                       lambda e, q=q0, s=st2:
                       e.dma_start(out=qview[:, q:q + 8, :],
                                   in_=qscr[:, q:q + 8, s * 512:(s + 1) * 512]),
                       extra=[qscrB[q0 + j].w for j in range(8) if qscrB[q0 + j].w],
                       writes=AH(QOFF + q0 * 512, QOFF + (q0 + 8) * 512), dma=True)
            P.emit("sync",
                   lambda e, s=st2:
                   e.dma_start(out=kview[:, :, s * 512:(s + 1) * 512],
                               in_=kscr[:, :, s * 512:(s + 1) * 512]),
                   extra=[kscrB[j].w for j in range(8) if kscrB[j].w],
                   writes=AH(KOFF, KOFF + 16384), dma=True)
            dcol = VOFF + st2 * 4096
            P.emit("sync",
                   lambda e, s=st2, d=dcol:
                   e.dma_start(out=arena[:, d:d + 4096],
                               in_=vflat[:, s * 4096:(s + 1) * 4096]),
                   extra=[vscrB[st2 * 4 + j].w for j in range(4) if vscrB[st2 * 4 + j].w],
                   writes=AH(dcol, dcol + 4096), dma=True)

            for kb in range(8):
                for g in range(4):
                    hq = kb * 4 + g
                    avt, avhl = [(psA, psAh), (psA, psAh), (psB, psBh), (psB, psBh)][g]
                    avhh = g % 2
                    av_ap = avt[:, avhh * 512:(avhh + 1) * 512]
                    ntc = st2 * 4 + 4
                    q_ap = arena[:, QOFF + hq * 512:QOFF + (hq + 1) * 512]
                    qg = AH(QOFF + hq * 512, QOFF + (hq + 1) * 512)

                    def qk_step(tc):
                        kcol = KOFF + kb * S + tc * 128
                        k_ap = arena[:, kcol:kcol + 128]
                        qk_ap = psC[:, (tc % 2) * 512:(tc % 2 + 1) * 512]
                        pt_col = PT0 + (tc % 4) * 1024
                        pt_ap = arena[:, pt_col:pt_col + 512]
                        P.emit("tensor",
                               lambda e, l=k_ap, r=q_ap, p=qk_ap:
                               e.matmul(p, l, r, start=True, stop=True,
                                        skip_group_check=True),
                               reads=AH(kcol, kcol + 128) + qg,
                               writes=(psCh[tc % 2],))
                        P.emit("scalar",
                               lambda e, o=pt_ap, i=qk_ap:
                               e.activation(o, i, AF.Exp,
                                            bias=np.float32(SHIFT), scale=np.float32(SC)),
                               reads=(psCh[tc % 2],),
                               writes=AH(pt_col, pt_col + 512))
                        if tc >= st2 * 4:
                            r_idx = tc - st2 * 4
                            m_ap = arena[:, MASK0 + r_idx * 512:MASK0 + (r_idx + 1) * 512]
                            P.emit("vector",
                                   lambda e, p=pt_ap, m=m_ap: e.tensor_mul(p, p, m),
                                   reads=AH(pt_col, pt_col + 512) +
                                         AH(MASK0 + r_idx * 512, MASK0 + (r_idx + 1) * 512),
                                   writes=AH(pt_col, pt_col + 512))

                    def dav_step(tc):
                        pt_col = PT0 + (tc % 4) * 1024
                        pt_ap = arena[:, pt_col:pt_col + 512]
                        vcol = VOFF + tc * 1024 + kb * 128
                        v_ap = arena[:, vcol:vcol + 128]
                        P.emit("tensor",
                               lambda e, p=pt_ap, s=(tc == 0), t=(tc == ntc - 1), gg=g:
                               e.matmul(psD[32 * (gg % 2):32 * (gg % 2) + 1, 0:512],
                                        onc[:, :], p,
                                        start=s, stop=t, skip_group_check=True),
                               reads=AH(pt_col, pt_col + 512) + [oncB],
                               writes=(psDd[g % 2],))
                        P.emit("tensor",
                               lambda e, l=v_ap, p=pt_ap, o=av_ap,
                                      s=(tc == 0), t=(tc == ntc - 1):
                               e.matmul(o, l, p, start=s, stop=t,
                                        skip_group_check=True),
                               reads=AH(pt_col, pt_col + 512) + AH(vcol, vcol + 128),
                               writes=(avhl[avhh],))

                    for tc in range(ntc + 1):
                        if tc < ntc:
                            qk_step(tc)
                        if tc >= 1:
                            dav_step(tc - 1)

                    def make_evac(g=g, hq=hq, st2=st2, av_ap=av_ap,
                                  avhl=avhl, avhh=avhh):
                        def evac():
                            P.emit("vector",
                                   lambda e, gg=g: e.reciprocal(
                                       rrec[:, :],
                                       psD[32 * (gg % 2):32 * (gg % 2) + 1, 0:512]),
                                   reads=(psDd[g % 2],), writes=(rrecB,))
                            P.emit("scalar",
                                   lambda e: e.copy(rowsb[0:1, 2048:2560], rrec[:, :]),
                                   reads=(rrecB,), writes=(rowbB[2],))
                            P.emit("tensor",
                                   lambda e: e.matmul(psD[:, 512:1024], onr[:, :],
                                                      rowsb[0:1, 2048:2560],
                                                      start=True, stop=True,
                                                      skip_group_check=True),
                                   reads=(rowbB[2], onrB), writes=(psDh[1],))
                            ar_col = ARAW0 + (g % 2) * 1024
                            ar_ap = arena[:, ar_col:ar_col + 512]
                            P.emit("scalar",
                                   lambda e, o=ar_ap, i=av_ap: e.copy(o, i),
                                   reads=(avhl[avhh],), writes=AH(ar_col, ar_col + 512))
                            af_col = AFIN0 + (g % 2) * 1024
                            af_ap = arena[:, af_col:af_col + 512]
                            P.emit("vector",
                                   lambda e, o=af_ap, a=ar_ap:
                                   e.tensor_mul(o, psD[:, 512:1024], a),
                                   reads=[psDh[1]] + AH(ar_col, ar_col + 512),
                                   writes=AH(af_col, af_col + 512))
                            P.emit("sync",
                                   lambda e, h=hq, s=st2, a=af_ap:
                                   e.dma_start(out=ascr[:, h, s * 512:(s + 1) * 512], in_=a),
                                   reads=AH(af_col, af_col + 512),
                                   writes=(ascrB[hq],), dma=True)
                        return evac
                    if pend[0] is not None:
                        pend[0]()
                    pend[0] = make_evac()

        if pend[0] is not None:
            pend[0]()
            pend[0] = None

        for hq in range(32):
            dcol = hq * S
            P.emit("sync",
                   lambda e, h=hq, d=dcol:
                   e.dma_start(out=arena[:, d:d + S], in_=ascr[:, h, :]),
                   reads=(ascrB[hq],), writes=AH(dcol, dcol + S), dma=True)

        def dense_compute(ob, wsrc, out_is_x1):
            def fn(slot):
                for st in range(2):
                    ps = psA if st == 0 else psB
                    psh = psAh if st == 0 else psBh
                    for c in range(NCH):
                        col = c * S + st * ST
                        lhs = arena[:, SLAB[slot] + c * 128:SLAB[slot] + (c + 1) * 128]
                        for h in range(2):
                            rhs = ah_ap(col + h * 512, 512)
                            P.emit("tensor",
                                   lambda e, l=lhs, r=rhs, p=ps, hh=h, s=(c == 0),
                                          t=(c == NCH - 1 and out_is_x1):
                                   e.matmul(p[:, hh * 512:(hh + 1) * 512], l, r,
                                            start=s, stop=t, skip_group_check=True),
                                   reads=AH(col, col + ST) + [slabB[slot]],
                                   writes=(psh[h],))
                    if not out_is_x1:
                        for h in range(2):
                            P.emit("tensor",
                                   lambda e, p=ps, o=ob, s2=st, hh=h:
                                   e.matmul(p[:, hh * 512:(hh + 1) * 512],
                                            bf_sb[0:32, o * 128:(o + 1) * 128],
                                            zmT[0:32, s2 * ST + hh * 512:s2 * ST + (hh + 1) * 512],
                                            start=False, stop=True, skip_group_check=True),
                                   reads=(bfB, zmTB), writes=(psh[h],))
                    for half in range(2):
                        src = (xt if out_is_x1 else x1scr)[
                            :, ob, st * ST + half * 512:st * ST + (half + 1) * 512]
                        fx_ap = fa[:, FAX[half]:FAX[half] + 512]
                        extra = [] if out_is_x1 else ([x1B[ob].w] if x1B[ob].w else [])
                        P.emit("sync",
                               lambda e, d=fx_ap, s=src: e.dma_start(out=d, in_=s),
                               writes=(faxB[half],), extra=extra, dma=True)
                        fo_ap = fa[:, FAO[half]:FAO[half] + 512]
                        ps_ap = ps[:, half * 512:(half + 1) * 512]
                        P.emit("vector",
                               lambda e, o=fo_ap, p=ps_ap, x=fx_ap:
                               e.tensor_tensor(o, p, x, ALU.add),
                               reads=(psh[half], faxB[half]), writes=(faoB[half],))
                        dstd = (x1scr if out_is_x1 else outt)[
                            :, ob, st * ST + half * 512:st * ST + (half + 1) * 512]
                        wb = (x1B[ob],) if out_is_x1 else ()
                        tok = P.emit("sync",
                                     lambda e, d=dstd, s=fo_ap: e.dma_start(out=d, in_=s),
                                     reads=(faoB[half],), writes=wb, dma=True)
                        if not out_is_x1:
                            out_toks.append(tok)
            return fn

        for ob in range(32):
            slab_seq.append((w2r, ob, dense_compute(ob, w2r, True)))
        run_slab_stream()
        slab_seq = []

        ln_phase(x1scr, x1B)

        for ssg in range(16):
            scol = ssg * 128
            for c in range(NCH):
                col = c * S + scol
                lhs = ah_ap(col, 128)
                rhs = arena[:, A_GA + c * 36:A_GA + (c + 1) * 36]
                P.emit("tensor",
                       lambda e, l=lhs, r=rhs, s=(c == 0):
                       e.matmul(psC[:, 0:36], l, r, start=s, stop=False,
                                skip_group_check=True),
                       reads=AH(col, col + 128) + [gaB],
                       writes=(psCh[0],))
            P.emit("tensor",
                   lambda e: e.matmul(psC[:, 0:4], onr[:, :], bias_sb[:, :],
                                      start=False, stop=True, skip_group_check=True),
                   reads=(biasB, onrB), writes=(psCh[0],))
            P.emit("vector",
                   lambda e: e.reduce_max(mx[:, :], psC[:, 0:4], axis=AX.X),
                   reads=(psCh[0],), writes=(mxB,))
            P.emit("vector",
                   lambda e: e.tensor_scalar(oh[:, :], psC[:, 0:4], mx[:, 0:1],
                                             None, ALU.is_ge),
                   reads=(psCh[0], mxB), writes=(ohB,))
            for ee in range(4):
                P.emit("vector",
                       lambda e, v=ee: e.tensor_scalar(zm[:, v * 8:(v + 1) * 8],
                                                       psC[:, 4 + v * 8:4 + (v + 1) * 8],
                                                       oh[:, v:v + 1], None, ALU.mult),
                       reads=(psCh[0], ohB), writes=(zmB,))
            P.emit("tensor",
                   lambda e: e.matmul(psD[0:32, 0:128], zm[:, :],
                                      arena[:, A_IDENT:A_IDENT + 128],
                                      start=True, stop=True, skip_group_check=True),
                   reads=(zmB, identB), writes=(psDh[0], psDd[0], psDd[1]))
            P.emit("scalar",
                   lambda e, sc=scol: e.copy(zmT[0:32, sc:sc + 128], psD[0:32, 0:128]),
                   reads=(psDh[0],), writes=(zmTB,))

        for ob in range(32):
            slab_seq.append((w3r, ob, dense_compute(ob, w3r, False)))
        run_slab_stream()

        P.emit("gpsimd", lambda e: e.nop(), extra=out_toks)

        P.finalize()

        blk.tensor(lambda e: P.replay("tensor", e, sems))
        blk.vector(lambda e: P.replay("vector", e, sems))
        blk.scalar(lambda e: P.replay("scalar", e, sems))
        blk.sync(lambda e: P.replay("sync", e, sems))
        blk.gpsimd(lambda e: P.replay("gpsimd", e, sems))

    return nc


_cache = {}


def kernel(L0_attn_q, L0_attn_w_qkv, L0_attn_w_dense, L0_ffn_W_up,
           L0_ffn_gate_w, L0_ffn_expert_bias, L0_ffn_lora_A, L0_ffn_lora_B):
    x = np.asarray(L0_attn_q, dtype=np.float32)
    xt_all = np.ascontiguousarray(
        x.reshape(8, 2048, 32, 128).transpose(0, 3, 2, 1))
    wq = np.ascontiguousarray(np.asarray(L0_attn_w_qkv, np.float32)
                              .reshape(48, 128, 32, 128).transpose(3, 0, 2, 1))
    wd = np.ascontiguousarray(np.asarray(L0_attn_w_dense, np.float32)
                              .reshape(32, 128, 32, 128).transpose(3, 0, 2, 1))
    wu = np.ascontiguousarray(np.asarray(L0_ffn_W_up, np.float32)
                              .reshape(32, 128, 32, 128).transpose(3, 0, 2, 1))
    ga = np.zeros((128, 32, 36), np.float32)
    ga[:, :, 0:4] = np.asarray(L0_ffn_gate_w, np.float32) \
        .reshape(4, 32, 128).transpose(2, 1, 0)
    ga[:, :, 4:36] = np.asarray(L0_ffn_lora_A, np.float32) \
        .reshape(32, 32, 128).transpose(2, 1, 0)
    bf = np.ascontiguousarray(np.asarray(L0_ffn_lora_B, np.float32)
                              .transpose(0, 2, 1).reshape(32, 4096))
    br = np.asarray(L0_ffn_expert_bias, np.float32).reshape(1, 4)
    idh = np.eye(128, dtype=np.float32)
    mkh = np.zeros((128, 2048), np.float32)
    pp = np.arange(128)[:, None]
    cc = np.arange(512)[None, :]
    for r_i, r in enumerate((0, 128, 256, 384)):
        mkh[:, r_i * 512:(r_i + 1) * 512] = (pp + r <= cc).astype(np.float32)

    if "nc" not in _cache:
        _cache["nc"] = build()
    nc = _cache["nc"]
    shared = dict(w1=wq, w2=wd, w3=wu, gah=ga, bfh=bf, brh=br, idh=idh, mkh=mkh)
    in_maps = [dict(xt=xt_all[i], **shared) for i in range(8)]
    res = run_bass_kernel_spmd(nc, in_maps, core_ids=list(range(8)))
    out = np.stack([res.results[i]["out"] for i in range(8)])
    return np.ascontiguousarray(
        out.transpose(0, 3, 2, 1).reshape(8, 2048, 4096)).astype(np.float32)


# version 05
# speedup vs baseline: 1.1328x; 1.0265x over previous
import math
import numpy as np
import concourse.bass as bass
import concourse.mybir as mybir
from concourse.bass_utils import run_bass_kernel_spmd
from contextlib import ExitStack

F32 = mybir.dt.float32
BF16 = mybir.dt.bfloat16
AF = mybir.ActivationFunctionType
ALU = mybir.AluOpType
AX = mybir.AxisListType

S = 2048
NCH = 32
ST = 1024
AST = 512
SC = 1.0 / math.sqrt(128.0)
SHIFT = -12.0

ENGS = ("tensor", "vector", "scalar", "sync", "gpsimd")

A_H = 0
A_SLAB0 = 65536
A_SLAB1 = 69632
A_STG0 = 73728
A_STG1 = 74752
A_ABC_A = 75776
A_ABC_B = 76800
A_GA = 77824
A_IDENT = 78976
ARENA_COLS = 79104

PT0 = 48 * 1024
ARAW0 = 52 * 1024
AFIN0 = 54 * 1024
MASK0 = 56 * 1024


class Buf:
    __slots__ = ("w", "r")

    def __init__(self):
        self.w = None
        self.r = []


class Prog:
    def __init__(self):
        self.ops = {e: [] for e in ENGS}
        self.cnt = {e: 0 for e in ENGS}

    def emit(self, eng, fn, reads=(), writes=(), extra=(), dma=False):
        deps = {}

        def add(t):
            if t is None:
                return
            e, v = t
            if v > deps.get(e, 0):
                deps[e] = v

        for t in extra:
            add(t)
        for b in reads:
            add(b.w)
        for b in writes:
            add(b.w)
            for t in b.r:
                add(t)
        if dma:
            step = 16
        elif eng in ("sync", "gpsimd"):
            step = 0
        else:
            step = 1
        self.ops[eng].append((deps, fn, step))
        self.cnt[eng] += step
        tok = (eng, self.cnt[eng])
        for b in reads:
            b.r.append(tok)
        for b in writes:
            b.w = tok
            b.r = []
        return tok

    def finalize(self):
        waited = {"sync": set(), "gpsimd": set()}
        for ename in ENGS:
            for deps, fn, step in self.ops[ename]:
                for de, dv in deps.items():
                    if de in waited:
                        waited[de].add(dv)
        for q in ("sync", "gpsimd"):
            newops = []
            cum = 0
            for deps, fn, step in self.ops[q]:
                deps = {k: v for k, v in deps.items() if k != q}
                newops.append((deps, fn, step))
                if step == 16:
                    cum += 16
                    if cum in waited[q]:
                        newops.append(({q: cum}, None, 0))
            self.ops[q] = newops

    def replay(self, eng_name, eng, sems):
        last_wait = {}
        last_drain = 0
        own = 0
        for deps, fn, step in self.ops[eng_name]:
            if fn is None:
                eng.wait_ge(sems[eng_name], deps[eng_name])
                continue
            for e, v in deps.items():
                if e == eng_name:
                    if eng_name != "tensor" and v > last_drain:
                        eng.drain()
                        last_drain = own
                else:
                    if v > last_wait.get(e, 0):
                        eng.wait_ge(sems[e], v)
                        last_wait[e] = v
            inst = fn(eng)
            if step:
                own += step
                inst.then_inc(sems[eng_name], step)


def build():
    nc = bass.Bass("TRN2", target_bir_lowering=False)

    xt = nc.dram_tensor("xt", [128, NCH, S], F32, kind="ExternalInput")
    w1 = nc.dram_tensor("w1", [128, 48, NCH, 128], F32, kind="ExternalInput")
    w2 = nc.dram_tensor("w2", [128, 32, NCH, 128], F32, kind="ExternalInput")
    w3 = nc.dram_tensor("w3", [128, 32, NCH, 128], F32, kind="ExternalInput")
    gah = nc.dram_tensor("gah", [128, NCH, 36], F32, kind="ExternalInput")
    bfh = nc.dram_tensor("bfh", [32, 4096], F32, kind="ExternalInput")
    brh = nc.dram_tensor("brh", [1, 4], F32, kind="ExternalInput")
    idh = nc.dram_tensor("idh", [128, 128], F32, kind="ExternalInput")
    mkh = nc.dram_tensor("mkh", [128, 2048], F32, kind="ExternalInput")

    qscr = nc.dram_tensor("qscr", [128, 32, S], BF16, kind="Internal")
    kscr = nc.dram_tensor("kscr", [128, 8, S], BF16, kind="Internal")
    vscr = nc.dram_tensor("vscr", [128, 16, 1024], BF16, kind="Internal")
    ascr = nc.dram_tensor("ascr", [128, 32, S], BF16, kind="Internal")
    x1scr = nc.dram_tensor("x1scr", [128, NCH, S], F32, kind="Internal")
    outt = nc.dram_tensor("out", [128, NCH, S], F32, kind="ExternalOutput")

    w1r = w1.rearrange("p j c k -> p j (c k)")
    w2r = w2.rearrange("p j c k -> p j (c k)")
    w3r = w3.rearrange("p j c k -> p j (c k)")
    gar = gah.rearrange("p c e -> p (c e)")

    P = Prog()

    with ExitStack() as ex:
        E = ex.enter_context
        arena = E(nc.sbuf_tensor("arena", [128, ARENA_COLS], BF16))
        fa = E(nc.sbuf_tensor("fa", [128, 2048], F32))
        rows = E(nc.sbuf_tensor("rows", [1, 4096], F32))
        rowsb = E(nc.sbuf_tensor("rowsb", [1, 3072], BF16))
        bf_sb = E(nc.sbuf_tensor("bf_sb", [32, 4096], BF16))
        zmT = E(nc.sbuf_tensor("zmT", [32, 2048], BF16))
        zm = E(nc.sbuf_tensor("zm", [128, 32], BF16))
        mx = E(nc.sbuf_tensor("mx", [128, 1], F32))
        oh = E(nc.sbuf_tensor("oh", [128, 4], F32))
        onc = E(nc.sbuf_tensor("onc", [128, 1], BF16))
        onr = E(nc.sbuf_tensor("onr", [1, 128], BF16))
        bias_sb = E(nc.sbuf_tensor("bias_sb", [1, 4], BF16))
        rrec = E(nc.sbuf_tensor("rrec", [1, 512], F32))
        rden = E(nc.sbuf_tensor("rden", [1, 512], F32))

        psA = E(nc.psum_tensor("psA", [128, 1024], F32))
        psB = E(nc.psum_tensor("psB", [128, 1024], F32))
        psC = E(nc.psum_tensor("psC", [128, 1024], F32))
        psD = E(nc.psum_tensor("psD", [128, 1024], F32))

        sems = {e: E(nc.semaphore(f"sem_{e}")) for e in ENGS}
        blk = E(nc.Block())

        AHg = [Buf() for _ in range(64)]

        def AH(c0, c1):
            return AHg[c0 // 1024:(c1 + 1023) // 1024]

        slabB = [Buf(), Buf()]
        stgB = [Buf(), Buf()]
        abcB = [Buf(), Buf()]
        gaB, identB, bfB, biasB = Buf(), Buf(), Buf(), Buf()
        oncB, onrB = Buf(), Buf()
        rowB = [Buf() for _ in range(4)]
        rowbB = [Buf() for _ in range(4)]
        psAh = [Buf(), Buf()]
        psBh = [Buf(), Buf()]
        psCh = [Buf(), Buf()]
        psDh = [Buf(), Buf()]
        psDd = [Buf(), Buf()]
        zmB, zmTB, mxB, ohB, rrecB = Buf(), Buf(), Buf(), Buf(), Buf()
        rdenB = Buf()
        faxB = [Buf(), Buf()]
        faoB = [Buf(), Buf()]
        qscrB = [Buf() for _ in range(32)]
        kscrB = [Buf() for _ in range(8)]
        vscrB = [Buf() for _ in range(16)]
        ascrB = [Buf() for _ in range(32)]
        x1B = [Buf() for _ in range(32)]
        out_toks = []

        SLAB = [A_SLAB0, A_SLAB1]
        STG = [A_STG0, A_STG1]
        FAX = [0, 512]
        FAO = [1024, 1536]

        def ah_ap(c0, width):
            return arena[:, c0:c0 + width]

        P.emit("gpsimd", lambda e: e.dma_start(out=bf_sb[:, :], in_=bfh[:, :]),
               writes=(bfB,), dma=True)
        P.emit("gpsimd", lambda e: e.dma_start(out=arena[:, A_GA:A_GA + 1152], in_=gar[:, :]),
               writes=(gaB,), dma=True)
        P.emit("gpsimd", lambda e: e.dma_start(out=bias_sb[:, :], in_=brh[:, :]),
               writes=(biasB,), dma=True)
        P.emit("gpsimd", lambda e: e.dma_start(out=arena[:, A_IDENT:A_IDENT + 128], in_=idh[:, :]),
               writes=(identB,), dma=True)
        P.emit("vector", lambda e: e.memset(onc[:, :], 1.0), writes=(oncB,))
        P.emit("vector", lambda e: e.memset(onr[:, :], 1.0), writes=(onrB,))

        def ln_phase(src_dram, src_bufs):
            ahview = arena[:, 0:65536].rearrange("p (c s) -> p c s", c=32)
            for st in range(2):
                for cg in range(4):
                    c0 = cg * 8
                    dst = ahview[:, c0:c0 + 8, st * ST:(st + 1) * ST]
                    srcv = src_dram[:, c0:c0 + 8, st * ST:(st + 1) * ST]
                    extra = []
                    if src_bufs is not None:
                        extra = [src_bufs[c].w for c in range(c0, c0 + 8)
                                 if src_bufs[c].w]
                    wr = []
                    for c in range(c0, c0 + 8):
                        wr.extend(AH(c * S + st * ST, c * S + st * ST + ST))
                    P.emit("gpsimd",
                           lambda e, d=dst, s=srcv: e.dma_start(out=d, in_=s),
                           writes=wr, extra=extra, dma=True)
            for st in range(2):
                for c in range(NCH):
                    col = c * S + st * ST
                    x_ap = ah_ap(col, ST)
                    sq_ap = arena[:, STG[c % 2]:STG[c % 2] + ST]
                    P.emit("vector",
                           lambda e, o=sq_ap, i=x_ap: e.tensor_mul(o, i, i),
                           reads=AH(col, col + ST), writes=(stgB[c % 2],))
                    for h in range(2):
                        P.emit("tensor",
                               lambda e, r=x_ap, hh=h, s=(c == 0), t=(c == NCH - 1):
                               e.matmul(psC[0:1, hh * 512:(hh + 1) * 512], onc[:, :],
                                        r[:, hh * 512:(hh + 1) * 512], start=s, stop=t,
                                        skip_group_check=True),
                               reads=AH(col, col + ST) + [oncB],
                               writes=(psCh[h],))
                        P.emit("tensor",
                               lambda e, r=sq_ap, hh=h, s=(c == 0), t=(c == NCH - 1):
                               e.matmul(psC[32:33, hh * 512:(hh + 1) * 512], onc[:, :],
                                        r[:, hh * 512:(hh + 1) * 512], start=s, stop=t,
                                        skip_group_check=True),
                               reads=(stgB[c % 2], oncB),
                               writes=(psCh[h],))
                P.emit("vector",
                       lambda e: e.tensor_scalar(rows[0:1, 0:1024], psC[0:1, :],
                                                 1.0 / 4096.0, None, ALU.mult),
                       reads=(psCh[0], psCh[1]), writes=(rowB[0],))
                P.emit("vector",
                       lambda e: e.tensor_scalar(rows[0:1, 1024:2048], psC[32:33, :],
                                                 1.0 / 4096.0, None, ALU.mult),
                       reads=(psCh[0], psCh[1]), writes=(rowB[1],))
                P.emit("vector",
                       lambda e: e.tensor_mul(rows[0:1, 2048:3072], rows[0:1, 0:1024], rows[0:1, 0:1024]),
                       reads=(rowB[0],), writes=(rowB[2],))
                P.emit("vector",
                       lambda e: e.tensor_sub(rows[0:1, 1024:2048], rows[0:1, 1024:2048], rows[0:1, 2048:3072]),
                       reads=(rowB[1], rowB[2]), writes=(rowB[1],))
                P.emit("scalar",
                       lambda e: e.activation(rows[0:1, 3072:4096], rows[0:1, 1024:2048], AF.Sqrt,
                                              bias=np.float32(1e-5), scale=np.float32(1.0)),
                       reads=(rowB[1],), writes=(rowB[3],))
                P.emit("vector",
                       lambda e: e.reciprocal(rows[0:1, 1024:2048], rows[0:1, 3072:4096]),
                       reads=(rowB[3],), writes=(rowB[1],))
                P.emit("scalar",
                       lambda e: e.copy(rowsb[0:1, 0:1024], rows[0:1, 1024:2048]),
                       reads=(rowB[1],), writes=(rowbB[0],))
                P.emit("vector",
                       lambda e: e.tensor_mul(rows[0:1, 2048:3072], rows[0:1, 0:1024], rows[0:1, 1024:2048]),
                       reads=(rowB[0], rowB[1]), writes=(rowB[2],))
                P.emit("vector",
                       lambda e: e.tensor_scalar(rowsb[0:1, 1024:2048], rows[0:1, 2048:3072],
                                                 -1.0, None, ALU.mult),
                       reads=(rowB[2],), writes=(rowbB[1],))
                for h in range(2):
                    P.emit("tensor",
                           lambda e, hh=h: e.matmul(psD[:, hh * 512:(hh + 1) * 512], onr[:, :],
                                              rowsb[0:1, hh * 512:(hh + 1) * 512],
                                              start=True, stop=True, skip_group_check=True),
                           reads=(rowbB[0], onrB),
                           writes=(psDh[h], psDd[0], psDd[1]) if h == 0 else (psDh[h],))
                    P.emit("tensor",
                           lambda e, hh=h: e.matmul(psC[:, hh * 512:(hh + 1) * 512], onr[:, :],
                                              rowsb[0:1, 1024 + hh * 512:1024 + (hh + 1) * 512],
                                              start=True, stop=True, skip_group_check=True),
                           reads=(rowbB[1], onrB), writes=(psCh[h],))
                P.emit("scalar",
                       lambda e: e.copy(arena[:, A_ABC_A:A_ABC_A + ST], psD[:, :]),
                       reads=(psDh[0], psDh[1]), writes=(abcB[0],))
                P.emit("scalar",
                       lambda e: e.copy(arena[:, A_ABC_B:A_ABC_B + ST], psC[:, :]),
                       reads=(psCh[0], psCh[1]), writes=(abcB[1],))
                for c in range(NCH):
                    col = c * S + st * ST
                    x_ap = ah_ap(col, ST)
                    a_ap = arena[:, A_ABC_A:A_ABC_A + ST]
                    b_ap = arena[:, A_ABC_B:A_ABC_B + ST]
                    P.emit("vector",
                           lambda e, x=x_ap, a=a_ap: e.tensor_mul(x, x, a),
                           reads=AH(col, col + ST) + [abcB[0]],
                           writes=AH(col, col + ST))
                    P.emit("vector",
                           lambda e, x=x_ap, b=b_ap: e.tensor_add(x, x, b),
                           reads=AH(col, col + ST) + [abcB[1]],
                           writes=AH(col, col + ST))

        slab_seq = []

        def run_slab_stream():
            n = len(slab_seq)
            for i in range(n + 1):
                if i < n:
                    src, jb, _ = slab_seq[i]
                    slot = i % 2
                    P.emit("gpsimd",
                           lambda e, s=slot, sr=src, j=jb:
                           e.dma_start(out=arena[:, SLAB[s]:SLAB[s] + 4096],
                                       in_=sr[:, j, :]),
                           writes=(slabB[slot],), dma=True)
                if i >= 1:
                    _, _, fn = slab_seq[i - 1]
                    fn((i - 1) % 2)

        ln_phase(xt, None)

        def qk_compute(jb):
            def fn(slot):
                for st in range(2):
                    ps = psA if st == 0 else psB
                    psh = psAh if st == 0 else psBh
                    for c in range(NCH):
                        col = c * S + st * ST
                        lhs = arena[:, SLAB[slot] + c * 128:SLAB[slot] + (c + 1) * 128]
                        for h in range(2):
                            rhs = ah_ap(col + h * 512, 512)
                            P.emit("tensor",
                                   lambda e, l=lhs, r=rhs, p=ps, hh=h,
                                          s=(c == 0), t=(c == NCH - 1):
                                   e.matmul(p[:, hh * 512:(hh + 1) * 512], l, r,
                                            start=s, stop=t, skip_group_check=True),
                                   reads=AH(col, col + ST) + [slabB[slot]],
                                   writes=(psh[h],))
                    stg_ap = arena[:, STG[st]:STG[st] + ST]
                    P.emit("scalar",
                           lambda e, o=stg_ap, p=ps: e.copy(o, p[:, :]),
                           reads=(psh[0], psh[1]), writes=(stgB[st],))
                    if jb < 32:
                        dst = qscr[:, jb, st * ST:(st + 1) * ST]
                        wb = qscrB[jb]
                    else:
                        dst = kscr[:, jb - 32, st * ST:(st + 1) * ST]
                        wb = kscrB[jb - 32]
                    P.emit("sync",
                           lambda e, d=dst, s=stg_ap: e.dma_start(out=d, in_=s),
                           reads=(stgB[st],), writes=(wb,), dma=True)
            return fn

        for jb in range(40):
            slab_seq.append((w1r, jb, qk_compute(jb)))

        def v_compute(dvt):
            def fn(slot):
                rot = [(psC, psCh, 0), (psC, psCh, 1), (psD, psDh, 0), (psD, psDh, 1)]
                for tsub in range(16):
                    ps, psh, hh = rot[tsub % 4]
                    ps_ap = ps[:, hh * 512:hh * 512 + 128]
                    st = tsub // 8
                    vw = (psh[hh], psDd[0], psDd[1]) if (ps is psD and hh == 0) else (psh[hh],)
                    for c in range(NCH):
                        col = c * S + tsub * 128
                        lhs = ah_ap(col, 128)
                        rhs = arena[:, SLAB[slot] + c * 128:SLAB[slot] + (c + 1) * 128]
                        P.emit("tensor",
                               lambda e, l=lhs, r=rhs, p=ps_ap, s=(c == 0), t=(c == NCH - 1):
                               e.matmul(p, l, r, start=s, stop=t,
                                        skip_group_check=True),
                               reads=AH(col, col + 128) + [slabB[slot]],
                               writes=vw)
                    stg_ap = arena[:, A_STG0 + tsub * 128:A_STG0 + (tsub + 1) * 128]
                    P.emit("scalar",
                           lambda e, o=stg_ap, p=ps_ap: e.copy(o, p),
                           reads=(psh[hh],), writes=(stgB[0], stgB[1]))
                dst = vscr[:, :, dvt * 128:(dvt + 1) * 128]
                vst = arena[:, A_STG0:A_STG0 + 2048].rearrange(
                    "p (t d) -> p t d", t=16)
                P.emit("sync",
                       lambda e, d=dst, s=vst: e.dma_start(out=d, in_=s),
                       reads=(stgB[0], stgB[1]),
                       extra=[vscrB[t].w for t in range(16) if vscrB[t].w],
                       writes=[vscrB[t] for t in range(16)], dma=True)
            return fn

        for dvt in range(8):
            slab_seq.append((w1r, 40 + dvt, v_compute(dvt)))

        run_slab_stream()
        slab_seq = []

        P.emit("gpsimd", lambda e: e.dma_start(out=arena[:, MASK0:MASK0 + 2048], in_=mkh[:, :]),
               writes=AH(MASK0, MASK0 + 2048), dma=True)
        QOFF, KOFF, VOFF = 0, 16384, 32768
        pend = [None]
        qview = arena[:, QOFF:QOFF + 16384].rearrange("p (hq s) -> p hq s", hq=32)
        kview = arena[:, KOFF:KOFF + 16384].rearrange("p (kb s) -> p kb s", kb=8)
        vflat = vscr.rearrange("p t d -> p (t d)")
        P.emit("sync",
               lambda e: e.dma_start(out=kview[:, :, :], in_=kscr[:, :, :]),
               extra=[kscrB[j].w for j in range(8) if kscrB[j].w],
               writes=AH(KOFF, KOFF + 16384), dma=True)
        P.emit("sync",
               lambda e: e.dma_start(out=arena[:, VOFF:VOFF + 16384],
                                     in_=vflat[:, :]),
               extra=[vscrB[j].w for j in range(16) if vscrB[j].w],
               writes=AH(VOFF, VOFF + 16384), dma=True)
        for st2 in range(4):
            for qg4 in range(4):
                q0 = qg4 * 8
                P.emit("sync",
                       lambda e, q=q0, s=st2:
                       e.dma_start(out=qview[:, q:q + 8, :],
                                   in_=qscr[:, q:q + 8, s * 512:(s + 1) * 512]),
                       extra=[qscrB[q0 + j].w for j in range(8) if qscrB[q0 + j].w],
                       writes=AH(QOFF + q0 * 512, QOFF + (q0 + 8) * 512), dma=True)

            for kb in range(8):
                for g in range(4):
                    hq = kb * 4 + g
                    avt, avhl = [(psA, psAh), (psA, psAh), (psB, psBh), (psB, psBh)][g]
                    avhh = g % 2
                    av_ap = avt[:, avhh * 512:(avhh + 1) * 512]
                    ntc = st2 * 4 + 4
                    q_ap = arena[:, QOFF + hq * 512:QOFF + (hq + 1) * 512]
                    qg = AH(QOFF + hq * 512, QOFF + (hq + 1) * 512)

                    def qk_step(tc):
                        kcol = KOFF + kb * S + tc * 128
                        k_ap = arena[:, kcol:kcol + 128]
                        qk_ap = psC[:, (tc % 2) * 512:(tc % 2 + 1) * 512]
                        pt_col = PT0 + (tc % 4) * 1024
                        pt_ap = arena[:, pt_col:pt_col + 512]
                        P.emit("tensor",
                               lambda e, l=k_ap, r=q_ap, p=qk_ap:
                               e.matmul(p, l, r, start=True, stop=True,
                                        skip_group_check=True),
                               reads=AH(kcol, kcol + 128) + qg,
                               writes=(psCh[tc % 2],))
                        P.emit("scalar",
                               lambda e, o=pt_ap, i=qk_ap:
                               e.activation(o, i, AF.Exp,
                                            bias=np.float32(SHIFT), scale=np.float32(SC)),
                               reads=(psCh[tc % 2],),
                               writes=AH(pt_col, pt_col + 512))
                        if tc >= st2 * 4:
                            r_idx = tc - st2 * 4
                            m_ap = arena[:, MASK0 + r_idx * 512:MASK0 + (r_idx + 1) * 512]
                            P.emit("vector",
                                   lambda e, p=pt_ap, m=m_ap: e.tensor_mul(p, p, m),
                                   reads=AH(pt_col, pt_col + 512) +
                                         AH(MASK0 + r_idx * 512, MASK0 + (r_idx + 1) * 512),
                                   writes=AH(pt_col, pt_col + 512))

                    def dav_step(tc):
                        pt_col = PT0 + (tc % 4) * 1024
                        pt_ap = arena[:, pt_col:pt_col + 512]
                        vcol = VOFF + tc * 1024 + kb * 128
                        v_ap = arena[:, vcol:vcol + 128]
                        P.emit("tensor",
                               lambda e, p=pt_ap, s=(tc == 0), t=(tc == ntc - 1), gg=g:
                               e.matmul(psD[32 * (gg % 2):32 * (gg % 2) + 1, 0:512],
                                        onc[:, :], p,
                                        start=s, stop=t, skip_group_check=True),
                               reads=AH(pt_col, pt_col + 512) + [oncB],
                               writes=(psDd[g % 2],))
                        P.emit("tensor",
                               lambda e, l=v_ap, p=pt_ap, o=av_ap,
                                      s=(tc == 0), t=(tc == ntc - 1):
                               e.matmul(o, l, p, start=s, stop=t,
                                        skip_group_check=True),
                               reads=AH(pt_col, pt_col + 512) + AH(vcol, vcol + 128),
                               writes=(avhl[avhh],))

                    for tc in range(ntc + 1):
                        if tc < ntc:
                            qk_step(tc)
                        if tc >= 1:
                            dav_step(tc - 1)

                    def make_evac(g=g, hq=hq, st2=st2, av_ap=av_ap,
                                  avhl=avhl, avhh=avhh):
                        def evac():
                            P.emit("vector",
                                   lambda e, gg=g: e.reciprocal(
                                       rrec[:, :],
                                       psD[32 * (gg % 2):32 * (gg % 2) + 1, 0:512]),
                                   reads=(psDd[g % 2],), writes=(rrecB,))
                            P.emit("scalar",
                                   lambda e: e.copy(rowsb[0:1, 2048:2560], rrec[:, :]),
                                   reads=(rrecB,), writes=(rowbB[2],))
                            P.emit("tensor",
                                   lambda e: e.matmul(psD[:, 512:1024], onr[:, :],
                                                      rowsb[0:1, 2048:2560],
                                                      start=True, stop=True,
                                                      skip_group_check=True),
                                   reads=(rowbB[2], onrB), writes=(psDh[1],))
                            ar_col = ARAW0 + (g % 2) * 1024
                            ar_ap = arena[:, ar_col:ar_col + 512]
                            P.emit("scalar",
                                   lambda e, o=ar_ap, i=av_ap: e.copy(o, i),
                                   reads=(avhl[avhh],), writes=AH(ar_col, ar_col + 512))
                            af_col = AFIN0 + (g % 2) * 1024
                            af_ap = arena[:, af_col:af_col + 512]
                            P.emit("vector",
                                   lambda e, o=af_ap, a=ar_ap:
                                   e.tensor_mul(o, psD[:, 512:1024], a),
                                   reads=[psDh[1]] + AH(ar_col, ar_col + 512),
                                   writes=AH(af_col, af_col + 512))
                            P.emit("sync",
                                   lambda e, h=hq, s=st2, a=af_ap:
                                   e.dma_start(out=ascr[:, h, s * 512:(s + 1) * 512], in_=a),
                                   reads=AH(af_col, af_col + 512),
                                   writes=(ascrB[hq],), dma=True)
                        return evac
                    if pend[0] is not None:
                        pend[0]()
                    pend[0] = make_evac()

        if pend[0] is not None:
            pend[0]()
            pend[0] = None

        for hq in range(32):
            dcol = hq * S
            P.emit("sync",
                   lambda e, h=hq, d=dcol:
                   e.dma_start(out=arena[:, d:d + S], in_=ascr[:, h, :]),
                   reads=(ascrB[hq],), writes=AH(dcol, dcol + S), dma=True)

        def dense_compute(ob, wsrc, out_is_x1):
            def fn(slot):
                for st in range(2):
                    ps = psA if st == 0 else psB
                    psh = psAh if st == 0 else psBh
                    for c in range(NCH):
                        col = c * S + st * ST
                        lhs = arena[:, SLAB[slot] + c * 128:SLAB[slot] + (c + 1) * 128]
                        for h in range(2):
                            rhs = ah_ap(col + h * 512, 512)
                            P.emit("tensor",
                                   lambda e, l=lhs, r=rhs, p=ps, hh=h, s=(c == 0),
                                          t=(c == NCH - 1 and out_is_x1):
                                   e.matmul(p[:, hh * 512:(hh + 1) * 512], l, r,
                                            start=s, stop=t, skip_group_check=True),
                                   reads=AH(col, col + ST) + [slabB[slot]],
                                   writes=(psh[h],))
                    if not out_is_x1:
                        for h in range(2):
                            P.emit("tensor",
                                   lambda e, p=ps, o=ob, s2=st, hh=h:
                                   e.matmul(p[:, hh * 512:(hh + 1) * 512],
                                            bf_sb[0:32, o * 128:(o + 1) * 128],
                                            zmT[0:32, s2 * ST + hh * 512:s2 * ST + (hh + 1) * 512],
                                            start=False, stop=True, skip_group_check=True),
                                   reads=(bfB, zmTB), writes=(psh[h],))
                    for half in range(2):
                        src = (xt if out_is_x1 else x1scr)[
                            :, ob, st * ST + half * 512:st * ST + (half + 1) * 512]
                        fx_ap = fa[:, FAX[half]:FAX[half] + 512]
                        extra = [] if out_is_x1 else ([x1B[ob].w] if x1B[ob].w else [])
                        P.emit("sync",
                               lambda e, d=fx_ap, s=src: e.dma_start(out=d, in_=s),
                               writes=(faxB[half],), extra=extra, dma=True)
                        fo_ap = fa[:, FAO[half]:FAO[half] + 512]
                        ps_ap = ps[:, half * 512:(half + 1) * 512]
                        P.emit("vector",
                               lambda e, o=fo_ap, p=ps_ap, x=fx_ap:
                               e.tensor_tensor(o, p, x, ALU.add),
                               reads=(psh[half], faxB[half]), writes=(faoB[half],))
                        dstd = (x1scr if out_is_x1 else outt)[
                            :, ob, st * ST + half * 512:st * ST + (half + 1) * 512]
                        wb = (x1B[ob],) if out_is_x1 else ()
                        tok = P.emit("sync",
                                     lambda e, d=dstd, s=fo_ap: e.dma_start(out=d, in_=s),
                                     reads=(faoB[half],), writes=wb, dma=True)
                        if not out_is_x1:
                            out_toks.append(tok)
            return fn

        for ob in range(32):
            slab_seq.append((w2r, ob, dense_compute(ob, w2r, True)))
        run_slab_stream()
        slab_seq = []

        ln_phase(x1scr, x1B)

        for ssg in range(16):
            scol = ssg * 128
            for c in range(NCH):
                col = c * S + scol
                lhs = ah_ap(col, 128)
                rhs = arena[:, A_GA + c * 36:A_GA + (c + 1) * 36]
                P.emit("tensor",
                       lambda e, l=lhs, r=rhs, s=(c == 0):
                       e.matmul(psC[:, 0:36], l, r, start=s, stop=False,
                                skip_group_check=True),
                       reads=AH(col, col + 128) + [gaB],
                       writes=(psCh[0],))
            P.emit("tensor",
                   lambda e: e.matmul(psC[:, 0:4], onr[:, :], bias_sb[:, :],
                                      start=False, stop=True, skip_group_check=True),
                   reads=(biasB, onrB), writes=(psCh[0],))
            P.emit("vector",
                   lambda e: e.reduce_max(mx[:, :], psC[:, 0:4], axis=AX.X),
                   reads=(psCh[0],), writes=(mxB,))
            P.emit("vector",
                   lambda e: e.tensor_scalar(oh[:, :], psC[:, 0:4], mx[:, 0:1],
                                             None, ALU.is_ge),
                   reads=(psCh[0], mxB), writes=(ohB,))
            for ee in range(4):
                P.emit("vector",
                       lambda e, v=ee: e.tensor_scalar(zm[:, v * 8:(v + 1) * 8],
                                                       psC[:, 4 + v * 8:4 + (v + 1) * 8],
                                                       oh[:, v:v + 1], None, ALU.mult),
                       reads=(psCh[0], ohB), writes=(zmB,))
            P.emit("tensor",
                   lambda e: e.matmul(psD[0:32, 0:128], zm[:, :],
                                      arena[:, A_IDENT:A_IDENT + 128],
                                      start=True, stop=True, skip_group_check=True),
                   reads=(zmB, identB), writes=(psDh[0], psDd[0], psDd[1]))
            P.emit("scalar",
                   lambda e, sc=scol: e.copy(zmT[0:32, sc:sc + 128], psD[0:32, 0:128]),
                   reads=(psDh[0],), writes=(zmTB,))

        for ob in range(32):
            slab_seq.append((w3r, ob, dense_compute(ob, w3r, False)))
        run_slab_stream()

        P.emit("gpsimd", lambda e: e.nop(), extra=out_toks)

        P.finalize()

        blk.tensor(lambda e: P.replay("tensor", e, sems))
        blk.vector(lambda e: P.replay("vector", e, sems))
        blk.scalar(lambda e: P.replay("scalar", e, sems))
        blk.sync(lambda e: P.replay("sync", e, sems))
        blk.gpsimd(lambda e: P.replay("gpsimd", e, sems))

    return nc


_cache = {}


def kernel(L0_attn_q, L0_attn_w_qkv, L0_attn_w_dense, L0_ffn_W_up,
           L0_ffn_gate_w, L0_ffn_expert_bias, L0_ffn_lora_A, L0_ffn_lora_B):
    x = np.asarray(L0_attn_q, dtype=np.float32)
    xt_all = np.ascontiguousarray(
        x.reshape(8, 2048, 32, 128).transpose(0, 3, 2, 1))
    wq = np.ascontiguousarray(np.asarray(L0_attn_w_qkv, np.float32)
                              .reshape(48, 128, 32, 128).transpose(3, 0, 2, 1))
    wd = np.ascontiguousarray(np.asarray(L0_attn_w_dense, np.float32)
                              .reshape(32, 128, 32, 128).transpose(3, 0, 2, 1))
    wu = np.ascontiguousarray(np.asarray(L0_ffn_W_up, np.float32)
                              .reshape(32, 128, 32, 128).transpose(3, 0, 2, 1))
    ga = np.zeros((128, 32, 36), np.float32)
    ga[:, :, 0:4] = np.asarray(L0_ffn_gate_w, np.float32) \
        .reshape(4, 32, 128).transpose(2, 1, 0)
    ga[:, :, 4:36] = np.asarray(L0_ffn_lora_A, np.float32) \
        .reshape(32, 32, 128).transpose(2, 1, 0)
    bf = np.ascontiguousarray(np.asarray(L0_ffn_lora_B, np.float32)
                              .transpose(0, 2, 1).reshape(32, 4096))
    br = np.asarray(L0_ffn_expert_bias, np.float32).reshape(1, 4)
    idh = np.eye(128, dtype=np.float32)
    mkh = np.zeros((128, 2048), np.float32)
    pp = np.arange(128)[:, None]
    cc = np.arange(512)[None, :]
    for r_i, r in enumerate((0, 128, 256, 384)):
        mkh[:, r_i * 512:(r_i + 1) * 512] = (pp + r <= cc).astype(np.float32)

    if "nc" not in _cache:
        _cache["nc"] = build()
    nc = _cache["nc"]
    shared = dict(w1=wq, w2=wd, w3=wu, gah=ga, bfh=bf, brh=br, idh=idh, mkh=mkh)
    in_maps = [dict(xt=xt_all[i], **shared) for i in range(8)]
    res = run_bass_kernel_spmd(nc, in_maps, core_ids=list(range(8)))
    out = np.stack([res.results[i]["out"] for i in range(8)])
    return np.ascontiguousarray(
        out.transpose(0, 3, 2, 1).reshape(8, 2048, 4096)).astype(np.float32)


# version 06
# speedup vs baseline: 1.1750x; 1.0373x over previous
import math
import numpy as np
import concourse.bass as bass
import concourse.mybir as mybir
from concourse.bass_utils import run_bass_kernel_spmd
from contextlib import ExitStack

F32 = mybir.dt.float32
BF16 = mybir.dt.bfloat16
AF = mybir.ActivationFunctionType
ALU = mybir.AluOpType
AX = mybir.AxisListType

S = 2048
NCH = 32
ST = 1024
AST = 512
SC = 1.0 / math.sqrt(128.0)
SHIFT = -12.0

ENGS = ("tensor", "vector", "scalar", "sync", "gpsimd")

A_H = 0
A_SLAB0 = 65536
A_SLAB1 = 69632
A_STG0 = 73728
A_STG1 = 74752
A_ABC_A = 75776
A_ABC_B = 76800
A_GA = 77824
A_IDENT = 78976
ARENA_COLS = 79104

PT0 = 48 * 1024
ARAW0 = 52 * 1024
AFIN0 = 54 * 1024
MASK0 = 56 * 1024


class Buf:
    __slots__ = ("w", "r")

    def __init__(self):
        self.w = None
        self.r = []


class Prog:
    def __init__(self):
        self.ops = {e: [] for e in ENGS}
        self.cnt = {e: 0 for e in ENGS}

    def emit(self, eng, fn, reads=(), writes=(), extra=(), dma=False):
        deps = {}

        def add(t):
            if t is None:
                return
            e, v = t
            if v > deps.get(e, 0):
                deps[e] = v

        for t in extra:
            add(t)
        for b in reads:
            add(b.w)
        for b in writes:
            add(b.w)
            for t in b.r:
                add(t)
        if dma:
            step = 16
        elif eng in ("sync", "gpsimd"):
            step = 0
        else:
            step = 1
        self.ops[eng].append((deps, fn, step))
        self.cnt[eng] += step
        tok = (eng, self.cnt[eng])
        for b in reads:
            b.r.append(tok)
        for b in writes:
            b.w = tok
            b.r = []
        return tok

    def finalize(self):
        waited = {"sync": set(), "gpsimd": set()}
        for ename in ENGS:
            for deps, fn, step in self.ops[ename]:
                for de, dv in deps.items():
                    if de in waited:
                        waited[de].add(dv)
        for q in ("sync", "gpsimd"):
            newops = []
            cum = 0
            for deps, fn, step in self.ops[q]:
                deps = {k: v for k, v in deps.items() if k != q}
                newops.append((deps, fn, step))
                if step == 16:
                    cum += 16
                    if cum in waited[q]:
                        newops.append(({q: cum}, None, 0))
            self.ops[q] = newops

    def replay(self, eng_name, eng, sems):
        last_wait = {}
        last_drain = 0
        own = 0
        for deps, fn, step in self.ops[eng_name]:
            if fn is None:
                eng.wait_ge(sems[eng_name], deps[eng_name])
                continue
            for e, v in deps.items():
                if e == eng_name:
                    if eng_name != "tensor" and v > last_drain:
                        eng.drain()
                        last_drain = own
                else:
                    if v > last_wait.get(e, 0):
                        eng.wait_ge(sems[e], v)
                        last_wait[e] = v
            inst = fn(eng)
            if step:
                own += step
                inst.then_inc(sems[eng_name], step)


def build():
    nc = bass.Bass("TRN2", target_bir_lowering=False)

    xt = nc.dram_tensor("xt", [128, NCH, S], F32, kind="ExternalInput")
    w1 = nc.dram_tensor("w1", [128, 48, NCH, 128], F32, kind="ExternalInput")
    w2 = nc.dram_tensor("w2", [128, 32, NCH, 128], F32, kind="ExternalInput")
    w3 = nc.dram_tensor("w3", [128, 32, NCH, 128], F32, kind="ExternalInput")
    gah = nc.dram_tensor("gah", [128, NCH, 36], F32, kind="ExternalInput")
    bfh = nc.dram_tensor("bfh", [32, 4096], F32, kind="ExternalInput")
    brh = nc.dram_tensor("brh", [1, 4], F32, kind="ExternalInput")
    idh = nc.dram_tensor("idh", [128, 128], F32, kind="ExternalInput")
    mkh = nc.dram_tensor("mkh", [128, 2048], F32, kind="ExternalInput")

    qscr = nc.dram_tensor("qscr", [128, 32, S], BF16, kind="Internal")
    kscr = nc.dram_tensor("kscr", [128, 8, S], BF16, kind="Internal")
    vscr = nc.dram_tensor("vscr", [128, 16, 1024], BF16, kind="Internal")
    ascr = nc.dram_tensor("ascr", [128, 32, S], BF16, kind="Internal")
    x1scr = nc.dram_tensor("x1scr", [128, NCH, S], F32, kind="Internal")
    outt = nc.dram_tensor("out", [128, NCH, S], F32, kind="ExternalOutput")

    w1r = w1.rearrange("p j c k -> p j (c k)")
    w2r = w2.rearrange("p j c k -> p j (c k)")
    w3r = w3.rearrange("p j c k -> p j (c k)")
    gar = gah.rearrange("p c e -> p (c e)")

    P = Prog()

    with ExitStack() as ex:
        E = ex.enter_context
        arena = E(nc.sbuf_tensor("arena", [128, ARENA_COLS], BF16))
        fa = E(nc.sbuf_tensor("fa", [128, 2048], F32))
        rows = E(nc.sbuf_tensor("rows", [1, 4096], F32))
        rowsb = E(nc.sbuf_tensor("rowsb", [1, 3072], BF16))
        bf_sb = E(nc.sbuf_tensor("bf_sb", [32, 4096], BF16))
        zmT = E(nc.sbuf_tensor("zmT", [32, 2048], BF16))
        zm = E(nc.sbuf_tensor("zm", [128, 32], BF16))
        mx = E(nc.sbuf_tensor("mx", [128, 1], F32))
        oh = E(nc.sbuf_tensor("oh", [128, 4], F32))
        onc = E(nc.sbuf_tensor("onc", [128, 1], BF16))
        onr = E(nc.sbuf_tensor("onr", [1, 128], BF16))
        bias_sb = E(nc.sbuf_tensor("bias_sb", [1, 4], BF16))
        rrec = E(nc.sbuf_tensor("rrec", [1, 512], F32))
        rden = E(nc.sbuf_tensor("rden", [1, 512], F32))

        psA = E(nc.psum_tensor("psA", [128, 1024], F32))
        psB = E(nc.psum_tensor("psB", [128, 1024], F32))
        psC = E(nc.psum_tensor("psC", [128, 1024], F32))
        psD = E(nc.psum_tensor("psD", [128, 1024], F32))

        sems = {e: E(nc.semaphore(f"sem_{e}")) for e in ENGS}
        blk = E(nc.Block())

        AHg = [Buf() for _ in range(64)]

        def AH(c0, c1):
            return AHg[c0 // 1024:(c1 + 1023) // 1024]

        slabB = [Buf(), Buf()]
        stgB = [Buf(), Buf()]
        abcB = [Buf(), Buf()]
        gaB, identB, bfB, biasB = Buf(), Buf(), Buf(), Buf()
        oncB, onrB = Buf(), Buf()
        rowB = [Buf() for _ in range(4)]
        rowbB = [Buf() for _ in range(4)]
        psAh = [Buf(), Buf()]
        psBh = [Buf(), Buf()]
        psCh = [Buf(), Buf()]
        psDh = [Buf(), Buf()]
        psDd = [Buf(), Buf()]
        zmB, zmTB, mxB, ohB, rrecB = Buf(), Buf(), Buf(), Buf(), Buf()
        rdenB = Buf()
        faxB = [Buf(), Buf()]
        faoB = [Buf(), Buf()]
        qscrB = [Buf() for _ in range(32)]
        kscrB = [Buf() for _ in range(8)]
        vscrB = [Buf() for _ in range(16)]
        ascrB = [Buf() for _ in range(32)]
        x1B = [Buf() for _ in range(32)]
        out_toks = []

        SLAB = [A_SLAB0, A_SLAB1]
        STG = [A_STG0, A_STG1]
        FAX = [0, 512]
        FAO = [1024, 1536]

        def ah_ap(c0, width):
            return arena[:, c0:c0 + width]

        P.emit("gpsimd", lambda e: e.dma_start(out=bf_sb[:, :], in_=bfh[:, :]),
               writes=(bfB,), dma=True)
        P.emit("gpsimd", lambda e: e.dma_start(out=arena[:, A_GA:A_GA + 1152], in_=gar[:, :]),
               writes=(gaB,), dma=True)
        P.emit("gpsimd", lambda e: e.dma_start(out=bias_sb[:, :], in_=brh[:, :]),
               writes=(biasB,), dma=True)
        P.emit("gpsimd", lambda e: e.dma_start(out=arena[:, A_IDENT:A_IDENT + 128], in_=idh[:, :]),
               writes=(identB,), dma=True)
        P.emit("vector", lambda e: e.memset(onc[:, :], 1.0), writes=(oncB,))
        P.emit("vector", lambda e: e.memset(onr[:, :], 1.0), writes=(onrB,))

        def ln_phase(src_dram, src_bufs):
            ahview = arena[:, 0:65536].rearrange("p (c s) -> p c s", c=32)
            for st in range(2):
                for cg in range(4):
                    c0 = cg * 8
                    dst = ahview[:, c0:c0 + 8, st * ST:(st + 1) * ST]
                    srcv = src_dram[:, c0:c0 + 8, st * ST:(st + 1) * ST]
                    extra = []
                    if src_bufs is not None:
                        extra = [src_bufs[c].w for c in range(c0, c0 + 8)
                                 if src_bufs[c].w]
                    wr = []
                    for c in range(c0, c0 + 8):
                        wr.extend(AH(c * S + st * ST, c * S + st * ST + ST))
                    P.emit("gpsimd",
                           lambda e, d=dst, s=srcv: e.dma_start(out=d, in_=s),
                           writes=wr, extra=extra, dma=True)
            for st in range(2):
                for c in range(NCH):
                    col = c * S + st * ST
                    x_ap = ah_ap(col, ST)
                    sq_ap = arena[:, STG[c % 2]:STG[c % 2] + ST]
                    P.emit("vector",
                           lambda e, o=sq_ap, i=x_ap: e.tensor_mul(o, i, i),
                           reads=AH(col, col + ST), writes=(stgB[c % 2],))
                    for h in range(2):
                        P.emit("tensor",
                               lambda e, r=x_ap, hh=h, s=(c == 0), t=(c == NCH - 1):
                               e.matmul(psC[0:1, hh * 512:(hh + 1) * 512], onc[:, :],
                                        r[:, hh * 512:(hh + 1) * 512], start=s, stop=t,
                                        skip_group_check=True),
                               reads=AH(col, col + ST) + [oncB],
                               writes=(psCh[h],))
                        P.emit("tensor",
                               lambda e, r=sq_ap, hh=h, s=(c == 0), t=(c == NCH - 1):
                               e.matmul(psC[32:33, hh * 512:(hh + 1) * 512], onc[:, :],
                                        r[:, hh * 512:(hh + 1) * 512], start=s, stop=t,
                                        skip_group_check=True),
                               reads=(stgB[c % 2], oncB),
                               writes=(psCh[h],))
                P.emit("vector",
                       lambda e: e.tensor_scalar(rows[0:1, 0:1024], psC[0:1, :],
                                                 1.0 / 4096.0, None, ALU.mult),
                       reads=(psCh[0], psCh[1]), writes=(rowB[0],))
                P.emit("vector",
                       lambda e: e.tensor_scalar(rows[0:1, 1024:2048], psC[32:33, :],
                                                 1.0 / 4096.0, None, ALU.mult),
                       reads=(psCh[0], psCh[1]), writes=(rowB[1],))
                P.emit("vector",
                       lambda e: e.tensor_mul(rows[0:1, 2048:3072], rows[0:1, 0:1024], rows[0:1, 0:1024]),
                       reads=(rowB[0],), writes=(rowB[2],))
                P.emit("vector",
                       lambda e: e.tensor_sub(rows[0:1, 1024:2048], rows[0:1, 1024:2048], rows[0:1, 2048:3072]),
                       reads=(rowB[1], rowB[2]), writes=(rowB[1],))
                P.emit("scalar",
                       lambda e: e.activation(rows[0:1, 3072:4096], rows[0:1, 1024:2048], AF.Sqrt,
                                              bias=np.float32(1e-5), scale=np.float32(1.0)),
                       reads=(rowB[1],), writes=(rowB[3],))
                P.emit("vector",
                       lambda e: e.reciprocal(rows[0:1, 1024:2048], rows[0:1, 3072:4096]),
                       reads=(rowB[3],), writes=(rowB[1],))
                P.emit("scalar",
                       lambda e: e.copy(rowsb[0:1, 0:1024], rows[0:1, 1024:2048]),
                       reads=(rowB[1],), writes=(rowbB[0],))
                P.emit("vector",
                       lambda e: e.tensor_mul(rows[0:1, 2048:3072], rows[0:1, 0:1024], rows[0:1, 1024:2048]),
                       reads=(rowB[0], rowB[1]), writes=(rowB[2],))
                P.emit("vector",
                       lambda e: e.tensor_scalar(rowsb[0:1, 1024:2048], rows[0:1, 2048:3072],
                                                 -1.0, None, ALU.mult),
                       reads=(rowB[2],), writes=(rowbB[1],))
                for h in range(2):
                    P.emit("tensor",
                           lambda e, hh=h: e.matmul(psD[:, hh * 512:(hh + 1) * 512], onr[:, :],
                                              rowsb[0:1, hh * 512:(hh + 1) * 512],
                                              start=True, stop=True, skip_group_check=True),
                           reads=(rowbB[0], onrB),
                           writes=(psDh[h], psDd[0], psDd[1]) if h == 0 else (psDh[h],))
                    P.emit("tensor",
                           lambda e, hh=h: e.matmul(psC[:, hh * 512:(hh + 1) * 512], onr[:, :],
                                              rowsb[0:1, 1024 + hh * 512:1024 + (hh + 1) * 512],
                                              start=True, stop=True, skip_group_check=True),
                           reads=(rowbB[1], onrB), writes=(psCh[h],))
                P.emit("scalar",
                       lambda e: e.copy(arena[:, A_ABC_A:A_ABC_A + ST], psD[:, :]),
                       reads=(psDh[0], psDh[1]), writes=(abcB[0],))
                P.emit("scalar",
                       lambda e: e.copy(arena[:, A_ABC_B:A_ABC_B + ST], psC[:, :]),
                       reads=(psCh[0], psCh[1]), writes=(abcB[1],))
                for c in range(NCH):
                    col = c * S + st * ST
                    x_ap = ah_ap(col, ST)
                    a_ap = arena[:, A_ABC_A:A_ABC_A + ST]
                    b_ap = arena[:, A_ABC_B:A_ABC_B + ST]
                    P.emit("vector",
                           lambda e, x=x_ap, a=a_ap: e.tensor_mul(x, x, a),
                           reads=AH(col, col + ST) + [abcB[0]],
                           writes=AH(col, col + ST))
                    P.emit("vector",
                           lambda e, x=x_ap, b=b_ap: e.tensor_add(x, x, b),
                           reads=AH(col, col + ST) + [abcB[1]],
                           writes=AH(col, col + ST))

        slab_seq = []

        def run_slab_stream():
            n = len(slab_seq)
            for i in range(n + 1):
                if i < n:
                    src, jb, _ = slab_seq[i]
                    slot = i % 2
                    P.emit("gpsimd",
                           lambda e, s=slot, sr=src, j=jb:
                           e.dma_start(out=arena[:, SLAB[s]:SLAB[s] + 4096],
                                       in_=sr[:, j, :]),
                           writes=(slabB[slot],), dma=True)
                if i >= 1:
                    _, _, fn = slab_seq[i - 1]
                    fn((i - 1) % 2)

        ln_phase(xt, None)

        def qk_compute(jb):
            def fn(slot):
                for st in range(2):
                    ps = psA if st == 0 else psB
                    psh = psAh if st == 0 else psBh
                    for c in range(NCH):
                        col = c * S + st * ST
                        lhs = arena[:, SLAB[slot] + c * 128:SLAB[slot] + (c + 1) * 128]
                        for h in range(2):
                            rhs = ah_ap(col + h * 512, 512)
                            P.emit("tensor",
                                   lambda e, l=lhs, r=rhs, p=ps, hh=h,
                                          s=(c == 0), t=(c == NCH - 1):
                                   e.matmul(p[:, hh * 512:(hh + 1) * 512], l, r,
                                            start=s, stop=t, skip_group_check=True),
                                   reads=AH(col, col + ST) + [slabB[slot]],
                                   writes=(psh[h],))
                    stg_ap = arena[:, STG[st]:STG[st] + ST]
                    P.emit("scalar",
                           lambda e, o=stg_ap, p=ps: e.copy(o, p[:, :]),
                           reads=(psh[0], psh[1]), writes=(stgB[st],))
                    if jb < 32:
                        dst = qscr[:, jb, st * ST:(st + 1) * ST]
                        wb = qscrB[jb]
                    else:
                        dst = kscr[:, jb - 32, st * ST:(st + 1) * ST]
                        wb = kscrB[jb - 32]
                    P.emit("sync",
                           lambda e, d=dst, s=stg_ap: e.dma_start(out=d, in_=s),
                           reads=(stgB[st],), writes=(wb,), dma=True)
            return fn

        for jb in range(40):
            slab_seq.append((w1r, jb, qk_compute(jb)))

        def v_compute(dvt):
            def fn(slot):
                rot = [(psC, psCh, 0), (psC, psCh, 1), (psD, psDh, 0), (psD, psDh, 1)]
                for tsub in range(16):
                    ps, psh, hh = rot[tsub % 4]
                    ps_ap = ps[:, hh * 512:hh * 512 + 128]
                    st = tsub // 8
                    vw = (psh[hh], psDd[0], psDd[1]) if (ps is psD and hh == 0) else (psh[hh],)
                    for c in range(NCH):
                        col = c * S + tsub * 128
                        lhs = ah_ap(col, 128)
                        rhs = arena[:, SLAB[slot] + c * 128:SLAB[slot] + (c + 1) * 128]
                        P.emit("tensor",
                               lambda e, l=lhs, r=rhs, p=ps_ap, s=(c == 0), t=(c == NCH - 1):
                               e.matmul(p, l, r, start=s, stop=t,
                                        skip_group_check=True),
                               reads=AH(col, col + 128) + [slabB[slot]],
                               writes=vw)
                    stg_ap = arena[:, A_STG0 + tsub * 128:A_STG0 + (tsub + 1) * 128]
                    P.emit("scalar",
                           lambda e, o=stg_ap, p=ps_ap: e.copy(o, p),
                           reads=(psh[hh],), writes=(stgB[0], stgB[1]))
                dst = vscr[:, :, dvt * 128:(dvt + 1) * 128]
                vst = arena[:, A_STG0:A_STG0 + 2048].rearrange(
                    "p (t d) -> p t d", t=16)
                P.emit("sync",
                       lambda e, d=dst, s=vst: e.dma_start(out=d, in_=s),
                       reads=(stgB[0], stgB[1]),
                       extra=[vscrB[t].w for t in range(16) if vscrB[t].w],
                       writes=[vscrB[t] for t in range(16)], dma=True)
            return fn

        for dvt in range(8):
            slab_seq.append((w1r, 40 + dvt, v_compute(dvt)))

        run_slab_stream()
        slab_seq = []

        P.emit("gpsimd", lambda e: e.dma_start(out=arena[:, MASK0:MASK0 + 2048], in_=mkh[:, :]),
               writes=AH(MASK0, MASK0 + 2048), dma=True)
        QOFF, KOFF, VOFF = 0, 16384, 32768
        pend = [None]
        qview = arena[:, QOFF:QOFF + 16384].rearrange("p (hq s) -> p hq s", hq=32)
        kview = arena[:, KOFF:KOFF + 16384].rearrange("p (kb s) -> p kb s", kb=8)
        vflat = vscr.rearrange("p t d -> p (t d)")
        P.emit("sync",
               lambda e: e.dma_start(out=kview[:, :, :], in_=kscr[:, :, :]),
               extra=[kscrB[j].w for j in range(8) if kscrB[j].w],
               writes=AH(KOFF, KOFF + 16384), dma=True)
        P.emit("sync",
               lambda e: e.dma_start(out=arena[:, VOFF:VOFF + 16384],
                                     in_=vflat[:, :]),
               extra=[vscrB[j].w for j in range(16) if vscrB[j].w],
               writes=AH(VOFF, VOFF + 16384), dma=True)
        for st2 in range(4):
            for qg4 in range(4):
                q0 = qg4 * 8
                P.emit("sync",
                       lambda e, q=q0, s=st2:
                       e.dma_start(out=qview[:, q:q + 8, :],
                                   in_=qscr[:, q:q + 8, s * 512:(s + 1) * 512]),
                       extra=[qscrB[q0 + j].w for j in range(8) if qscrB[q0 + j].w],
                       writes=AH(QOFF + q0 * 512, QOFF + (q0 + 8) * 512), dma=True)

            for kb in range(8):
                for g in range(4):
                    hq = kb * 4 + g
                    avt, avhl = [(psA, psAh), (psA, psAh), (psB, psBh), (psB, psBh)][g]
                    avhh = g % 2
                    av_ap = avt[:, avhh * 512:(avhh + 1) * 512]
                    ntc = st2 * 4 + 4
                    q_ap = arena[:, QOFF + hq * 512:QOFF + (hq + 1) * 512]
                    qg = AH(QOFF + hq * 512, QOFF + (hq + 1) * 512)

                    def qk_step(tc):
                        kcol = KOFF + kb * S + tc * 128
                        k_ap = arena[:, kcol:kcol + 128]
                        qk_ap = psC[:, (tc % 2) * 512:(tc % 2 + 1) * 512]
                        pt_col = PT0 + (tc % 4) * 1024
                        pt_ap = arena[:, pt_col:pt_col + 512]
                        P.emit("tensor",
                               lambda e, l=k_ap, r=q_ap, p=qk_ap:
                               e.matmul(p, l, r, start=True, stop=True,
                                        skip_group_check=True),
                               reads=AH(kcol, kcol + 128) + qg,
                               writes=(psCh[tc % 2],))
                        P.emit("scalar",
                               lambda e, o=pt_ap, i=qk_ap:
                               e.activation(o, i, AF.Exp,
                                            bias=np.float32(SHIFT), scale=np.float32(SC)),
                               reads=(psCh[tc % 2],),
                               writes=AH(pt_col, pt_col + 512))
                        if tc >= st2 * 4:
                            r_idx = tc - st2 * 4
                            m_ap = arena[:, MASK0 + r_idx * 512:MASK0 + (r_idx + 1) * 512]
                            P.emit("vector",
                                   lambda e, p=pt_ap, m=m_ap: e.tensor_mul(p, p, m),
                                   reads=AH(pt_col, pt_col + 512) +
                                         AH(MASK0 + r_idx * 512, MASK0 + (r_idx + 1) * 512),
                                   writes=AH(pt_col, pt_col + 512))

                    def dav_step(tc):
                        pt_col = PT0 + (tc % 4) * 1024
                        pt_ap = arena[:, pt_col:pt_col + 512]
                        vcol = VOFF + tc * 1024 + kb * 128
                        v_ap = arena[:, vcol:vcol + 128]
                        P.emit("tensor",
                               lambda e, p=pt_ap, s=(tc == 0), t=(tc == ntc - 1), gg=g:
                               e.matmul(psD[32 * (gg % 2):32 * (gg % 2) + 1, 0:512],
                                        onc[:, :], p,
                                        start=s, stop=t, skip_group_check=True),
                               reads=AH(pt_col, pt_col + 512) + [oncB],
                               writes=(psDd[g % 2],))
                        P.emit("tensor",
                               lambda e, l=v_ap, p=pt_ap, o=av_ap,
                                      s=(tc == 0), t=(tc == ntc - 1):
                               e.matmul(o, l, p, start=s, stop=t,
                                        skip_group_check=True),
                               reads=AH(pt_col, pt_col + 512) + AH(vcol, vcol + 128),
                               writes=(avhl[avhh],))

                    for tc in range(ntc + 2):
                        if tc < ntc:
                            qk_step(tc)
                        if tc >= 2:
                            dav_step(tc - 2)

                    def make_evac(g=g, hq=hq, st2=st2, av_ap=av_ap,
                                  avhl=avhl, avhh=avhh):
                        def evac():
                            P.emit("vector",
                                   lambda e, gg=g: e.reciprocal(
                                       rrec[:, :],
                                       psD[32 * (gg % 2):32 * (gg % 2) + 1, 0:512]),
                                   reads=(psDd[g % 2],), writes=(rrecB,))
                            P.emit("scalar",
                                   lambda e: e.copy(rowsb[0:1, 2048:2560], rrec[:, :]),
                                   reads=(rrecB,), writes=(rowbB[2],))
                            P.emit("tensor",
                                   lambda e: e.matmul(psD[:, 512:1024], onr[:, :],
                                                      rowsb[0:1, 2048:2560],
                                                      start=True, stop=True,
                                                      skip_group_check=True),
                                   reads=(rowbB[2], onrB), writes=(psDh[1],))
                            ar_col = ARAW0 + (g % 2) * 1024
                            ar_ap = arena[:, ar_col:ar_col + 512]
                            P.emit("scalar",
                                   lambda e, o=ar_ap, i=av_ap: e.copy(o, i),
                                   reads=(avhl[avhh],), writes=AH(ar_col, ar_col + 512))
                            af_col = AFIN0 + (g % 2) * 1024
                            af_ap = arena[:, af_col:af_col + 512]
                            P.emit("vector",
                                   lambda e, o=af_ap, a=ar_ap:
                                   e.tensor_mul(o, psD[:, 512:1024], a),
                                   reads=[psDh[1]] + AH(ar_col, ar_col + 512),
                                   writes=AH(af_col, af_col + 512))
                            P.emit("sync",
                                   lambda e, h=hq, s=st2, a=af_ap:
                                   e.dma_start(out=ascr[:, h, s * 512:(s + 1) * 512], in_=a),
                                   reads=AH(af_col, af_col + 512),
                                   writes=(ascrB[hq],), dma=True)
                        return evac
                    if pend[0] is not None:
                        pend[0]()
                    pend[0] = make_evac()

        if pend[0] is not None:
            pend[0]()
            pend[0] = None

        for hq in range(32):
            dcol = hq * S
            P.emit("sync",
                   lambda e, h=hq, d=dcol:
                   e.dma_start(out=arena[:, d:d + S], in_=ascr[:, h, :]),
                   reads=(ascrB[hq],), writes=AH(dcol, dcol + S), dma=True)

        def dense_compute(ob, wsrc, out_is_x1):
            def fn(slot):
                for st in range(2):
                    ps = psA if st == 0 else psB
                    psh = psAh if st == 0 else psBh
                    for c in range(NCH):
                        col = c * S + st * ST
                        lhs = arena[:, SLAB[slot] + c * 128:SLAB[slot] + (c + 1) * 128]
                        for h in range(2):
                            rhs = ah_ap(col + h * 512, 512)
                            P.emit("tensor",
                                   lambda e, l=lhs, r=rhs, p=ps, hh=h, s=(c == 0),
                                          t=(c == NCH - 1 and out_is_x1):
                                   e.matmul(p[:, hh * 512:(hh + 1) * 512], l, r,
                                            start=s, stop=t, skip_group_check=True),
                                   reads=AH(col, col + ST) + [slabB[slot]],
                                   writes=(psh[h],))
                    if not out_is_x1:
                        for h in range(2):
                            P.emit("tensor",
                                   lambda e, p=ps, o=ob, s2=st, hh=h:
                                   e.matmul(p[:, hh * 512:(hh + 1) * 512],
                                            bf_sb[0:32, o * 128:(o + 1) * 128],
                                            zmT[0:32, s2 * ST + hh * 512:s2 * ST + (hh + 1) * 512],
                                            start=False, stop=True, skip_group_check=True),
                                   reads=(bfB, zmTB), writes=(psh[h],))
                    for half in range(2):
                        src = (xt if out_is_x1 else x1scr)[
                            :, ob, st * ST + half * 512:st * ST + (half + 1) * 512]
                        fx_ap = fa[:, FAX[half]:FAX[half] + 512]
                        extra = [] if out_is_x1 else ([x1B[ob].w] if x1B[ob].w else [])
                        P.emit("sync",
                               lambda e, d=fx_ap, s=src: e.dma_start(out=d, in_=s),
                               writes=(faxB[half],), extra=extra, dma=True)
                        fo_ap = fa[:, FAO[half]:FAO[half] + 512]
                        ps_ap = ps[:, half * 512:(half + 1) * 512]
                        P.emit("vector",
                               lambda e, o=fo_ap, p=ps_ap, x=fx_ap:
                               e.tensor_tensor(o, p, x, ALU.add),
                               reads=(psh[half], faxB[half]), writes=(faoB[half],))
                        dstd = (x1scr if out_is_x1 else outt)[
                            :, ob, st * ST + half * 512:st * ST + (half + 1) * 512]
                        wb = (x1B[ob],) if out_is_x1 else ()
                        tok = P.emit("sync",
                                     lambda e, d=dstd, s=fo_ap: e.dma_start(out=d, in_=s),
                                     reads=(faoB[half],), writes=wb, dma=True)
                        if not out_is_x1:
                            out_toks.append(tok)
            return fn

        for ob in range(32):
            slab_seq.append((w2r, ob, dense_compute(ob, w2r, True)))
        run_slab_stream()
        slab_seq = []

        ln_phase(x1scr, x1B)

        for ssg in range(16):
            scol = ssg * 128
            for c in range(NCH):
                col = c * S + scol
                lhs = ah_ap(col, 128)
                rhs = arena[:, A_GA + c * 36:A_GA + (c + 1) * 36]
                P.emit("tensor",
                       lambda e, l=lhs, r=rhs, s=(c == 0):
                       e.matmul(psC[:, 0:36], l, r, start=s, stop=False,
                                skip_group_check=True),
                       reads=AH(col, col + 128) + [gaB],
                       writes=(psCh[0],))
            P.emit("tensor",
                   lambda e: e.matmul(psC[:, 0:4], onr[:, :], bias_sb[:, :],
                                      start=False, stop=True, skip_group_check=True),
                   reads=(biasB, onrB), writes=(psCh[0],))
            P.emit("vector",
                   lambda e: e.reduce_max(mx[:, :], psC[:, 0:4], axis=AX.X),
                   reads=(psCh[0],), writes=(mxB,))
            P.emit("vector",
                   lambda e: e.tensor_scalar(oh[:, :], psC[:, 0:4], mx[:, 0:1],
                                             None, ALU.is_ge),
                   reads=(psCh[0], mxB), writes=(ohB,))
            for ee in range(4):
                P.emit("vector",
                       lambda e, v=ee: e.tensor_scalar(zm[:, v * 8:(v + 1) * 8],
                                                       psC[:, 4 + v * 8:4 + (v + 1) * 8],
                                                       oh[:, v:v + 1], None, ALU.mult),
                       reads=(psCh[0], ohB), writes=(zmB,))
            P.emit("tensor",
                   lambda e: e.matmul(psD[0:32, 0:128], zm[:, :],
                                      arena[:, A_IDENT:A_IDENT + 128],
                                      start=True, stop=True, skip_group_check=True),
                   reads=(zmB, identB), writes=(psDh[0], psDd[0], psDd[1]))
            P.emit("scalar",
                   lambda e, sc=scol: e.copy(zmT[0:32, sc:sc + 128], psD[0:32, 0:128]),
                   reads=(psDh[0],), writes=(zmTB,))

        for ob in range(32):
            slab_seq.append((w3r, ob, dense_compute(ob, w3r, False)))
        run_slab_stream()

        P.emit("gpsimd", lambda e: e.nop(), extra=out_toks)

        P.finalize()

        blk.tensor(lambda e: P.replay("tensor", e, sems))
        blk.vector(lambda e: P.replay("vector", e, sems))
        blk.scalar(lambda e: P.replay("scalar", e, sems))
        blk.sync(lambda e: P.replay("sync", e, sems))
        blk.gpsimd(lambda e: P.replay("gpsimd", e, sems))

    return nc


_cache = {}


def kernel(L0_attn_q, L0_attn_w_qkv, L0_attn_w_dense, L0_ffn_W_up,
           L0_ffn_gate_w, L0_ffn_expert_bias, L0_ffn_lora_A, L0_ffn_lora_B):
    x = np.asarray(L0_attn_q, dtype=np.float32)
    xt_all = np.ascontiguousarray(
        x.reshape(8, 2048, 32, 128).transpose(0, 3, 2, 1))
    wq = np.ascontiguousarray(np.asarray(L0_attn_w_qkv, np.float32)
                              .reshape(48, 128, 32, 128).transpose(3, 0, 2, 1))
    wd = np.ascontiguousarray(np.asarray(L0_attn_w_dense, np.float32)
                              .reshape(32, 128, 32, 128).transpose(3, 0, 2, 1))
    wu = np.ascontiguousarray(np.asarray(L0_ffn_W_up, np.float32)
                              .reshape(32, 128, 32, 128).transpose(3, 0, 2, 1))
    ga = np.zeros((128, 32, 36), np.float32)
    ga[:, :, 0:4] = np.asarray(L0_ffn_gate_w, np.float32) \
        .reshape(4, 32, 128).transpose(2, 1, 0)
    ga[:, :, 4:36] = np.asarray(L0_ffn_lora_A, np.float32) \
        .reshape(32, 32, 128).transpose(2, 1, 0)
    bf = np.ascontiguousarray(np.asarray(L0_ffn_lora_B, np.float32)
                              .transpose(0, 2, 1).reshape(32, 4096))
    br = np.asarray(L0_ffn_expert_bias, np.float32).reshape(1, 4)
    idh = np.eye(128, dtype=np.float32)
    mkh = np.zeros((128, 2048), np.float32)
    pp = np.arange(128)[:, None]
    cc = np.arange(512)[None, :]
    for r_i, r in enumerate((0, 128, 256, 384)):
        mkh[:, r_i * 512:(r_i + 1) * 512] = (pp + r <= cc).astype(np.float32)

    if "nc" not in _cache:
        _cache["nc"] = build()
    nc = _cache["nc"]
    shared = dict(w1=wq, w2=wd, w3=wu, gah=ga, bfh=bf, brh=br, idh=idh, mkh=mkh)
    in_maps = [dict(xt=xt_all[i], **shared) for i in range(8)]
    res = run_bass_kernel_spmd(nc, in_maps, core_ids=list(range(8)))
    out = np.stack([res.results[i]["out"] for i in range(8)])
    return np.ascontiguousarray(
        out.transpose(0, 3, 2, 1).reshape(8, 2048, 4096)).astype(np.float32)


# version 07
# speedup vs baseline: 1.2710x; 1.0817x over previous
import math
import numpy as np
import concourse.bass as bass
import concourse.mybir as mybir
from concourse.bass_utils import run_bass_kernel_spmd
from contextlib import ExitStack

F32 = mybir.dt.float32
BF16 = mybir.dt.bfloat16
AF = mybir.ActivationFunctionType
ALU = mybir.AluOpType
AX = mybir.AxisListType

S = 2048
NCH = 32
ST = 1024
AST = 512
SC = 1.0 / math.sqrt(128.0)
SHIFT = -12.0

ENGS = ("tensor", "vector", "scalar", "sync", "gpsimd")

A_H = 0
A_SLAB0 = 65536
A_SLAB1 = 69632
A_STG0 = 73728
A_STG1 = 74752
A_ABC_A = 75776
A_ABC_B = 76800
A_GA = 77824
A_IDENT = 78976
ARENA_COLS = 79104

PT0 = 48 * 1024
ARAW0 = 52 * 1024
AFIN0 = 54 * 1024
MASK0 = 56 * 1024


class Buf:
    __slots__ = ("w", "r")

    def __init__(self):
        self.w = None
        self.r = []


class Prog:
    def __init__(self):
        self.ops = {e: [] for e in ENGS}
        self.cnt = {e: 0 for e in ENGS}

    def emit(self, eng, fn, reads=(), writes=(), extra=(), dma=False):
        deps = {}

        def add(t):
            if t is None:
                return
            e, v = t
            if v > deps.get(e, 0):
                deps[e] = v

        for t in extra:
            add(t)
        for b in reads:
            add(b.w)
        for b in writes:
            add(b.w)
            for t in b.r:
                add(t)
        if dma:
            step = 16
        elif eng in ("sync", "gpsimd"):
            step = 0
        else:
            step = 1
        self.ops[eng].append((deps, fn, step))
        self.cnt[eng] += step
        tok = (eng, self.cnt[eng])
        for b in reads:
            b.r.append(tok)
        for b in writes:
            b.w = tok
            b.r = []
        return tok

    def finalize(self):
        waited = {"sync": set(), "gpsimd": set()}
        for ename in ENGS:
            for deps, fn, step in self.ops[ename]:
                for de, dv in deps.items():
                    if de in waited:
                        waited[de].add(dv)
        for q in ("sync", "gpsimd"):
            newops = []
            cum = 0
            for deps, fn, step in self.ops[q]:
                deps = {k: v for k, v in deps.items() if k != q}
                newops.append((deps, fn, step))
                if step == 16:
                    cum += 16
                    if cum in waited[q]:
                        newops.append(({q: cum}, None, 0))
            self.ops[q] = newops

    def replay(self, eng_name, eng, sems):
        last_wait = {}
        last_drain = 0
        own = 0
        for deps, fn, step in self.ops[eng_name]:
            if fn is None:
                eng.wait_ge(sems[eng_name], deps[eng_name])
                continue
            for e, v in deps.items():
                if e == eng_name:
                    if eng_name != "tensor" and v > last_drain:
                        eng.drain()
                        last_drain = own
                else:
                    if v > last_wait.get(e, 0):
                        eng.wait_ge(sems[e], v)
                        last_wait[e] = v
            inst = fn(eng)
            if step:
                own += step
                inst.then_inc(sems[eng_name], step)


def build():
    nc = bass.Bass("TRN2", target_bir_lowering=False)

    xt = nc.dram_tensor("xt", [128, NCH, S], F32, kind="ExternalInput")
    w1 = nc.dram_tensor("w1", [128, 48, NCH, 128], F32, kind="ExternalInput")
    w2 = nc.dram_tensor("w2", [128, 32, NCH, 128], F32, kind="ExternalInput")
    w3 = nc.dram_tensor("w3", [128, 32, NCH, 128], F32, kind="ExternalInput")
    gah = nc.dram_tensor("gah", [128, NCH, 36], F32, kind="ExternalInput")
    bfh = nc.dram_tensor("bfh", [32, 4096], F32, kind="ExternalInput")
    brh = nc.dram_tensor("brh", [1, 4], F32, kind="ExternalInput")
    idh = nc.dram_tensor("idh", [128, 128], F32, kind="ExternalInput")
    mkh = nc.dram_tensor("mkh", [128, 2048], F32, kind="ExternalInput")

    qscr = nc.dram_tensor("qscr", [128, 32, S], BF16, kind="Internal")
    kscr = nc.dram_tensor("kscr", [128, 8, S], BF16, kind="Internal")
    vscr = nc.dram_tensor("vscr", [128, 16, 1024], BF16, kind="Internal")
    ascr = nc.dram_tensor("ascr", [128, 32, S], BF16, kind="Internal")
    x1scr = nc.dram_tensor("x1scr", [128, NCH, S], F32, kind="Internal")
    dscr = nc.dram_tensor("dscr", [32, 2048], F32, kind="Internal")
    outt = nc.dram_tensor("out", [128, NCH, S], F32, kind="ExternalOutput")

    w1r = w1.rearrange("p j c k -> p j (c k)")
    w2r = w2.rearrange("p j c k -> p j (c k)")
    w3r = w3.rearrange("p j c k -> p j (c k)")
    gar = gah.rearrange("p c e -> p (c e)")

    P = Prog()

    with ExitStack() as ex:
        E = ex.enter_context
        arena = E(nc.sbuf_tensor("arena", [128, ARENA_COLS], BF16))
        fa = E(nc.sbuf_tensor("fa", [128, 2048], F32))
        rows = E(nc.sbuf_tensor("rows", [1, 4096], F32))
        rowsb = E(nc.sbuf_tensor("rowsb", [1, 2048], BF16))
        bf_sb = E(nc.sbuf_tensor("bf_sb", [32, 4096], BF16))
        zmT = E(nc.sbuf_tensor("zmT", [32, 2048], BF16))
        zm = E(nc.sbuf_tensor("zm", [128, 32], BF16))
        mx = E(nc.sbuf_tensor("mx", [128, 1], F32))
        oh = E(nc.sbuf_tensor("oh", [128, 4], F32))
        onc = E(nc.sbuf_tensor("onc", [128, 1], BF16))
        onr = E(nc.sbuf_tensor("onr", [1, 128], BF16))
        bias_sb = E(nc.sbuf_tensor("bias_sb", [1, 4], BF16))
        denr = E(nc.sbuf_tensor("denr", [1, 1024], F32))
        rall = E(nc.sbuf_tensor("rall", [32, 2048], BF16))
        rrc2 = E(nc.sbuf_tensor("rrc2", [1, 2048], BF16))

        psA = E(nc.psum_tensor("psA", [128, 1024], F32))
        psB = E(nc.psum_tensor("psB", [128, 1024], F32))
        psC = E(nc.psum_tensor("psC", [128, 1024], F32))
        psD = E(nc.psum_tensor("psD", [128, 1024], F32))

        sems = {e: E(nc.semaphore(f"sem_{e}")) for e in ENGS}
        blk = E(nc.Block())

        AHg = [Buf() for _ in range(64)]

        def AH(c0, c1):
            return AHg[c0 // 1024:(c1 + 1023) // 1024]

        slabB = [Buf(), Buf()]
        stgB = [Buf(), Buf()]
        abcB = [Buf(), Buf()]
        gaB, identB, bfB, biasB = Buf(), Buf(), Buf(), Buf()
        oncB, onrB = Buf(), Buf()
        rowB = [Buf() for _ in range(4)]
        rowbB = [Buf() for _ in range(4)]
        psAh = [Buf(), Buf()]
        psBh = [Buf(), Buf()]
        psCh = [Buf(), Buf()]
        psDh = [Buf(), Buf()]
        psDd = [Buf(), Buf()]
        zmB, zmTB, mxB, ohB, rrecB = Buf(), Buf(), Buf(), Buf(), Buf()
        denrB = [Buf(), Buf()]
        dallB, rallB, rallfB = Buf(), Buf(), Buf()
        rrc2B = [Buf(), Buf()]
        faxB = [Buf(), Buf()]
        faoB = [Buf(), Buf()]
        qscrB = [Buf() for _ in range(32)]
        kscrB = [Buf() for _ in range(8)]
        vscrB = [Buf() for _ in range(16)]
        ascrB = [Buf() for _ in range(32)]
        x1B = [Buf() for _ in range(32)]
        out_toks = []

        SLAB = [A_SLAB0, A_SLAB1]
        STG = [A_STG0, A_STG1]
        FAX = [0, 512]
        FAO = [1024, 1536]

        def ah_ap(c0, width):
            return arena[:, c0:c0 + width]

        P.emit("gpsimd", lambda e: e.dma_start(out=bf_sb[:, :], in_=bfh[:, :]),
               writes=(bfB,), dma=True)
        P.emit("gpsimd", lambda e: e.dma_start(out=arena[:, A_GA:A_GA + 1152], in_=gar[:, :]),
               writes=(gaB,), dma=True)
        P.emit("gpsimd", lambda e: e.dma_start(out=bias_sb[:, :], in_=brh[:, :]),
               writes=(biasB,), dma=True)
        P.emit("gpsimd", lambda e: e.dma_start(out=arena[:, A_IDENT:A_IDENT + 128], in_=idh[:, :]),
               writes=(identB,), dma=True)
        P.emit("vector", lambda e: e.memset(onc[:, :], 1.0), writes=(oncB,))
        P.emit("vector", lambda e: e.memset(onr[:, :], 1.0), writes=(onrB,))

        def ln_phase(src_dram, src_bufs):
            ahview = arena[:, 0:65536].rearrange("p (c s) -> p c s", c=32)
            for st in range(2):
                for cg in range(4):
                    c0 = cg * 8
                    dst = ahview[:, c0:c0 + 8, st * ST:(st + 1) * ST]
                    srcv = src_dram[:, c0:c0 + 8, st * ST:(st + 1) * ST]
                    extra = []
                    if src_bufs is not None:
                        extra = [src_bufs[c].w for c in range(c0, c0 + 8)
                                 if src_bufs[c].w]
                    wr = []
                    for c in range(c0, c0 + 8):
                        wr.extend(AH(c * S + st * ST, c * S + st * ST + ST))
                    P.emit("gpsimd",
                           lambda e, d=dst, s=srcv: e.dma_start(out=d, in_=s),
                           writes=wr, extra=extra, dma=True)
            for st in range(2):
                for c in range(NCH):
                    col = c * S + st * ST
                    x_ap = ah_ap(col, ST)
                    sq_ap = arena[:, STG[c % 2]:STG[c % 2] + ST]
                    P.emit("vector",
                           lambda e, o=sq_ap, i=x_ap: e.tensor_mul(o, i, i),
                           reads=AH(col, col + ST), writes=(stgB[c % 2],))
                    for h in range(2):
                        P.emit("tensor",
                               lambda e, r=x_ap, hh=h, s=(c == 0), t=(c == NCH - 1):
                               e.matmul(psC[0:1, hh * 512:(hh + 1) * 512], onc[:, :],
                                        r[:, hh * 512:(hh + 1) * 512], start=s, stop=t,
                                        skip_group_check=True),
                               reads=AH(col, col + ST) + [oncB],
                               writes=(psCh[h],))
                        P.emit("tensor",
                               lambda e, r=sq_ap, hh=h, s=(c == 0), t=(c == NCH - 1):
                               e.matmul(psC[32:33, hh * 512:(hh + 1) * 512], onc[:, :],
                                        r[:, hh * 512:(hh + 1) * 512], start=s, stop=t,
                                        skip_group_check=True),
                               reads=(stgB[c % 2], oncB),
                               writes=(psCh[h],))
                P.emit("vector",
                       lambda e: e.tensor_scalar(rows[0:1, 0:1024], psC[0:1, :],
                                                 1.0 / 4096.0, None, ALU.mult),
                       reads=(psCh[0], psCh[1]), writes=(rowB[0],))
                P.emit("vector",
                       lambda e: e.tensor_scalar(rows[0:1, 1024:2048], psC[32:33, :],
                                                 1.0 / 4096.0, None, ALU.mult),
                       reads=(psCh[0], psCh[1]), writes=(rowB[1],))
                P.emit("vector",
                       lambda e: e.tensor_mul(rows[0:1, 2048:3072], rows[0:1, 0:1024], rows[0:1, 0:1024]),
                       reads=(rowB[0],), writes=(rowB[2],))
                P.emit("vector",
                       lambda e: e.tensor_sub(rows[0:1, 1024:2048], rows[0:1, 1024:2048], rows[0:1, 2048:3072]),
                       reads=(rowB[1], rowB[2]), writes=(rowB[1],))
                P.emit("scalar",
                       lambda e: e.activation(rows[0:1, 3072:4096], rows[0:1, 1024:2048], AF.Sqrt,
                                              bias=np.float32(1e-5), scale=np.float32(1.0)),
                       reads=(rowB[1],), writes=(rowB[3],))
                P.emit("vector",
                       lambda e: e.reciprocal(rows[0:1, 1024:2048], rows[0:1, 3072:4096]),
                       reads=(rowB[3],), writes=(rowB[1],))
                P.emit("scalar",
                       lambda e: e.copy(rowsb[0:1, 0:1024], rows[0:1, 1024:2048]),
                       reads=(rowB[1],), writes=(rowbB[0],))
                P.emit("vector",
                       lambda e: e.tensor_mul(rows[0:1, 2048:3072], rows[0:1, 0:1024], rows[0:1, 1024:2048]),
                       reads=(rowB[0], rowB[1]), writes=(rowB[2],))
                P.emit("vector",
                       lambda e: e.tensor_scalar(rowsb[0:1, 1024:2048], rows[0:1, 2048:3072],
                                                 -1.0, None, ALU.mult),
                       reads=(rowB[2],), writes=(rowbB[1],))
                for h in range(2):
                    P.emit("tensor",
                           lambda e, hh=h: e.matmul(psD[:, hh * 512:(hh + 1) * 512], onr[:, :],
                                              rowsb[0:1, hh * 512:(hh + 1) * 512],
                                              start=True, stop=True, skip_group_check=True),
                           reads=(rowbB[0], onrB),
                           writes=(psDh[h], psDd[0], psDd[1]) if h == 0 else (psDh[h],))
                    P.emit("tensor",
                           lambda e, hh=h: e.matmul(psC[:, hh * 512:(hh + 1) * 512], onr[:, :],
                                              rowsb[0:1, 1024 + hh * 512:1024 + (hh + 1) * 512],
                                              start=True, stop=True, skip_group_check=True),
                           reads=(rowbB[1], onrB), writes=(psCh[h],))
                P.emit("scalar",
                       lambda e: e.copy(arena[:, A_ABC_A:A_ABC_A + ST], psD[:, :]),
                       reads=(psDh[0], psDh[1]), writes=(abcB[0],))
                P.emit("scalar",
                       lambda e: e.copy(arena[:, A_ABC_B:A_ABC_B + ST], psC[:, :]),
                       reads=(psCh[0], psCh[1]), writes=(abcB[1],))
                for c in range(NCH):
                    col = c * S + st * ST
                    x_ap = ah_ap(col, ST)
                    a_ap = arena[:, A_ABC_A:A_ABC_A + ST]
                    b_ap = arena[:, A_ABC_B:A_ABC_B + ST]
                    P.emit("vector",
                           lambda e, x=x_ap, a=a_ap: e.tensor_mul(x, x, a),
                           reads=AH(col, col + ST) + [abcB[0]],
                           writes=AH(col, col + ST))
                    P.emit("vector",
                           lambda e, x=x_ap, b=b_ap: e.tensor_add(x, x, b),
                           reads=AH(col, col + ST) + [abcB[1]],
                           writes=AH(col, col + ST))

        slab_seq = []

        def run_slab_stream():
            n = len(slab_seq)
            for i in range(n + 1):
                if i < n:
                    src, jb, _ = slab_seq[i]
                    slot = i % 2
                    P.emit("gpsimd",
                           lambda e, s=slot, sr=src, j=jb:
                           e.dma_start(out=arena[:, SLAB[s]:SLAB[s] + 4096],
                                       in_=sr[:, j, :]),
                           writes=(slabB[slot],), dma=True)
                if i >= 1:
                    _, _, fn = slab_seq[i - 1]
                    fn((i - 1) % 2)

        ln_phase(xt, None)

        def qk_compute(jb):
            def fn(slot):
                for st in range(2):
                    ps = psA if st == 0 else psB
                    psh = psAh if st == 0 else psBh
                    for c in range(NCH):
                        col = c * S + st * ST
                        lhs = arena[:, SLAB[slot] + c * 128:SLAB[slot] + (c + 1) * 128]
                        for h in range(2):
                            rhs = ah_ap(col + h * 512, 512)
                            P.emit("tensor",
                                   lambda e, l=lhs, r=rhs, p=ps, hh=h,
                                          s=(c == 0), t=(c == NCH - 1):
                                   e.matmul(p[:, hh * 512:(hh + 1) * 512], l, r,
                                            start=s, stop=t, skip_group_check=True),
                                   reads=AH(col, col + ST) + [slabB[slot]],
                                   writes=(psh[h],))
                    stg_ap = arena[:, STG[st]:STG[st] + ST]
                    P.emit("scalar",
                           lambda e, o=stg_ap, p=ps: e.copy(o, p[:, :]),
                           reads=(psh[0], psh[1]), writes=(stgB[st],))
                    if jb < 32:
                        dst = qscr[:, jb, st * ST:(st + 1) * ST]
                        wb = qscrB[jb]
                    else:
                        dst = kscr[:, jb - 32, st * ST:(st + 1) * ST]
                        wb = kscrB[jb - 32]
                    P.emit("sync",
                           lambda e, d=dst, s=stg_ap: e.dma_start(out=d, in_=s),
                           reads=(stgB[st],), writes=(wb,), dma=True)
            return fn

        for jb in range(40):
            slab_seq.append((w1r, jb, qk_compute(jb)))

        def v_compute(dvt):
            def fn(slot):
                rot = [(psC, psCh, 0), (psC, psCh, 1), (psD, psDh, 0), (psD, psDh, 1)]
                for tsub in range(16):
                    ps, psh, hh = rot[tsub % 4]
                    ps_ap = ps[:, hh * 512:hh * 512 + 128]
                    st = tsub // 8
                    vw = (psh[hh], psDd[0], psDd[1]) if (ps is psD and hh == 0) else (psh[hh],)
                    for c in range(NCH):
                        col = c * S + tsub * 128
                        lhs = ah_ap(col, 128)
                        rhs = arena[:, SLAB[slot] + c * 128:SLAB[slot] + (c + 1) * 128]
                        P.emit("tensor",
                               lambda e, l=lhs, r=rhs, p=ps_ap, s=(c == 0), t=(c == NCH - 1):
                               e.matmul(p, l, r, start=s, stop=t,
                                        skip_group_check=True),
                               reads=AH(col, col + 128) + [slabB[slot]],
                               writes=vw)
                    stg_ap = arena[:, A_STG0 + tsub * 128:A_STG0 + (tsub + 1) * 128]
                    P.emit("scalar",
                           lambda e, o=stg_ap, p=ps_ap: e.copy(o, p),
                           reads=(psh[hh],), writes=(stgB[0], stgB[1]))
                dst = vscr[:, :, dvt * 128:(dvt + 1) * 128]
                vst = arena[:, A_STG0:A_STG0 + 2048].rearrange(
                    "p (t d) -> p t d", t=16)
                P.emit("sync",
                       lambda e, d=dst, s=vst: e.dma_start(out=d, in_=s),
                       reads=(stgB[0], stgB[1]),
                       extra=[vscrB[t].w for t in range(16) if vscrB[t].w],
                       writes=[vscrB[t] for t in range(16)], dma=True)
            return fn

        for dvt in range(8):
            slab_seq.append((w1r, 40 + dvt, v_compute(dvt)))

        run_slab_stream()
        slab_seq = []

        P.emit("gpsimd", lambda e: e.dma_start(out=arena[:, MASK0:MASK0 + 2048], in_=mkh[:, :]),
               writes=AH(MASK0, MASK0 + 2048), dma=True)
        QOFF, KOFF, VOFF = 0, 16384, 32768
        qview = arena[:, QOFF:QOFF + 16384].rearrange("p (hq s) -> p hq s", hq=32)
        kview = arena[:, KOFF:KOFF + 16384].rearrange("p (kb s) -> p kb s", kb=8)
        vflat = vscr.rearrange("p t d -> p (t d)")
        P.emit("sync",
               lambda e: e.dma_start(out=kview[:, :, :], in_=kscr[:, :, :]),
               extra=[kscrB[j].w for j in range(8) if kscrB[j].w],
               writes=AH(KOFF, KOFF + 16384), dma=True)
        P.emit("sync",
               lambda e: e.dma_start(out=arena[:, VOFF:VOFF + 16384],
                                     in_=vflat[:, :]),
               extra=[vscrB[j].w for j in range(16) if vscrB[j].w],
               writes=AH(VOFF, VOFF + 16384), dma=True)
        for st2 in range(4):
            for qg4 in range(4):
                q0 = qg4 * 8
                P.emit("sync",
                       lambda e, q=q0, s=st2:
                       e.dma_start(out=qview[:, q:q + 8, :],
                                   in_=qscr[:, q:q + 8, s * 512:(s + 1) * 512]),
                       extra=[qscrB[q0 + j].w for j in range(8) if qscrB[q0 + j].w],
                       writes=AH(QOFF + q0 * 512, QOFF + (q0 + 8) * 512), dma=True)

            for kb in range(8):
                for g in range(4):
                    hq = kb * 4 + g
                    avt, avhl = [(psA, psAh), (psA, psAh), (psB, psBh), (psB, psBh)][g]
                    avhh = g % 2
                    av_ap = avt[:, avhh * 512:(avhh + 1) * 512]
                    ntc = st2 * 4 + 4
                    q_ap = arena[:, QOFF + hq * 512:QOFF + (hq + 1) * 512]
                    qg = AH(QOFF + hq * 512, QOFF + (hq + 1) * 512)

                    def qk_step(tc):
                        kcol = KOFF + kb * S + tc * 128
                        k_ap = arena[:, kcol:kcol + 128]
                        qk_ap = psC[:, (tc % 2) * 512:(tc % 2 + 1) * 512]
                        pt_col = PT0 + (tc % 4) * 1024
                        pt_ap = arena[:, pt_col:pt_col + 512]
                        P.emit("tensor",
                               lambda e, l=k_ap, r=q_ap, p=qk_ap:
                               e.matmul(p, l, r, start=True, stop=True,
                                        skip_group_check=True),
                               reads=AH(kcol, kcol + 128) + qg,
                               writes=(psCh[tc % 2],))
                        P.emit("scalar",
                               lambda e, o=pt_ap, i=qk_ap:
                               e.activation(o, i, AF.Exp,
                                            bias=np.float32(SHIFT), scale=np.float32(SC)),
                               reads=(psCh[tc % 2],),
                               writes=AH(pt_col, pt_col + 512))
                        if tc >= st2 * 4:
                            r_idx = tc - st2 * 4
                            m_ap = arena[:, MASK0 + r_idx * 512:MASK0 + (r_idx + 1) * 512]
                            P.emit("vector",
                                   lambda e, p=pt_ap, m=m_ap: e.tensor_mul(p, p, m),
                                   reads=AH(pt_col, pt_col + 512) +
                                         AH(MASK0 + r_idx * 512, MASK0 + (r_idx + 1) * 512),
                                   writes=AH(pt_col, pt_col + 512))

                    def dav_step(tc):
                        pt_col = PT0 + (tc % 4) * 1024
                        pt_ap = arena[:, pt_col:pt_col + 512]
                        vcol = VOFF + tc * 1024 + kb * 128
                        v_ap = arena[:, vcol:vcol + 128]
                        P.emit("tensor",
                               lambda e, p=pt_ap, s=(tc == 0), t=(tc == ntc - 1), gg=g:
                               e.matmul(psD[32 * (gg % 2):32 * (gg % 2) + 1, 0:512],
                                        onc[:, :], p,
                                        start=s, stop=t, skip_group_check=True),
                               reads=AH(pt_col, pt_col + 512) + [oncB],
                               writes=(psDd[g % 2],))
                        P.emit("tensor",
                               lambda e, l=v_ap, p=pt_ap, o=av_ap,
                                      s=(tc == 0), t=(tc == ntc - 1):
                               e.matmul(o, l, p, start=s, stop=t,
                                        skip_group_check=True),
                               reads=AH(pt_col, pt_col + 512) + AH(vcol, vcol + 128),
                               writes=(avhl[avhh],))

                    for tc in range(ntc + 2):
                        if tc < ntc:
                            qk_step(tc)
                        if tc >= 2:
                            dav_step(tc - 2)

                    P.emit("scalar",
                           lambda e, gg=g: e.copy(
                               denr[0:1, (gg % 2) * 512:(gg % 2) * 512 + 512],
                               psD[32 * (gg % 2):32 * (gg % 2) + 1, 0:512]),
                           reads=(psDd[g % 2],), writes=(denrB[g % 2],))
                    P.emit("sync",
                           lambda e, h=hq, s=st2, gg=g:
                           e.dma_start(out=dscr[h:h + 1, s * 512:(s + 1) * 512],
                                       in_=denr[0:1, (gg % 2) * 512:(gg % 2) * 512 + 512]),
                           reads=(denrB[g % 2],), dma=True)
                    ar_col = ARAW0 + (g % 2) * 1024
                    ar_ap = arena[:, ar_col:ar_col + 512]
                    P.emit("scalar",
                           lambda e, o=ar_ap, i=av_ap: e.copy(o, i),
                           reads=(avhl[avhh],), writes=AH(ar_col, ar_col + 512))
                    P.emit("sync",
                           lambda e, h=hq, s=st2, a=ar_ap:
                           e.dma_start(out=ascr[:, h, s * 512:(s + 1) * 512], in_=a),
                           reads=AH(ar_col, ar_col + 512),
                           writes=(ascrB[hq],), dma=True)

        P.emit("sync",
               lambda e: e.dma_start(out=fa[0:32, :], in_=dscr[:, :]),
               writes=(dallB, faxB[0], faxB[1]), dma=True)
        P.emit("vector",
               lambda e: e.reciprocal(fa[32:64, :], fa[0:32, :]),
               reads=(dallB,),
               writes=(rallfB, faoB[0], faoB[1]))
        P.emit("scalar",
               lambda e: e.copy(rall[:, :], fa[32:64, :]),
               reads=(rallfB, faxB[0], faxB[1], faoB[0], faoB[1]),
               writes=(rallB,))

        for hq in range(32):
            dcol = hq * S
            P.emit("sync",
                   lambda e, h=hq, d=dcol:
                   e.dma_start(out=arena[:, d:d + S], in_=ascr[:, h, :]),
                   reads=(ascrB[hq],), writes=AH(dcol, dcol + S), dma=True)
            P.emit("sync",
                   lambda e, h=hq:
                   e.dma_start(out=rrc2[0:1, :],
                               in_=rall[h:h + 1, :]),
                   reads=(rallB,), writes=(rrc2B[0],), dma=True)
            for st2 in range(4):
                bc_ap = psC[:, (st2 % 2) * 512:(st2 % 2 + 1) * 512]
                a_ap = arena[:, dcol + st2 * 512:dcol + (st2 + 1) * 512]
                P.emit("tensor",
                       lambda e, h=hq, s=st2, b=bc_ap:
                       e.matmul(b, onr[:, :],
                                rrc2[0:1, s * 512:(s + 1) * 512],
                                start=True, stop=True, skip_group_check=True),
                       reads=(rrc2B[0], onrB), writes=(psCh[st2 % 2],))
                P.emit("vector",
                       lambda e, a=a_ap, b=bc_ap: e.tensor_mul(a, b, a),
                       reads=[psCh[st2 % 2]] + AH(dcol + st2 * 512, dcol + (st2 + 1) * 512),
                       writes=AH(dcol + st2 * 512, dcol + (st2 + 1) * 512))

        def dense_compute(ob, wsrc, out_is_x1):
            def fn(slot):
                for st in range(2):
                    ps = psA if st == 0 else psB
                    psh = psAh if st == 0 else psBh
                    for c in range(NCH):
                        col = c * S + st * ST
                        lhs = arena[:, SLAB[slot] + c * 128:SLAB[slot] + (c + 1) * 128]
                        for h in range(2):
                            rhs = ah_ap(col + h * 512, 512)
                            P.emit("tensor",
                                   lambda e, l=lhs, r=rhs, p=ps, hh=h, s=(c == 0),
                                          t=(c == NCH - 1 and out_is_x1):
                                   e.matmul(p[:, hh * 512:(hh + 1) * 512], l, r,
                                            start=s, stop=t, skip_group_check=True),
                                   reads=AH(col, col + ST) + [slabB[slot]],
                                   writes=(psh[h],))
                    if not out_is_x1:
                        for h in range(2):
                            P.emit("tensor",
                                   lambda e, p=ps, o=ob, s2=st, hh=h:
                                   e.matmul(p[:, hh * 512:(hh + 1) * 512],
                                            bf_sb[0:32, o * 128:(o + 1) * 128],
                                            zmT[0:32, s2 * ST + hh * 512:s2 * ST + (hh + 1) * 512],
                                            start=False, stop=True, skip_group_check=True),
                                   reads=(bfB, zmTB), writes=(psh[h],))
                    for half in range(2):
                        src = (xt if out_is_x1 else x1scr)[
                            :, ob, st * ST + half * 512:st * ST + (half + 1) * 512]
                        fx_ap = fa[:, FAX[half]:FAX[half] + 512]
                        extra = [] if out_is_x1 else ([x1B[ob].w] if x1B[ob].w else [])
                        P.emit("sync",
                               lambda e, d=fx_ap, s=src: e.dma_start(out=d, in_=s),
                               writes=(faxB[half],), extra=extra, dma=True)
                        fo_ap = fa[:, FAO[half]:FAO[half] + 512]
                        ps_ap = ps[:, half * 512:(half + 1) * 512]
                        P.emit("vector",
                               lambda e, o=fo_ap, p=ps_ap, x=fx_ap:
                               e.tensor_tensor(o, p, x, ALU.add),
                               reads=(psh[half], faxB[half]), writes=(faoB[half],))
                        dstd = (x1scr if out_is_x1 else outt)[
                            :, ob, st * ST + half * 512:st * ST + (half + 1) * 512]
                        wb = (x1B[ob],) if out_is_x1 else ()
                        tok = P.emit("sync",
                                     lambda e, d=dstd, s=fo_ap: e.dma_start(out=d, in_=s),
                                     reads=(faoB[half],), writes=wb, dma=True)
                        if not out_is_x1:
                            out_toks.append(tok)
            return fn

        for ob in range(32):
            slab_seq.append((w2r, ob, dense_compute(ob, w2r, True)))
        run_slab_stream()
        slab_seq = []

        ln_phase(x1scr, x1B)

        for ssg in range(16):
            scol = ssg * 128
            for c in range(NCH):
                col = c * S + scol
                lhs = ah_ap(col, 128)
                rhs = arena[:, A_GA + c * 36:A_GA + (c + 1) * 36]
                P.emit("tensor",
                       lambda e, l=lhs, r=rhs, s=(c == 0):
                       e.matmul(psC[:, 0:36], l, r, start=s, stop=False,
                                skip_group_check=True),
                       reads=AH(col, col + 128) + [gaB],
                       writes=(psCh[0],))
            P.emit("tensor",
                   lambda e: e.matmul(psC[:, 0:4], onr[:, :], bias_sb[:, :],
                                      start=False, stop=True, skip_group_check=True),
                   reads=(biasB, onrB), writes=(psCh[0],))
            P.emit("vector",
                   lambda e: e.reduce_max(mx[:, :], psC[:, 0:4], axis=AX.X),
                   reads=(psCh[0],), writes=(mxB,))
            P.emit("vector",
                   lambda e: e.tensor_scalar(oh[:, :], psC[:, 0:4], mx[:, 0:1],
                                             None, ALU.is_ge),
                   reads=(psCh[0], mxB), writes=(ohB,))
            for ee in range(4):
                P.emit("vector",
                       lambda e, v=ee: e.tensor_scalar(zm[:, v * 8:(v + 1) * 8],
                                                       psC[:, 4 + v * 8:4 + (v + 1) * 8],
                                                       oh[:, v:v + 1], None, ALU.mult),
                       reads=(psCh[0], ohB), writes=(zmB,))
            P.emit("tensor",
                   lambda e: e.matmul(psD[0:32, 0:128], zm[:, :],
                                      arena[:, A_IDENT:A_IDENT + 128],
                                      start=True, stop=True, skip_group_check=True),
                   reads=(zmB, identB), writes=(psDh[0], psDd[0], psDd[1]))
            P.emit("scalar",
                   lambda e, sc=scol: e.copy(zmT[0:32, sc:sc + 128], psD[0:32, 0:128]),
                   reads=(psDh[0],), writes=(zmTB,))

        for ob in range(32):
            slab_seq.append((w3r, ob, dense_compute(ob, w3r, False)))
        run_slab_stream()

        P.emit("gpsimd", lambda e: e.nop(), extra=out_toks)

        P.finalize()

        blk.tensor(lambda e: P.replay("tensor", e, sems))
        blk.vector(lambda e: P.replay("vector", e, sems))
        blk.scalar(lambda e: P.replay("scalar", e, sems))
        blk.sync(lambda e: P.replay("sync", e, sems))
        blk.gpsimd(lambda e: P.replay("gpsimd", e, sems))

    return nc


_cache = {}


def kernel(L0_attn_q, L0_attn_w_qkv, L0_attn_w_dense, L0_ffn_W_up,
           L0_ffn_gate_w, L0_ffn_expert_bias, L0_ffn_lora_A, L0_ffn_lora_B):
    x = np.asarray(L0_attn_q, dtype=np.float32)
    xt_all = np.ascontiguousarray(
        x.reshape(8, 2048, 32, 128).transpose(0, 3, 2, 1))
    wq = np.ascontiguousarray(np.asarray(L0_attn_w_qkv, np.float32)
                              .reshape(48, 128, 32, 128).transpose(3, 0, 2, 1))
    wd = np.ascontiguousarray(np.asarray(L0_attn_w_dense, np.float32)
                              .reshape(32, 128, 32, 128).transpose(3, 0, 2, 1))
    wu = np.ascontiguousarray(np.asarray(L0_ffn_W_up, np.float32)
                              .reshape(32, 128, 32, 128).transpose(3, 0, 2, 1))
    ga = np.zeros((128, 32, 36), np.float32)
    ga[:, :, 0:4] = np.asarray(L0_ffn_gate_w, np.float32) \
        .reshape(4, 32, 128).transpose(2, 1, 0)
    ga[:, :, 4:36] = np.asarray(L0_ffn_lora_A, np.float32) \
        .reshape(32, 32, 128).transpose(2, 1, 0)
    bf = np.ascontiguousarray(np.asarray(L0_ffn_lora_B, np.float32)
                              .transpose(0, 2, 1).reshape(32, 4096))
    br = np.asarray(L0_ffn_expert_bias, np.float32).reshape(1, 4)
    idh = np.eye(128, dtype=np.float32)
    mkh = np.zeros((128, 2048), np.float32)
    pp = np.arange(128)[:, None]
    cc = np.arange(512)[None, :]
    for r_i, r in enumerate((0, 128, 256, 384)):
        mkh[:, r_i * 512:(r_i + 1) * 512] = (pp + r <= cc).astype(np.float32)

    if "nc" not in _cache:
        _cache["nc"] = build()
    nc = _cache["nc"]
    shared = dict(w1=wq, w2=wd, w3=wu, gah=ga, bfh=bf, brh=br, idh=idh, mkh=mkh)
    in_maps = [dict(xt=xt_all[i], **shared) for i in range(8)]
    res = run_bass_kernel_spmd(nc, in_maps, core_ids=list(range(8)))
    out = np.stack([res.results[i]["out"] for i in range(8)])
    return np.ascontiguousarray(
        out.transpose(0, 3, 2, 1).reshape(8, 2048, 4096)).astype(np.float32)


# version 08 (final)
# speedup vs baseline: 1.3188x; 1.0376x over previous
import math
import numpy as np
import concourse.bass as bass
import concourse.mybir as mybir
from concourse.bass_utils import run_bass_kernel_spmd
from contextlib import ExitStack

F32 = mybir.dt.float32
BF16 = mybir.dt.bfloat16
AF = mybir.ActivationFunctionType
ALU = mybir.AluOpType
AX = mybir.AxisListType

S = 2048
NCH = 32
ST = 1024
AST = 512
SC = 1.0 / math.sqrt(128.0)
SHIFT = -12.0

ENGS = ("tensor", "vector", "scalar", "sync", "gpsimd")

A_H = 0
A_SLAB0 = 65536
A_SLAB1 = 69632
A_STG0 = 73728
A_STG1 = 74752
A_ABC_A = 75776
A_ABC_B = 76800
A_GA = 77824
A_IDENT = 78976
ARENA_COLS = 79104

PT0 = 48 * 1024
ARAW0 = 52 * 1024
AFIN0 = 54 * 1024
MASK0 = 56 * 1024


class Buf:
    __slots__ = ("w", "r")

    def __init__(self):
        self.w = None
        self.r = []


class Prog:
    def __init__(self):
        self.ops = {e: [] for e in ENGS}
        self.cnt = {e: 0 for e in ENGS}

    def emit(self, eng, fn, reads=(), writes=(), extra=(), dma=False):
        deps = {}

        def add(t):
            if t is None:
                return
            e, v = t
            if v > deps.get(e, 0):
                deps[e] = v

        for t in extra:
            add(t)
        for b in reads:
            add(b.w)
        for b in writes:
            add(b.w)
            for t in b.r:
                add(t)
        if dma:
            step = 16
        elif eng in ("sync", "gpsimd"):
            step = 0
        else:
            step = 1
        self.ops[eng].append((deps, fn, step))
        self.cnt[eng] += step
        tok = (eng, self.cnt[eng])
        for b in reads:
            b.r.append(tok)
        for b in writes:
            b.w = tok
            b.r = []
        return tok

    def finalize(self):
        waited = {"sync": set(), "gpsimd": set()}
        for ename in ENGS:
            for deps, fn, step in self.ops[ename]:
                for de, dv in deps.items():
                    if de in waited:
                        waited[de].add(dv)
        for q in ("sync", "gpsimd"):
            newops = []
            cum = 0
            for deps, fn, step in self.ops[q]:
                deps = {k: v for k, v in deps.items() if k != q}
                newops.append((deps, fn, step))
                if step == 16:
                    cum += 16
                    if cum in waited[q]:
                        newops.append(({q: cum}, None, 0))
            self.ops[q] = newops

    def replay(self, eng_name, eng, sems):
        last_wait = {}
        last_drain = 0
        own = 0
        for deps, fn, step in self.ops[eng_name]:
            if fn is None:
                eng.wait_ge(sems[eng_name], deps[eng_name])
                continue
            for e, v in deps.items():
                if e == eng_name:
                    if eng_name != "tensor" and v > last_drain:
                        eng.drain()
                        last_drain = own
                else:
                    if v > last_wait.get(e, 0):
                        eng.wait_ge(sems[e], v)
                        last_wait[e] = v
            inst = fn(eng)
            if step:
                own += step
                inst.then_inc(sems[eng_name], step)


def build():
    nc = bass.Bass("TRN2", target_bir_lowering=False)

    xt = nc.dram_tensor("xt", [128, NCH, S], F32, kind="ExternalInput")
    w1 = nc.dram_tensor("w1", [128, 48, NCH, 128], F32, kind="ExternalInput")
    w2 = nc.dram_tensor("w2", [128, 32, NCH, 128], F32, kind="ExternalInput")
    w3 = nc.dram_tensor("w3", [128, 32, NCH, 128], F32, kind="ExternalInput")
    gah = nc.dram_tensor("gah", [128, NCH, 36], F32, kind="ExternalInput")
    bfh = nc.dram_tensor("bfh", [32, 4096], F32, kind="ExternalInput")
    brh = nc.dram_tensor("brh", [1, 4], F32, kind="ExternalInput")
    idh = nc.dram_tensor("idh", [128, 128], F32, kind="ExternalInput")
    mkh = nc.dram_tensor("mkh", [128, 2048], F32, kind="ExternalInput")

    qscr = nc.dram_tensor("qscr", [128, 32, S], BF16, kind="Internal")
    kscr = nc.dram_tensor("kscr", [128, 8, S], BF16, kind="Internal")
    vscr = nc.dram_tensor("vscr", [128, 16, 1024], BF16, kind="Internal")
    ascr = nc.dram_tensor("ascr", [128, 32, S], BF16, kind="Internal")
    x1scr = nc.dram_tensor("x1scr", [128, NCH, S], F32, kind="Internal")
    dscr = nc.dram_tensor("dscr", [32, 2048], F32, kind="Internal")
    outt = nc.dram_tensor("out", [128, NCH, S], F32, kind="ExternalOutput")

    w1r = w1.rearrange("p j c k -> p j (c k)")
    w2r = w2.rearrange("p j c k -> p j (c k)")
    w3r = w3.rearrange("p j c k -> p j (c k)")
    gar = gah.rearrange("p c e -> p (c e)")

    P = Prog()

    with ExitStack() as ex:
        E = ex.enter_context
        arena = E(nc.sbuf_tensor("arena", [128, ARENA_COLS], BF16))
        fa = E(nc.sbuf_tensor("fa", [128, 2048], F32))
        rows = E(nc.sbuf_tensor("rows", [1, 4096], F32))
        rowsb = E(nc.sbuf_tensor("rowsb", [1, 2048], BF16))
        bf_sb = E(nc.sbuf_tensor("bf_sb", [32, 4096], BF16))
        zmT = E(nc.sbuf_tensor("zmT", [32, 2048], BF16))
        zm = E(nc.sbuf_tensor("zm", [128, 32], BF16))
        mx = E(nc.sbuf_tensor("mx", [128, 1], F32))
        oh = E(nc.sbuf_tensor("oh", [128, 4], F32))
        onc = E(nc.sbuf_tensor("onc", [128, 1], BF16))
        onr = E(nc.sbuf_tensor("onr", [1, 128], BF16))
        bias_sb = E(nc.sbuf_tensor("bias_sb", [1, 4], BF16))
        denr = E(nc.sbuf_tensor("denr", [1, 1024], F32))
        rall = E(nc.sbuf_tensor("rall", [32, 2048], BF16))
        rrc2 = E(nc.sbuf_tensor("rrc2", [1, 2048], BF16))

        psA = E(nc.psum_tensor("psA", [128, 1024], F32))
        psB = E(nc.psum_tensor("psB", [128, 1024], F32))
        psC = E(nc.psum_tensor("psC", [128, 1024], F32))
        psD = E(nc.psum_tensor("psD", [128, 1024], F32))

        sems = {e: E(nc.semaphore(f"sem_{e}")) for e in ENGS}
        blk = E(nc.Block())

        AHg = [Buf() for _ in range(64)]

        def AH(c0, c1):
            return AHg[c0 // 1024:(c1 + 1023) // 1024]

        slabB = [Buf(), Buf()]
        stgB = [Buf(), Buf()]
        abcB = [Buf(), Buf()]
        gaB, identB, bfB, biasB = Buf(), Buf(), Buf(), Buf()
        oncB, onrB = Buf(), Buf()
        rowB = [Buf() for _ in range(4)]
        rowbB = [Buf() for _ in range(4)]
        psAh = [Buf(), Buf()]
        psBh = [Buf(), Buf()]
        psCh = [Buf(), Buf()]
        psDh = [Buf(), Buf()]
        psDd = [Buf(), Buf()]
        zmB, zmTB, mxB, ohB, rrecB = Buf(), Buf(), Buf(), Buf(), Buf()
        denrB = [Buf(), Buf()]
        dallB, rallB, rallfB = Buf(), Buf(), Buf()
        rrc2B = [Buf(), Buf()]
        faxB = [Buf(), Buf()]
        faoB = [Buf(), Buf()]
        qscrB = [Buf() for _ in range(32)]
        kscrB = [Buf() for _ in range(8)]
        vscrB = [Buf() for _ in range(16)]
        ascrB = [Buf() for _ in range(32)]
        x1B = [Buf() for _ in range(32)]
        out_toks = []

        SLAB = [A_SLAB0, A_SLAB1]
        STG = [A_STG0, A_STG1]
        FAX = [0, 512]
        FAO = [1024, 1536]

        def ah_ap(c0, width):
            return arena[:, c0:c0 + width]

        P.emit("gpsimd", lambda e: e.dma_start(out=bf_sb[:, :], in_=bfh[:, :]),
               writes=(bfB,), dma=True)
        P.emit("gpsimd", lambda e: e.dma_start(out=arena[:, A_GA:A_GA + 1152], in_=gar[:, :]),
               writes=(gaB,), dma=True)
        P.emit("gpsimd", lambda e: e.dma_start(out=bias_sb[:, :], in_=brh[:, :]),
               writes=(biasB,), dma=True)
        P.emit("gpsimd", lambda e: e.dma_start(out=arena[:, A_IDENT:A_IDENT + 128], in_=idh[:, :]),
               writes=(identB,), dma=True)
        P.emit("vector", lambda e: e.memset(onc[:, :], 1.0), writes=(oncB,))
        P.emit("vector", lambda e: e.memset(onr[:, :], 1.0), writes=(onrB,))

        def ln_phase(src_dram, src_bufs):
            ahview = arena[:, 0:65536].rearrange("p (c s) -> p c s", c=32)
            for st in range(2):
                for cg in range(4):
                    c0 = cg * 8
                    dst = ahview[:, c0:c0 + 8, st * ST:(st + 1) * ST]
                    srcv = src_dram[:, c0:c0 + 8, st * ST:(st + 1) * ST]
                    extra = []
                    if src_bufs is not None:
                        extra = [src_bufs[c].w for c in range(c0, c0 + 8)
                                 if src_bufs[c].w]
                    wr = []
                    for c in range(c0, c0 + 8):
                        wr.extend(AH(c * S + st * ST, c * S + st * ST + ST))
                    P.emit("gpsimd",
                           lambda e, d=dst, s=srcv: e.dma_start(out=d, in_=s),
                           writes=wr, extra=extra, dma=True)
            for st in range(2):
                for c in range(NCH):
                    col = c * S + st * ST
                    x_ap = ah_ap(col, ST)
                    sq_ap = arena[:, STG[c % 2]:STG[c % 2] + ST]
                    P.emit("scalar",
                           lambda e, o=sq_ap, i=x_ap: e.square(o, i),
                           reads=AH(col, col + ST), writes=(stgB[c % 2],))
                    for h in range(2):
                        P.emit("tensor",
                               lambda e, r=x_ap, hh=h, s=(c == 0), t=(c == NCH - 1):
                               e.matmul(psC[0:1, hh * 512:(hh + 1) * 512], onc[:, :],
                                        r[:, hh * 512:(hh + 1) * 512], start=s, stop=t,
                                        skip_group_check=True),
                               reads=AH(col, col + ST) + [oncB],
                               writes=(psCh[h],))
                        P.emit("tensor",
                               lambda e, r=sq_ap, hh=h, s=(c == 0), t=(c == NCH - 1):
                               e.matmul(psC[32:33, hh * 512:(hh + 1) * 512], onc[:, :],
                                        r[:, hh * 512:(hh + 1) * 512], start=s, stop=t,
                                        skip_group_check=True),
                               reads=(stgB[c % 2], oncB),
                               writes=(psCh[h],))
                P.emit("vector",
                       lambda e: e.tensor_scalar(rows[0:1, 0:1024], psC[0:1, :],
                                                 1.0 / 4096.0, None, ALU.mult),
                       reads=(psCh[0], psCh[1]), writes=(rowB[0],))
                P.emit("vector",
                       lambda e: e.tensor_scalar(rows[0:1, 1024:2048], psC[32:33, :],
                                                 1.0 / 4096.0, None, ALU.mult),
                       reads=(psCh[0], psCh[1]), writes=(rowB[1],))
                P.emit("vector",
                       lambda e: e.tensor_mul(rows[0:1, 2048:3072], rows[0:1, 0:1024], rows[0:1, 0:1024]),
                       reads=(rowB[0],), writes=(rowB[2],))
                P.emit("vector",
                       lambda e: e.tensor_sub(rows[0:1, 1024:2048], rows[0:1, 1024:2048], rows[0:1, 2048:3072]),
                       reads=(rowB[1], rowB[2]), writes=(rowB[1],))
                P.emit("scalar",
                       lambda e: e.activation(rows[0:1, 3072:4096], rows[0:1, 1024:2048], AF.Sqrt,
                                              bias=np.float32(1e-5), scale=np.float32(1.0)),
                       reads=(rowB[1],), writes=(rowB[3],))
                P.emit("vector",
                       lambda e: e.reciprocal(rows[0:1, 1024:2048], rows[0:1, 3072:4096]),
                       reads=(rowB[3],), writes=(rowB[1],))
                P.emit("scalar",
                       lambda e: e.copy(rowsb[0:1, 0:1024], rows[0:1, 1024:2048]),
                       reads=(rowB[1],), writes=(rowbB[0],))
                P.emit("vector",
                       lambda e: e.tensor_mul(rows[0:1, 2048:3072], rows[0:1, 0:1024], rows[0:1, 1024:2048]),
                       reads=(rowB[0], rowB[1]), writes=(rowB[2],))
                P.emit("vector",
                       lambda e: e.tensor_scalar(rowsb[0:1, 1024:2048], rows[0:1, 2048:3072],
                                                 -1.0, None, ALU.mult),
                       reads=(rowB[2],), writes=(rowbB[1],))
                for h in range(2):
                    P.emit("tensor",
                           lambda e, hh=h: e.matmul(psD[:, hh * 512:(hh + 1) * 512], onr[:, :],
                                              rowsb[0:1, hh * 512:(hh + 1) * 512],
                                              start=True, stop=True, skip_group_check=True),
                           reads=(rowbB[0], onrB),
                           writes=(psDh[h], psDd[0], psDd[1]) if h == 0 else (psDh[h],))
                    P.emit("tensor",
                           lambda e, hh=h: e.matmul(psC[:, hh * 512:(hh + 1) * 512], onr[:, :],
                                              rowsb[0:1, 1024 + hh * 512:1024 + (hh + 1) * 512],
                                              start=True, stop=True, skip_group_check=True),
                           reads=(rowbB[1], onrB), writes=(psCh[h],))
                P.emit("scalar",
                       lambda e: e.copy(arena[:, A_ABC_A:A_ABC_A + ST], psD[:, :]),
                       reads=(psDh[0], psDh[1]), writes=(abcB[0],))
                P.emit("scalar",
                       lambda e: e.copy(arena[:, A_ABC_B:A_ABC_B + ST], psC[:, :]),
                       reads=(psCh[0], psCh[1]), writes=(abcB[1],))
                for c in range(NCH):
                    col = c * S + st * ST
                    x_ap = ah_ap(col, ST)
                    a_ap = arena[:, A_ABC_A:A_ABC_A + ST]
                    b_ap = arena[:, A_ABC_B:A_ABC_B + ST]
                    P.emit("vector",
                           lambda e, x=x_ap, a=a_ap: e.tensor_mul(x, x, a),
                           reads=AH(col, col + ST) + [abcB[0]],
                           writes=AH(col, col + ST))
                    P.emit("vector",
                           lambda e, x=x_ap, b=b_ap: e.tensor_add(x, x, b),
                           reads=AH(col, col + ST) + [abcB[1]],
                           writes=AH(col, col + ST))

        slab_seq = []

        def run_slab_stream():
            n = len(slab_seq)
            for i in range(n + 1):
                if i < n:
                    src, jb, _ = slab_seq[i]
                    slot = i % 2
                    P.emit("gpsimd",
                           lambda e, s=slot, sr=src, j=jb:
                           e.dma_start(out=arena[:, SLAB[s]:SLAB[s] + 4096],
                                       in_=sr[:, j, :]),
                           writes=(slabB[slot],), dma=True)
                if i >= 1:
                    _, _, fn = slab_seq[i - 1]
                    fn((i - 1) % 2)

        ln_phase(xt, None)

        def qk_compute(jb):
            def fn(slot):
                for st in range(2):
                    ps = psA if st == 0 else psB
                    psh = psAh if st == 0 else psBh
                    for c in range(NCH):
                        col = c * S + st * ST
                        lhs = arena[:, SLAB[slot] + c * 128:SLAB[slot] + (c + 1) * 128]
                        for h in range(2):
                            rhs = ah_ap(col + h * 512, 512)
                            P.emit("tensor",
                                   lambda e, l=lhs, r=rhs, p=ps, hh=h,
                                          s=(c == 0), t=(c == NCH - 1):
                                   e.matmul(p[:, hh * 512:(hh + 1) * 512], l, r,
                                            start=s, stop=t, skip_group_check=True),
                                   reads=AH(col, col + ST) + [slabB[slot]],
                                   writes=(psh[h],))
                    stg_ap = arena[:, STG[st]:STG[st] + ST]
                    P.emit("scalar",
                           lambda e, o=stg_ap, p=ps: e.copy(o, p[:, :]),
                           reads=(psh[0], psh[1]), writes=(stgB[st],))
                    if jb < 32:
                        dst = qscr[:, jb, st * ST:(st + 1) * ST]
                        wb = qscrB[jb]
                    else:
                        dst = kscr[:, jb - 32, st * ST:(st + 1) * ST]
                        wb = kscrB[jb - 32]
                    P.emit("sync",
                           lambda e, d=dst, s=stg_ap: e.dma_start(out=d, in_=s),
                           reads=(stgB[st],), writes=(wb,), dma=True)
            return fn

        for jb in range(40):
            slab_seq.append((w1r, jb, qk_compute(jb)))

        def v_compute(dvt):
            def fn(slot):
                rot = [(psC, psCh, 0), (psC, psCh, 1), (psD, psDh, 0), (psD, psDh, 1)]
                for tsub in range(16):
                    ps, psh, hh = rot[tsub % 4]
                    ps_ap = ps[:, hh * 512:hh * 512 + 128]
                    st = tsub // 8
                    vw = (psh[hh], psDd[0], psDd[1]) if (ps is psD and hh == 0) else (psh[hh],)
                    for c in range(NCH):
                        col = c * S + tsub * 128
                        lhs = ah_ap(col, 128)
                        rhs = arena[:, SLAB[slot] + c * 128:SLAB[slot] + (c + 1) * 128]
                        P.emit("tensor",
                               lambda e, l=lhs, r=rhs, p=ps_ap, s=(c == 0), t=(c == NCH - 1):
                               e.matmul(p, l, r, start=s, stop=t,
                                        skip_group_check=True),
                               reads=AH(col, col + 128) + [slabB[slot]],
                               writes=vw)
                    stg_ap = arena[:, A_STG0 + tsub * 128:A_STG0 + (tsub + 1) * 128]
                    P.emit("scalar",
                           lambda e, o=stg_ap, p=ps_ap: e.copy(o, p),
                           reads=(psh[hh],), writes=(stgB[0], stgB[1]))
                dst = vscr[:, :, dvt * 128:(dvt + 1) * 128]
                vst = arena[:, A_STG0:A_STG0 + 2048].rearrange(
                    "p (t d) -> p t d", t=16)
                P.emit("sync",
                       lambda e, d=dst, s=vst: e.dma_start(out=d, in_=s),
                       reads=(stgB[0], stgB[1]),
                       extra=[vscrB[t].w for t in range(16) if vscrB[t].w],
                       writes=[vscrB[t] for t in range(16)], dma=True)
            return fn

        for dvt in range(8):
            slab_seq.append((w1r, 40 + dvt, v_compute(dvt)))

        run_slab_stream()
        slab_seq = []

        P.emit("gpsimd", lambda e: e.dma_start(out=arena[:, MASK0:MASK0 + 2048], in_=mkh[:, :]),
               writes=AH(MASK0, MASK0 + 2048), dma=True)
        QOFF, KOFF, VOFF = 0, 16384, 32768
        qview = arena[:, QOFF:QOFF + 16384].rearrange("p (hq s) -> p hq s", hq=32)
        kview = arena[:, KOFF:KOFF + 16384].rearrange("p (kb s) -> p kb s", kb=8)
        vflat = vscr.rearrange("p t d -> p (t d)")
        P.emit("sync",
               lambda e: e.dma_start(out=kview[:, :, :], in_=kscr[:, :, :]),
               extra=[kscrB[j].w for j in range(8) if kscrB[j].w],
               writes=AH(KOFF, KOFF + 16384), dma=True)
        P.emit("sync",
               lambda e: e.dma_start(out=arena[:, VOFF:VOFF + 16384],
                                     in_=vflat[:, :]),
               extra=[vscrB[j].w for j in range(16) if vscrB[j].w],
               writes=AH(VOFF, VOFF + 16384), dma=True)
        for st2 in range(4):
            for qg4 in range(4):
                q0 = qg4 * 8
                P.emit("sync",
                       lambda e, q=q0, s=st2:
                       e.dma_start(out=qview[:, q:q + 8, :],
                                   in_=qscr[:, q:q + 8, s * 512:(s + 1) * 512]),
                       extra=[qscrB[q0 + j].w for j in range(8) if qscrB[q0 + j].w],
                       writes=AH(QOFF + q0 * 512, QOFF + (q0 + 8) * 512), dma=True)

            for kb in range(8):
                for g in range(4):
                    hq = kb * 4 + g
                    avt, avhl = [(psA, psAh), (psA, psAh), (psB, psBh), (psB, psBh)][g]
                    avhh = g % 2
                    av_ap = avt[:, avhh * 512:(avhh + 1) * 512]
                    ntc = st2 * 4 + 4
                    q_ap = arena[:, QOFF + hq * 512:QOFF + (hq + 1) * 512]
                    qg = AH(QOFF + hq * 512, QOFF + (hq + 1) * 512)

                    qk_rot = [(psC[:, 0:512], psCh[0]),
                              (psC[:, 512:1024], psCh[1]),
                              (psD[:, 512:1024], psDh[1])]

                    def qk_step(tc):
                        kcol = KOFF + kb * S + tc * 128
                        k_ap = arena[:, kcol:kcol + 128]
                        qk_ap, qk_buf = qk_rot[tc % 3]
                        pt_col = PT0 + (tc % 4) * 1024
                        pt_ap = arena[:, pt_col:pt_col + 512]
                        P.emit("tensor",
                               lambda e, l=k_ap, r=q_ap, p=qk_ap:
                               e.matmul(p, l, r, start=True, stop=True,
                                        skip_group_check=True),
                               reads=AH(kcol, kcol + 128) + qg,
                               writes=(qk_buf,))
                        P.emit("scalar",
                               lambda e, o=pt_ap, i=qk_ap:
                               e.activation(o, i, AF.Exp,
                                            bias=np.float32(SHIFT), scale=np.float32(SC)),
                               reads=(qk_buf,),
                               writes=AH(pt_col, pt_col + 512))
                        if tc >= st2 * 4:
                            r_idx = tc - st2 * 4
                            m_ap = arena[:, MASK0 + r_idx * 512:MASK0 + (r_idx + 1) * 512]
                            P.emit("vector",
                                   lambda e, p=pt_ap, m=m_ap: e.tensor_mul(p, p, m),
                                   reads=AH(pt_col, pt_col + 512) +
                                         AH(MASK0 + r_idx * 512, MASK0 + (r_idx + 1) * 512),
                                   writes=AH(pt_col, pt_col + 512))

                    def dav_step(tc):
                        pt_col = PT0 + (tc % 4) * 1024
                        pt_ap = arena[:, pt_col:pt_col + 512]
                        vcol = VOFF + tc * 1024 + kb * 128
                        v_ap = arena[:, vcol:vcol + 128]
                        P.emit("tensor",
                               lambda e, p=pt_ap, s=(tc == 0), t=(tc == ntc - 1), gg=g:
                               e.matmul(psD[32 * (gg % 2):32 * (gg % 2) + 1, 0:512],
                                        onc[:, :], p,
                                        start=s, stop=t, skip_group_check=True),
                               reads=AH(pt_col, pt_col + 512) + [oncB],
                               writes=(psDd[g % 2],))
                        P.emit("tensor",
                               lambda e, l=v_ap, p=pt_ap, o=av_ap,
                                      s=(tc == 0), t=(tc == ntc - 1):
                               e.matmul(o, l, p, start=s, stop=t,
                                        skip_group_check=True),
                               reads=AH(pt_col, pt_col + 512) + AH(vcol, vcol + 128),
                               writes=(avhl[avhh],))

                    for tc in range(ntc + 2):
                        if tc < ntc:
                            qk_step(tc)
                        if tc >= 2:
                            dav_step(tc - 2)

                    P.emit("scalar",
                           lambda e, gg=g: e.copy(
                               denr[0:1, (gg % 2) * 512:(gg % 2) * 512 + 512],
                               psD[32 * (gg % 2):32 * (gg % 2) + 1, 0:512]),
                           reads=(psDd[g % 2],), writes=(denrB[g % 2],))
                    P.emit("sync",
                           lambda e, h=hq, s=st2, gg=g:
                           e.dma_start(out=dscr[h:h + 1, s * 512:(s + 1) * 512],
                                       in_=denr[0:1, (gg % 2) * 512:(gg % 2) * 512 + 512]),
                           reads=(denrB[g % 2],), dma=True)
                    ar_col = ARAW0 + (g % 2) * 1024
                    ar_ap = arena[:, ar_col:ar_col + 512]
                    P.emit("scalar",
                           lambda e, o=ar_ap, i=av_ap: e.copy(o, i),
                           reads=(avhl[avhh],), writes=AH(ar_col, ar_col + 512))
                    P.emit("sync",
                           lambda e, h=hq, s=st2, a=ar_ap:
                           e.dma_start(out=ascr[:, h, s * 512:(s + 1) * 512], in_=a),
                           reads=AH(ar_col, ar_col + 512),
                           writes=(ascrB[hq],), dma=True)

        P.emit("sync",
               lambda e: e.dma_start(out=fa[0:32, :], in_=dscr[:, :]),
               writes=(dallB, faxB[0], faxB[1]), dma=True)
        P.emit("vector",
               lambda e: e.reciprocal(fa[32:64, :], fa[0:32, :]),
               reads=(dallB,),
               writes=(rallfB, faoB[0], faoB[1]))
        P.emit("scalar",
               lambda e: e.copy(rall[:, :], fa[32:64, :]),
               reads=(rallfB, faxB[0], faxB[1], faoB[0], faoB[1]),
               writes=(rallB,))

        for hq in range(32):
            dcol = hq * S
            P.emit("sync",
                   lambda e, h=hq, d=dcol:
                   e.dma_start(out=arena[:, d:d + S], in_=ascr[:, h, :]),
                   reads=(ascrB[hq],), writes=AH(dcol, dcol + S), dma=True)
            P.emit("sync",
                   lambda e, h=hq:
                   e.dma_start(out=rrc2[0:1, :],
                               in_=rall[h:h + 1, :]),
                   reads=(rallB,), writes=(rrc2B[0],), dma=True)
            for st2 in range(4):
                bc_ap = psC[:, (st2 % 2) * 512:(st2 % 2 + 1) * 512]
                a_ap = arena[:, dcol + st2 * 512:dcol + (st2 + 1) * 512]
                P.emit("tensor",
                       lambda e, h=hq, s=st2, b=bc_ap:
                       e.matmul(b, onr[:, :],
                                rrc2[0:1, s * 512:(s + 1) * 512],
                                start=True, stop=True, skip_group_check=True),
                       reads=(rrc2B[0], onrB), writes=(psCh[st2 % 2],))
                P.emit("vector",
                       lambda e, a=a_ap, b=bc_ap: e.tensor_mul(a, b, a),
                       reads=[psCh[st2 % 2]] + AH(dcol + st2 * 512, dcol + (st2 + 1) * 512),
                       writes=AH(dcol + st2 * 512, dcol + (st2 + 1) * 512))

        def dense_compute(ob, wsrc, out_is_x1):
            def fn(slot):
                for st in range(2):
                    ps = psA if st == 0 else psB
                    psh = psAh if st == 0 else psBh
                    for c in range(NCH):
                        col = c * S + st * ST
                        lhs = arena[:, SLAB[slot] + c * 128:SLAB[slot] + (c + 1) * 128]
                        for h in range(2):
                            rhs = ah_ap(col + h * 512, 512)
                            P.emit("tensor",
                                   lambda e, l=lhs, r=rhs, p=ps, hh=h, s=(c == 0),
                                          t=(c == NCH - 1 and out_is_x1):
                                   e.matmul(p[:, hh * 512:(hh + 1) * 512], l, r,
                                            start=s, stop=t, skip_group_check=True),
                                   reads=AH(col, col + ST) + [slabB[slot]],
                                   writes=(psh[h],))
                    if not out_is_x1:
                        for h in range(2):
                            P.emit("tensor",
                                   lambda e, p=ps, o=ob, s2=st, hh=h:
                                   e.matmul(p[:, hh * 512:(hh + 1) * 512],
                                            bf_sb[0:32, o * 128:(o + 1) * 128],
                                            zmT[0:32, s2 * ST + hh * 512:s2 * ST + (hh + 1) * 512],
                                            start=False, stop=True, skip_group_check=True),
                                   reads=(bfB, zmTB), writes=(psh[h],))
                    for half in range(2):
                        src = (xt if out_is_x1 else x1scr)[
                            :, ob, st * ST + half * 512:st * ST + (half + 1) * 512]
                        fx_ap = fa[:, FAX[half]:FAX[half] + 512]
                        extra = [] if out_is_x1 else ([x1B[ob].w] if x1B[ob].w else [])
                        P.emit("sync",
                               lambda e, d=fx_ap, s=src: e.dma_start(out=d, in_=s),
                               writes=(faxB[half],), extra=extra, dma=True)
                        fo_ap = fa[:, FAO[half]:FAO[half] + 512]
                        ps_ap = ps[:, half * 512:(half + 1) * 512]
                        P.emit("vector",
                               lambda e, o=fo_ap, p=ps_ap, x=fx_ap:
                               e.tensor_tensor(o, p, x, ALU.add),
                               reads=(psh[half], faxB[half]), writes=(faoB[half],))
                        dstd = (x1scr if out_is_x1 else outt)[
                            :, ob, st * ST + half * 512:st * ST + (half + 1) * 512]
                        wb = (x1B[ob],) if out_is_x1 else ()
                        tok = P.emit("sync",
                                     lambda e, d=dstd, s=fo_ap: e.dma_start(out=d, in_=s),
                                     reads=(faoB[half],), writes=wb, dma=True)
                        if not out_is_x1:
                            out_toks.append(tok)
            return fn

        for ob in range(32):
            slab_seq.append((w2r, ob, dense_compute(ob, w2r, True)))
        run_slab_stream()
        slab_seq = []

        ln_phase(x1scr, x1B)

        for ssg in range(16):
            scol = ssg * 128
            zo = (ssg % 2) * 512
            zb = psCh[ssg % 2]
            for c in range(NCH):
                col = c * S + scol
                lhs = ah_ap(col, 128)
                rhs = arena[:, A_GA + c * 36:A_GA + (c + 1) * 36]
                P.emit("tensor",
                       lambda e, l=lhs, r=rhs, s=(c == 0), z=zo:
                       e.matmul(psC[:, z:z + 36], l, r, start=s, stop=False,
                                skip_group_check=True),
                       reads=AH(col, col + 128) + [gaB],
                       writes=(zb,))
            P.emit("tensor",
                   lambda e, z=zo: e.matmul(psC[:, z:z + 4], onr[:, :], bias_sb[:, :],
                                      start=False, stop=True, skip_group_check=True),
                   reads=(biasB, onrB), writes=(zb,))
            P.emit("vector",
                   lambda e, z=zo: e.reduce_max(mx[:, :], psC[:, z:z + 4], axis=AX.X),
                   reads=(zb,), writes=(mxB,))
            P.emit("vector",
                   lambda e, z=zo: e.tensor_scalar(oh[:, :], psC[:, z:z + 4], mx[:, 0:1],
                                             None, ALU.is_ge),
                   reads=(zb, mxB), writes=(ohB,))
            for ee in range(4):
                P.emit("vector",
                       lambda e, v=ee, z=zo: e.tensor_scalar(zm[:, v * 8:(v + 1) * 8],
                                                       psC[:, z + 4 + v * 8:z + 4 + (v + 1) * 8],
                                                       oh[:, v:v + 1], None, ALU.mult),
                       reads=(zb, ohB), writes=(zmB,))
            P.emit("tensor",
                   lambda e: e.matmul(psD[0:32, 0:128], zm[:, :],
                                      arena[:, A_IDENT:A_IDENT + 128],
                                      start=True, stop=True, skip_group_check=True),
                   reads=(zmB, identB), writes=(psDh[0], psDd[0], psDd[1]))
            P.emit("scalar",
                   lambda e, sc=scol: e.copy(zmT[0:32, sc:sc + 128], psD[0:32, 0:128]),
                   reads=(psDh[0],), writes=(zmTB,))

        for ob in range(32):
            slab_seq.append((w3r, ob, dense_compute(ob, w3r, False)))
        run_slab_stream()

        P.emit("gpsimd", lambda e: e.nop(), extra=out_toks)

        P.finalize()

        blk.tensor(lambda e: P.replay("tensor", e, sems))
        blk.vector(lambda e: P.replay("vector", e, sems))
        blk.scalar(lambda e: P.replay("scalar", e, sems))
        blk.sync(lambda e: P.replay("sync", e, sems))
        blk.gpsimd(lambda e: P.replay("gpsimd", e, sems))

    return nc


_cache = {}


def kernel(L0_attn_q, L0_attn_w_qkv, L0_attn_w_dense, L0_ffn_W_up,
           L0_ffn_gate_w, L0_ffn_expert_bias, L0_ffn_lora_A, L0_ffn_lora_B):
    x = np.asarray(L0_attn_q, dtype=np.float32)
    xt_all = np.ascontiguousarray(
        x.reshape(8, 2048, 32, 128).transpose(0, 3, 2, 1))
    wq = np.ascontiguousarray(np.asarray(L0_attn_w_qkv, np.float32)
                              .reshape(48, 128, 32, 128).transpose(3, 0, 2, 1))
    wd = np.ascontiguousarray(np.asarray(L0_attn_w_dense, np.float32)
                              .reshape(32, 128, 32, 128).transpose(3, 0, 2, 1))
    wu = np.ascontiguousarray(np.asarray(L0_ffn_W_up, np.float32)
                              .reshape(32, 128, 32, 128).transpose(3, 0, 2, 1))
    ga = np.zeros((128, 32, 36), np.float32)
    ga[:, :, 0:4] = np.asarray(L0_ffn_gate_w, np.float32) \
        .reshape(4, 32, 128).transpose(2, 1, 0)
    ga[:, :, 4:36] = np.asarray(L0_ffn_lora_A, np.float32) \
        .reshape(32, 32, 128).transpose(2, 1, 0)
    bf = np.ascontiguousarray(np.asarray(L0_ffn_lora_B, np.float32)
                              .transpose(0, 2, 1).reshape(32, 4096))
    br = np.asarray(L0_ffn_expert_bias, np.float32).reshape(1, 4)
    idh = np.eye(128, dtype=np.float32)
    mkh = np.zeros((128, 2048), np.float32)
    pp = np.arange(128)[:, None]
    cc = np.arange(512)[None, :]
    for r_i, r in enumerate((0, 128, 256, 384)):
        mkh[:, r_i * 512:(r_i + 1) * 512] = (pp + r <= cc).astype(np.float32)

    if "nc" not in _cache:
        _cache["nc"] = build()
    nc = _cache["nc"]
    shared = dict(w1=wq, w2=wd, w3=wu, gah=ga, bfh=bf, brh=br, idh=idh, mkh=mkh)
    in_maps = [dict(xt=xt_all[i], **shared) for i in range(8)]
    res = run_bass_kernel_spmd(nc, in_maps, core_ids=list(range(8)))
    out = np.stack([res.results[i]["out"] for i in range(8)])
    return np.ascontiguousarray(
        out.transpose(0, 3, 2, 1).reshape(8, 2048, 4096)).astype(np.float32)


# version 09
# speedup vs baseline: 1.3215x; 1.0021x over previous
import math
import numpy as np
import concourse.bass as bass
import concourse.mybir as mybir
from concourse.bass_utils import run_bass_kernel_spmd
from contextlib import ExitStack

F32 = mybir.dt.float32
BF16 = mybir.dt.bfloat16
AF = mybir.ActivationFunctionType
ALU = mybir.AluOpType
AX = mybir.AxisListType

S = 2048
NCH = 32
ST = 1024
AST = 512
SC = 1.0 / math.sqrt(128.0)
SHIFT = -12.0

ENGS = ("tensor", "vector", "scalar", "sync", "gpsimd")

A_H = 0
A_SLAB0 = 65536
A_SLAB1 = 69632
A_STG0 = 73728
A_STG1 = 74752
A_ABC_A = 75776
A_ABC_B = 76800
A_GA = 77824
A_IDENT = 78976
ARENA_COLS = 79104

PT0 = 48 * 1024
ARAW0 = 52 * 1024
AFIN0 = 54 * 1024
MASK0 = 56 * 1024


class Buf:
    __slots__ = ("w", "r")

    def __init__(self):
        self.w = None
        self.r = []


class Prog:
    def __init__(self):
        self.ops = {e: [] for e in ENGS}
        self.cnt = {e: 0 for e in ENGS}

    def emit(self, eng, fn, reads=(), writes=(), extra=(), dma=False):
        deps = {}

        def add(t):
            if t is None:
                return
            e, v = t
            if v > deps.get(e, 0):
                deps[e] = v

        for t in extra:
            add(t)
        for b in reads:
            add(b.w)
        for b in writes:
            add(b.w)
            for t in b.r:
                add(t)
        if dma:
            step = 16
        elif eng in ("sync", "gpsimd"):
            step = 0
        else:
            step = 1
        self.ops[eng].append((deps, fn, step))
        self.cnt[eng] += step
        tok = (eng, self.cnt[eng])
        for b in reads:
            b.r.append(tok)
        for b in writes:
            b.w = tok
            b.r = []
        return tok

    def finalize(self):
        waited = {"sync": set(), "gpsimd": set()}
        for ename in ENGS:
            for deps, fn, step in self.ops[ename]:
                for de, dv in deps.items():
                    if de in waited:
                        waited[de].add(dv)
        for q in ("sync", "gpsimd"):
            newops = []
            cum = 0
            for deps, fn, step in self.ops[q]:
                deps = {k: v for k, v in deps.items() if k != q}
                newops.append((deps, fn, step))
                if step == 16:
                    cum += 16
                    if cum in waited[q]:
                        newops.append(({q: cum}, None, 0))
            self.ops[q] = newops

    def replay(self, eng_name, eng, sems):
        last_wait = {}
        last_drain = 0
        own = 0
        for deps, fn, step in self.ops[eng_name]:
            if fn is None:
                eng.wait_ge(sems[eng_name], deps[eng_name])
                continue
            for e, v in deps.items():
                if e == eng_name:
                    if eng_name != "tensor" and v > last_drain:
                        eng.drain()
                        last_drain = own
                else:
                    if v > last_wait.get(e, 0):
                        eng.wait_ge(sems[e], v)
                        last_wait[e] = v
            inst = fn(eng)
            if step:
                own += step
                inst.then_inc(sems[eng_name], step)


def build():
    nc = bass.Bass("TRN2", target_bir_lowering=False)

    xt = nc.dram_tensor("xt", [128, NCH, S], F32, kind="ExternalInput")
    w1 = nc.dram_tensor("w1", [128, 48, NCH, 128], F32, kind="ExternalInput")
    w2 = nc.dram_tensor("w2", [128, 32, NCH, 128], F32, kind="ExternalInput")
    w3 = nc.dram_tensor("w3", [128, 32, NCH, 128], F32, kind="ExternalInput")
    gah = nc.dram_tensor("gah", [128, NCH, 36], F32, kind="ExternalInput")
    bfh = nc.dram_tensor("bfh", [32, 4096], F32, kind="ExternalInput")
    brh = nc.dram_tensor("brh", [1, 4], F32, kind="ExternalInput")
    idh = nc.dram_tensor("idh", [128, 128], F32, kind="ExternalInput")
    mkh = nc.dram_tensor("mkh", [128, 2048], F32, kind="ExternalInput")

    qscr = nc.dram_tensor("qscr", [128, 32, S], BF16, kind="Internal")
    kscr = nc.dram_tensor("kscr", [128, 8, S], BF16, kind="Internal")
    vscr = nc.dram_tensor("vscr", [128, 16, 1024], BF16, kind="Internal")
    ascr = nc.dram_tensor("ascr", [128, 32, S], BF16, kind="Internal")
    x1scr = nc.dram_tensor("x1scr", [128, NCH, S], F32, kind="Internal")
    dscr = nc.dram_tensor("dscr", [32, 2048], F32, kind="Internal")
    outt = nc.dram_tensor("out", [128, NCH, S], F32, kind="ExternalOutput")

    w1r = w1.rearrange("p j c k -> p j (c k)")
    w2r = w2.rearrange("p j c k -> p j (c k)")
    w3r = w3.rearrange("p j c k -> p j (c k)")
    gar = gah.rearrange("p c e -> p (c e)")

    P = Prog()

    with ExitStack() as ex:
        E = ex.enter_context
        arena = E(nc.sbuf_tensor("arena", [128, ARENA_COLS], BF16))
        fa = E(nc.sbuf_tensor("fa", [128, 2048], F32))
        rows = E(nc.sbuf_tensor("rows", [1, 4096], F32))
        rowsb = E(nc.sbuf_tensor("rowsb", [1, 2048], BF16))
        bf_sb = E(nc.sbuf_tensor("bf_sb", [32, 4096], BF16))
        zmT = E(nc.sbuf_tensor("zmT", [32, 2048], BF16))
        zm = E(nc.sbuf_tensor("zm", [128, 32], BF16))
        mx = E(nc.sbuf_tensor("mx", [128, 1], F32))
        oh = E(nc.sbuf_tensor("oh", [128, 4], F32))
        onc = E(nc.sbuf_tensor("onc", [128, 1], BF16))
        onc64 = E(nc.sbuf_tensor("onc64", [128, 64], BF16))
        onr = E(nc.sbuf_tensor("onr", [1, 128], BF16))
        bias_sb = E(nc.sbuf_tensor("bias_sb", [1, 4], BF16))
        denr = E(nc.sbuf_tensor("denr", [1, 1024], F32))
        rall = E(nc.sbuf_tensor("rall", [32, 2048], BF16))
        rrc2 = E(nc.sbuf_tensor("rrc2", [1, 2048], BF16))

        psA = E(nc.psum_tensor("psA", [128, 1024], F32))
        psB = E(nc.psum_tensor("psB", [128, 1024], F32))
        psC = E(nc.psum_tensor("psC", [128, 1024], F32))
        psD = E(nc.psum_tensor("psD", [128, 1024], F32))

        sems = {e: E(nc.semaphore(f"sem_{e}")) for e in ENGS}
        blk = E(nc.Block())

        AHg = [Buf() for _ in range(64)]

        def AH(c0, c1):
            return AHg[c0 // 1024:(c1 + 1023) // 1024]

        slabB = [Buf(), Buf()]
        stgB = [Buf(), Buf()]
        abcB = [Buf(), Buf()]
        gaB, identB, bfB, biasB = Buf(), Buf(), Buf(), Buf()
        oncB, onrB, onc64B = Buf(), Buf(), Buf()
        rowB = [Buf() for _ in range(4)]
        rowbB = [Buf() for _ in range(4)]
        psAh = [Buf(), Buf()]
        psBh = [Buf(), Buf()]
        psCh = [Buf(), Buf()]
        psDh = [Buf(), Buf()]
        psDd = [Buf(), Buf()]
        zmB, zmTB, mxB, ohB, rrecB = Buf(), Buf(), Buf(), Buf(), Buf()
        denrB = [Buf(), Buf()]
        dallB, rallB, rallfB = Buf(), Buf(), Buf()
        rrc2B = [Buf(), Buf()]
        faxB = [Buf(), Buf()]
        faoB = [Buf(), Buf()]
        qscrB = [Buf() for _ in range(32)]
        kscrB = [Buf() for _ in range(8)]
        vscrB = [Buf() for _ in range(16)]
        ascrB = [Buf() for _ in range(32)]
        x1B = [Buf() for _ in range(32)]
        out_toks = []

        SLAB = [A_SLAB0, A_SLAB1]
        STG = [A_STG0, A_STG1]
        FAX = [0, 512]
        FAO = [1024, 1536]

        def ah_ap(c0, width):
            return arena[:, c0:c0 + width]

        P.emit("gpsimd", lambda e: e.dma_start(out=bf_sb[:, :], in_=bfh[:, :]),
               writes=(bfB,), dma=True)
        P.emit("gpsimd", lambda e: e.dma_start(out=arena[:, A_GA:A_GA + 1152], in_=gar[:, :]),
               writes=(gaB,), dma=True)
        P.emit("gpsimd", lambda e: e.dma_start(out=bias_sb[:, :], in_=brh[:, :]),
               writes=(biasB,), dma=True)
        P.emit("gpsimd", lambda e: e.dma_start(out=arena[:, A_IDENT:A_IDENT + 128], in_=idh[:, :]),
               writes=(identB,), dma=True)
        P.emit("vector", lambda e: e.memset(onc[:, :], 1.0), writes=(oncB,))
        P.emit("vector", lambda e: e.memset(onc64[:, :], 1.0), writes=(onc64B,))
        P.emit("vector", lambda e: e.memset(onr[:, :], 1.0), writes=(onrB,))

        def ln_phase(src_dram, src_bufs):
            ahview = arena[:, 0:65536].rearrange("p (c s) -> p c s", c=32)
            for st in range(2):
                for cg in range(4):
                    c0 = cg * 8
                    dst = ahview[:, c0:c0 + 8, st * ST:(st + 1) * ST]
                    srcv = src_dram[:, c0:c0 + 8, st * ST:(st + 1) * ST]
                    extra = []
                    if src_bufs is not None:
                        extra = [src_bufs[c].w for c in range(c0, c0 + 8)
                                 if src_bufs[c].w]
                    wr = []
                    for c in range(c0, c0 + 8):
                        wr.extend(AH(c * S + st * ST, c * S + st * ST + ST))
                    P.emit("gpsimd",
                           lambda e, d=dst, s=srcv: e.dma_start(out=d, in_=s),
                           writes=wr, extra=extra, dma=True)
            for st in range(2):
                for c in range(NCH):
                    col = c * S + st * ST
                    x_ap = ah_ap(col, ST)
                    sq_ap = arena[:, STG[c % 2]:STG[c % 2] + ST]
                    P.emit("scalar",
                           lambda e, o=sq_ap, i=x_ap: e.square(o, i),
                           reads=AH(col, col + ST), writes=(stgB[c % 2],))
                    for h in range(2):
                        P.emit("tensor",
                               lambda e, r=x_ap, hh=h, s=(c == 0), t=(c == NCH - 1):
                               e.matmul(psC[0:64, hh * 512:(hh + 1) * 512], onc64[:, :],
                                        r[:, hh * 512:(hh + 1) * 512], start=s, stop=t,
                                        skip_group_check=True),
                               reads=AH(col, col + ST) + [onc64B],
                               writes=(psCh[h],))
                        P.emit("tensor",
                               lambda e, r=sq_ap, hh=h, s=(c == 0), t=(c == NCH - 1):
                               e.matmul(psC[64:128, hh * 512:(hh + 1) * 512], onc64[:, :],
                                        r[:, hh * 512:(hh + 1) * 512], start=s, stop=t,
                                        skip_group_check=True),
                               reads=(stgB[c % 2], onc64B),
                               writes=(psCh[h],))
                P.emit("vector",
                       lambda e: e.tensor_scalar(rows[0:1, 0:1024], psC[0:1, :],
                                                 1.0 / 4096.0, None, ALU.mult),
                       reads=(psCh[0], psCh[1]), writes=(rowB[0],))
                P.emit("vector",
                       lambda e: e.tensor_scalar(rows[0:1, 1024:2048], psC[64:65, :],
                                                 1.0 / 4096.0, None, ALU.mult),
                       reads=(psCh[0], psCh[1]), writes=(rowB[1],))
                P.emit("vector",
                       lambda e: e.tensor_mul(rows[0:1, 2048:3072], rows[0:1, 0:1024], rows[0:1, 0:1024]),
                       reads=(rowB[0],), writes=(rowB[2],))
                P.emit("vector",
                       lambda e: e.tensor_sub(rows[0:1, 1024:2048], rows[0:1, 1024:2048], rows[0:1, 2048:3072]),
                       reads=(rowB[1], rowB[2]), writes=(rowB[1],))
                P.emit("scalar",
                       lambda e: e.activation(rows[0:1, 3072:4096], rows[0:1, 1024:2048], AF.Sqrt,
                                              bias=np.float32(1e-5), scale=np.float32(1.0)),
                       reads=(rowB[1],), writes=(rowB[3],))
                P.emit("vector",
                       lambda e: e.reciprocal(rows[0:1, 1024:2048], rows[0:1, 3072:4096]),
                       reads=(rowB[3],), writes=(rowB[1],))
                P.emit("scalar",
                       lambda e: e.copy(rowsb[0:1, 0:1024], rows[0:1, 1024:2048]),
                       reads=(rowB[1],), writes=(rowbB[0],))
                P.emit("vector",
                       lambda e: e.tensor_mul(rows[0:1, 2048:3072], rows[0:1, 0:1024], rows[0:1, 1024:2048]),
                       reads=(rowB[0], rowB[1]), writes=(rowB[2],))
                P.emit("vector",
                       lambda e: e.tensor_scalar(rowsb[0:1, 1024:2048], rows[0:1, 2048:3072],
                                                 -1.0, None, ALU.mult),
                       reads=(rowB[2],), writes=(rowbB[1],))
                for h in range(2):
                    P.emit("tensor",
                           lambda e, hh=h: e.matmul(psD[:, hh * 512:(hh + 1) * 512], onr[:, :],
                                              rowsb[0:1, hh * 512:(hh + 1) * 512],
                                              start=True, stop=True, skip_group_check=True),
                           reads=(rowbB[0], onrB),
                           writes=(psDh[h], psDd[0], psDd[1]) if h == 0 else (psDh[h],))
                    P.emit("tensor",
                           lambda e, hh=h: e.matmul(psC[:, hh * 512:(hh + 1) * 512], onr[:, :],
                                              rowsb[0:1, 1024 + hh * 512:1024 + (hh + 1) * 512],
                                              start=True, stop=True, skip_group_check=True),
                           reads=(rowbB[1], onrB), writes=(psCh[h],))
                P.emit("scalar",
                       lambda e: e.copy(arena[:, A_ABC_A:A_ABC_A + ST], psD[:, :]),
                       reads=(psDh[0], psDh[1]), writes=(abcB[0],))
                P.emit("scalar",
                       lambda e: e.copy(arena[:, A_ABC_B:A_ABC_B + ST], psC[:, :]),
                       reads=(psCh[0], psCh[1]), writes=(abcB[1],))
                for c in range(NCH):
                    col = c * S + st * ST
                    x_ap = ah_ap(col, ST)
                    a_ap = arena[:, A_ABC_A:A_ABC_A + ST]
                    b_ap = arena[:, A_ABC_B:A_ABC_B + ST]
                    P.emit("vector",
                           lambda e, x=x_ap, a=a_ap: e.tensor_mul(x, x, a),
                           reads=AH(col, col + ST) + [abcB[0]],
                           writes=AH(col, col + ST))
                    P.emit("vector",
                           lambda e, x=x_ap, b=b_ap: e.tensor_add(x, x, b),
                           reads=AH(col, col + ST) + [abcB[1]],
                           writes=AH(col, col + ST))

        slab_seq = []

        def run_slab_stream():
            n = len(slab_seq)
            for i in range(n + 1):
                if i < n:
                    src, jb, _ = slab_seq[i]
                    slot = i % 2
                    P.emit("gpsimd",
                           lambda e, s=slot, sr=src, j=jb:
                           e.dma_start(out=arena[:, SLAB[s]:SLAB[s] + 4096],
                                       in_=sr[:, j, :]),
                           writes=(slabB[slot],), dma=True)
                if i >= 1:
                    _, _, fn = slab_seq[i - 1]
                    fn((i - 1) % 2)

        ln_phase(xt, None)

        def qk_compute(jb):
            def fn(slot):
                for st in range(2):
                    ps = psA if st == 0 else psB
                    psh = psAh if st == 0 else psBh
                    for c in range(NCH):
                        col = c * S + st * ST
                        lhs = arena[:, SLAB[slot] + c * 128:SLAB[slot] + (c + 1) * 128]
                        for h in range(2):
                            rhs = ah_ap(col + h * 512, 512)
                            P.emit("tensor",
                                   lambda e, l=lhs, r=rhs, p=ps, hh=h,
                                          s=(c == 0), t=(c == NCH - 1):
                                   e.matmul(p[:, hh * 512:(hh + 1) * 512], l, r,
                                            start=s, stop=t, skip_group_check=True),
                                   reads=AH(col, col + ST) + [slabB[slot]],
                                   writes=(psh[h],))
                    stg_ap = arena[:, STG[st]:STG[st] + ST]
                    P.emit("scalar",
                           lambda e, o=stg_ap, p=ps: e.copy(o, p[:, :]),
                           reads=(psh[0], psh[1]), writes=(stgB[st],))
                    if jb < 32:
                        dst = qscr[:, jb, st * ST:(st + 1) * ST]
                        wb = qscrB[jb]
                    else:
                        dst = kscr[:, jb - 32, st * ST:(st + 1) * ST]
                        wb = kscrB[jb - 32]
                    P.emit("sync",
                           lambda e, d=dst, s=stg_ap: e.dma_start(out=d, in_=s),
                           reads=(stgB[st],), writes=(wb,), dma=True)
            return fn

        for jb in range(40):
            slab_seq.append((w1r, jb, qk_compute(jb)))

        def v_compute(dvt):
            def fn(slot):
                rot = [(psC, psCh, 0), (psC, psCh, 1), (psD, psDh, 0), (psD, psDh, 1)]
                for tsub in range(16):
                    ps, psh, hh = rot[tsub % 4]
                    ps_ap = ps[:, hh * 512:hh * 512 + 128]
                    st = tsub // 8
                    vw = (psh[hh], psDd[0], psDd[1]) if (ps is psD and hh == 0) else (psh[hh],)
                    for c in range(NCH):
                        col = c * S + tsub * 128
                        lhs = ah_ap(col, 128)
                        rhs = arena[:, SLAB[slot] + c * 128:SLAB[slot] + (c + 1) * 128]
                        P.emit("tensor",
                               lambda e, l=lhs, r=rhs, p=ps_ap, s=(c == 0), t=(c == NCH - 1):
                               e.matmul(p, l, r, start=s, stop=t,
                                        skip_group_check=True),
                               reads=AH(col, col + 128) + [slabB[slot]],
                               writes=vw)
                    stg_ap = arena[:, A_STG0 + tsub * 128:A_STG0 + (tsub + 1) * 128]
                    P.emit("scalar",
                           lambda e, o=stg_ap, p=ps_ap: e.copy(o, p),
                           reads=(psh[hh],), writes=(stgB[0], stgB[1]))
                dst = vscr[:, :, dvt * 128:(dvt + 1) * 128]
                vst = arena[:, A_STG0:A_STG0 + 2048].rearrange(
                    "p (t d) -> p t d", t=16)
                P.emit("sync",
                       lambda e, d=dst, s=vst: e.dma_start(out=d, in_=s),
                       reads=(stgB[0], stgB[1]),
                       extra=[vscrB[t].w for t in range(16) if vscrB[t].w],
                       writes=[vscrB[t] for t in range(16)], dma=True)
            return fn

        for dvt in range(8):
            slab_seq.append((w1r, 40 + dvt, v_compute(dvt)))

        run_slab_stream()
        slab_seq = []

        P.emit("gpsimd", lambda e: e.dma_start(out=arena[:, MASK0:MASK0 + 2048], in_=mkh[:, :]),
               writes=AH(MASK0, MASK0 + 2048), dma=True)
        QOFF, KOFF, VOFF = 0, 16384, 32768
        qview = arena[:, QOFF:QOFF + 16384].rearrange("p (hq s) -> p hq s", hq=32)
        kview = arena[:, KOFF:KOFF + 16384].rearrange("p (kb s) -> p kb s", kb=8)
        vflat = vscr.rearrange("p t d -> p (t d)")
        P.emit("sync",
               lambda e: e.dma_start(out=kview[:, :, :], in_=kscr[:, :, :]),
               extra=[kscrB[j].w for j in range(8) if kscrB[j].w],
               writes=AH(KOFF, KOFF + 16384), dma=True)
        P.emit("sync",
               lambda e: e.dma_start(out=arena[:, VOFF:VOFF + 16384],
                                     in_=vflat[:, :]),
               extra=[vscrB[j].w for j in range(16) if vscrB[j].w],
               writes=AH(VOFF, VOFF + 16384), dma=True)
        for st2 in range(4):
            for qg4 in range(4):
                q0 = qg4 * 8
                P.emit("sync",
                       lambda e, q=q0, s=st2:
                       e.dma_start(out=qview[:, q:q + 8, :],
                                   in_=qscr[:, q:q + 8, s * 512:(s + 1) * 512]),
                       extra=[qscrB[q0 + j].w for j in range(8) if qscrB[q0 + j].w],
                       writes=AH(QOFF + q0 * 512, QOFF + (q0 + 8) * 512), dma=True)

            for kb in range(8):
                for g in range(4):
                    hq = kb * 4 + g
                    avt, avhl = [(psA, psAh), (psA, psAh), (psB, psBh), (psB, psBh)][g]
                    avhh = g % 2
                    av_ap = avt[:, avhh * 512:(avhh + 1) * 512]
                    ntc = st2 * 4 + 4
                    q_ap = arena[:, QOFF + hq * 512:QOFF + (hq + 1) * 512]
                    qg = AH(QOFF + hq * 512, QOFF + (hq + 1) * 512)

                    qk_rot = [(psC[:, 0:512], psCh[0]),
                              (psC[:, 512:1024], psCh[1]),
                              (psD[:, 512:1024], psDh[1])]

                    def qk_step(tc):
                        kcol = KOFF + kb * S + tc * 128
                        k_ap = arena[:, kcol:kcol + 128]
                        qk_ap, qk_buf = qk_rot[tc % 3]
                        pt_col = PT0 + (tc % 4) * 1024
                        pt_ap = arena[:, pt_col:pt_col + 512]
                        P.emit("tensor",
                               lambda e, l=k_ap, r=q_ap, p=qk_ap:
                               e.matmul(p, l, r, start=True, stop=True,
                                        skip_group_check=True),
                               reads=AH(kcol, kcol + 128) + qg,
                               writes=(qk_buf,))
                        P.emit("scalar",
                               lambda e, o=pt_ap, i=qk_ap:
                               e.activation(o, i, AF.Exp,
                                            bias=np.float32(SHIFT), scale=np.float32(SC)),
                               reads=(qk_buf,),
                               writes=AH(pt_col, pt_col + 512))
                        if tc >= st2 * 4:
                            r_idx = tc - st2 * 4
                            m_ap = arena[:, MASK0 + r_idx * 512:MASK0 + (r_idx + 1) * 512]
                            P.emit("vector",
                                   lambda e, p=pt_ap, m=m_ap: e.tensor_mul(p, p, m),
                                   reads=AH(pt_col, pt_col + 512) +
                                         AH(MASK0 + r_idx * 512, MASK0 + (r_idx + 1) * 512),
                                   writes=AH(pt_col, pt_col + 512))

                    def dav_step(tc):
                        pt_col = PT0 + (tc % 4) * 1024
                        pt_ap = arena[:, pt_col:pt_col + 512]
                        vcol = VOFF + tc * 1024 + kb * 128
                        v_ap = arena[:, vcol:vcol + 128]
                        P.emit("tensor",
                               lambda e, p=pt_ap, s=(tc == 0), t=(tc == ntc - 1), gg=g:
                               e.matmul(psD[64 * (gg % 2):64 * (gg % 2) + 64, 0:512],
                                        onc64[:, :], p,
                                        start=s, stop=t, skip_group_check=True),
                               reads=AH(pt_col, pt_col + 512) + [oncB],
                               writes=(psDd[g % 2],))
                        P.emit("tensor",
                               lambda e, l=v_ap, p=pt_ap, o=av_ap,
                                      s=(tc == 0), t=(tc == ntc - 1):
                               e.matmul(o, l, p, start=s, stop=t,
                                        skip_group_check=True),
                               reads=AH(pt_col, pt_col + 512) + AH(vcol, vcol + 128),
                               writes=(avhl[avhh],))

                    for tc in range(ntc + 2):
                        if tc < ntc:
                            qk_step(tc)
                        if tc >= 2:
                            dav_step(tc - 2)

                    P.emit("scalar",
                           lambda e, gg=g: e.copy(
                               denr[0:1, (gg % 2) * 512:(gg % 2) * 512 + 512],
                               psD[64 * (gg % 2):64 * (gg % 2) + 1, 0:512]),
                           reads=(psDd[g % 2],), writes=(denrB[g % 2],))
                    P.emit("sync",
                           lambda e, h=hq, s=st2, gg=g:
                           e.dma_start(out=dscr[h:h + 1, s * 512:(s + 1) * 512],
                                       in_=denr[0:1, (gg % 2) * 512:(gg % 2) * 512 + 512]),
                           reads=(denrB[g % 2],), dma=True)
                    ar_col = ARAW0 + (g % 2) * 1024
                    ar_ap = arena[:, ar_col:ar_col + 512]
                    P.emit("scalar",
                           lambda e, o=ar_ap, i=av_ap: e.copy(o, i),
                           reads=(avhl[avhh],), writes=AH(ar_col, ar_col + 512))
                    P.emit("sync",
                           lambda e, h=hq, s=st2, a=ar_ap:
                           e.dma_start(out=ascr[:, h, s * 512:(s + 1) * 512], in_=a),
                           reads=AH(ar_col, ar_col + 512),
                           writes=(ascrB[hq],), dma=True)

        P.emit("sync",
               lambda e: e.dma_start(out=fa[0:32, :], in_=dscr[:, :]),
               writes=(dallB, faxB[0], faxB[1]), dma=True)
        P.emit("vector",
               lambda e: e.reciprocal(fa[32:64, :], fa[0:32, :]),
               reads=(dallB,),
               writes=(rallfB, faoB[0], faoB[1]))
        P.emit("scalar",
               lambda e: e.copy(rall[:, :], fa[32:64, :]),
               reads=(rallfB, faxB[0], faxB[1], faoB[0], faoB[1]),
               writes=(rallB,))

        for hq in range(32):
            dcol = hq * S
            P.emit("sync",
                   lambda e, h=hq, d=dcol:
                   e.dma_start(out=arena[:, d:d + S], in_=ascr[:, h, :]),
                   reads=(ascrB[hq],), writes=AH(dcol, dcol + S), dma=True)
            P.emit("sync",
                   lambda e, h=hq:
                   e.dma_start(out=rrc2[0:1, :],
                               in_=rall[h:h + 1, :]),
                   reads=(rallB,), writes=(rrc2B[0],), dma=True)
            for st2 in range(4):
                bc_ap = psC[:, (st2 % 2) * 512:(st2 % 2 + 1) * 512]
                a_ap = arena[:, dcol + st2 * 512:dcol + (st2 + 1) * 512]
                P.emit("tensor",
                       lambda e, h=hq, s=st2, b=bc_ap:
                       e.matmul(b, onr[:, :],
                                rrc2[0:1, s * 512:(s + 1) * 512],
                                start=True, stop=True, skip_group_check=True),
                       reads=(rrc2B[0], onrB), writes=(psCh[st2 % 2],))
                P.emit("vector",
                       lambda e, a=a_ap, b=bc_ap: e.tensor_mul(a, b, a),
                       reads=[psCh[st2 % 2]] + AH(dcol + st2 * 512, dcol + (st2 + 1) * 512),
                       writes=AH(dcol + st2 * 512, dcol + (st2 + 1) * 512))

        def dense_compute(ob, wsrc, out_is_x1):
            def fn(slot):
                for st in range(2):
                    ps = psA if st == 0 else psB
                    psh = psAh if st == 0 else psBh
                    for c in range(NCH):
                        col = c * S + st * ST
                        lhs = arena[:, SLAB[slot] + c * 128:SLAB[slot] + (c + 1) * 128]
                        for h in range(2):
                            rhs = ah_ap(col + h * 512, 512)
                            P.emit("tensor",
                                   lambda e, l=lhs, r=rhs, p=ps, hh=h, s=(c == 0),
                                          t=(c == NCH - 1 and out_is_x1):
                                   e.matmul(p[:, hh * 512:(hh + 1) * 512], l, r,
                                            start=s, stop=t, skip_group_check=True),
                                   reads=AH(col, col + ST) + [slabB[slot]],
                                   writes=(psh[h],))
                    if not out_is_x1:
                        for h in range(2):
                            P.emit("tensor",
                                   lambda e, p=ps, o=ob, s2=st, hh=h:
                                   e.matmul(p[:, hh * 512:(hh + 1) * 512],
                                            bf_sb[0:32, o * 128:(o + 1) * 128],
                                            zmT[0:32, s2 * ST + hh * 512:s2 * ST + (hh + 1) * 512],
                                            start=False, stop=True, skip_group_check=True),
                                   reads=(bfB, zmTB), writes=(psh[h],))
                    for half in range(2):
                        src = (xt if out_is_x1 else x1scr)[
                            :, ob, st * ST + half * 512:st * ST + (half + 1) * 512]
                        fx_ap = fa[:, FAX[half]:FAX[half] + 512]
                        extra = [] if out_is_x1 else ([x1B[ob].w] if x1B[ob].w else [])
                        P.emit("sync",
                               lambda e, d=fx_ap, s=src: e.dma_start(out=d, in_=s),
                               writes=(faxB[half],), extra=extra, dma=True)
                        fo_ap = fa[:, FAO[half]:FAO[half] + 512]
                        ps_ap = ps[:, half * 512:(half + 1) * 512]
                        P.emit("vector",
                               lambda e, o=fo_ap, p=ps_ap, x=fx_ap:
                               e.tensor_tensor(o, p, x, ALU.add),
                               reads=(psh[half], faxB[half]), writes=(faoB[half],))
                        dstd = (x1scr if out_is_x1 else outt)[
                            :, ob, st * ST + half * 512:st * ST + (half + 1) * 512]
                        wb = (x1B[ob],) if out_is_x1 else ()
                        tok = P.emit("sync",
                                     lambda e, d=dstd, s=fo_ap: e.dma_start(out=d, in_=s),
                                     reads=(faoB[half],), writes=wb, dma=True)
                        if not out_is_x1:
                            out_toks.append(tok)
            return fn

        for ob in range(32):
            slab_seq.append((w2r, ob, dense_compute(ob, w2r, True)))
        run_slab_stream()
        slab_seq = []

        ln_phase(x1scr, x1B)

        for ssg in range(16):
            scol = ssg * 128
            zo = (ssg % 2) * 512
            zb = psCh[ssg % 2]
            for c in range(NCH):
                col = c * S + scol
                lhs = ah_ap(col, 128)
                rhs = arena[:, A_GA + c * 36:A_GA + (c + 1) * 36]
                P.emit("tensor",
                       lambda e, l=lhs, r=rhs, s=(c == 0), z=zo:
                       e.matmul(psC[:, z:z + 36], l, r, start=s, stop=False,
                                skip_group_check=True),
                       reads=AH(col, col + 128) + [gaB],
                       writes=(zb,))
            P.emit("tensor",
                   lambda e, z=zo: e.matmul(psC[:, z:z + 4], onr[:, :], bias_sb[:, :],
                                      start=False, stop=True, skip_group_check=True),
                   reads=(biasB, onrB), writes=(zb,))
            P.emit("vector",
                   lambda e, z=zo: e.reduce_max(mx[:, :], psC[:, z:z + 4], axis=AX.X),
                   reads=(zb,), writes=(mxB,))
            P.emit("vector",
                   lambda e, z=zo: e.tensor_scalar(oh[:, :], psC[:, z:z + 4], mx[:, 0:1],
                                             None, ALU.is_ge),
                   reads=(zb, mxB), writes=(ohB,))
            for ee in range(4):
                P.emit("vector",
                       lambda e, v=ee, z=zo: e.tensor_scalar(zm[:, v * 8:(v + 1) * 8],
                                                       psC[:, z + 4 + v * 8:z + 4 + (v + 1) * 8],
                                                       oh[:, v:v + 1], None, ALU.mult),
                       reads=(zb, ohB), writes=(zmB,))
            P.emit("tensor",
                   lambda e: e.matmul(psD[0:32, 0:128], zm[:, :],
                                      arena[:, A_IDENT:A_IDENT + 128],
                                      start=True, stop=True, skip_group_check=True),
                   reads=(zmB, identB), writes=(psDh[0], psDd[0], psDd[1]))
            P.emit("scalar",
                   lambda e, sc=scol: e.copy(zmT[0:32, sc:sc + 128], psD[0:32, 0:128]),
                   reads=(psDh[0],), writes=(zmTB,))

        for ob in range(32):
            slab_seq.append((w3r, ob, dense_compute(ob, w3r, False)))
        run_slab_stream()

        P.emit("gpsimd", lambda e: e.nop(), extra=out_toks)

        P.finalize()

        blk.tensor(lambda e: P.replay("tensor", e, sems))
        blk.vector(lambda e: P.replay("vector", e, sems))
        blk.scalar(lambda e: P.replay("scalar", e, sems))
        blk.sync(lambda e: P.replay("sync", e, sems))
        blk.gpsimd(lambda e: P.replay("gpsimd", e, sems))

    return nc


_cache = {}


def kernel(L0_attn_q, L0_attn_w_qkv, L0_attn_w_dense, L0_ffn_W_up,
           L0_ffn_gate_w, L0_ffn_expert_bias, L0_ffn_lora_A, L0_ffn_lora_B):
    x = np.asarray(L0_attn_q, dtype=np.float32)
    xt_all = np.ascontiguousarray(
        x.reshape(8, 2048, 32, 128).transpose(0, 3, 2, 1))
    wq = np.ascontiguousarray(np.asarray(L0_attn_w_qkv, np.float32)
                              .reshape(48, 128, 32, 128).transpose(3, 0, 2, 1))
    wd = np.ascontiguousarray(np.asarray(L0_attn_w_dense, np.float32)
                              .reshape(32, 128, 32, 128).transpose(3, 0, 2, 1))
    wu = np.ascontiguousarray(np.asarray(L0_ffn_W_up, np.float32)
                              .reshape(32, 128, 32, 128).transpose(3, 0, 2, 1))
    ga = np.zeros((128, 32, 36), np.float32)
    ga[:, :, 0:4] = np.asarray(L0_ffn_gate_w, np.float32) \
        .reshape(4, 32, 128).transpose(2, 1, 0)
    ga[:, :, 4:36] = np.asarray(L0_ffn_lora_A, np.float32) \
        .reshape(32, 32, 128).transpose(2, 1, 0)
    bf = np.ascontiguousarray(np.asarray(L0_ffn_lora_B, np.float32)
                              .transpose(0, 2, 1).reshape(32, 4096))
    br = np.asarray(L0_ffn_expert_bias, np.float32).reshape(1, 4)
    idh = np.eye(128, dtype=np.float32)
    mkh = np.zeros((128, 2048), np.float32)
    pp = np.arange(128)[:, None]
    cc = np.arange(512)[None, :]
    for r_i, r in enumerate((0, 128, 256, 384)):
        mkh[:, r_i * 512:(r_i + 1) * 512] = (pp + r <= cc).astype(np.float32)

    if "nc" not in _cache:
        _cache["nc"] = build()
    nc = _cache["nc"]
    shared = dict(w1=wq, w2=wd, w3=wu, gah=ga, bfh=bf, brh=br, idh=idh, mkh=mkh)
    in_maps = [dict(xt=xt_all[i], **shared) for i in range(8)]
    res = run_bass_kernel_spmd(nc, in_maps, core_ids=list(range(8)))
    out = np.stack([res.results[i]["out"] for i in range(8)])
    return np.ascontiguousarray(
        out.transpose(0, 3, 2, 1).reshape(8, 2048, 4096)).astype(np.float32)
